# version 15 (final)
# baseline (speedup 1.0000x reference)
import contextlib
import numpy as np
import ml_dtypes
import concourse.bass as bass
import concourse.mybir as mybir
from concourse.bass_utils import run_bass_kernel_spmd

F32 = mybir.dt.float32
BF16 = mybir.dt.bfloat16
AF = mybir.ActivationFunctionType
ALU = mybir.AluOpType
AX = mybir.AxisListType.X

DM = 1024
CH = 1024
NST = 16
L = 4096
T = 2048
NH = 16
NE = 8
HD = 64
EPS_LN = 1e-5
EPS_R = 1e-9
CAP = 78.0

_STAGE = "full"
_NSC = 4


class Sem:
    __slots__ = ("h", "n")

    def __init__(self, h):
        self.h = h
        self.n = 0

    def inc(self, k=1):
        self.n += k
        return self.n


def build_program(stage):
    nc = bass.Bass("TRN2", target_bir_lowering=False, debug=False,
                   detect_race_conditions=False)
    dbg = stage != "full"

    def din(name, shape, dt=F32):
        return nc.dram_tensor(name, shape, dt, kind="ExternalInput").ap()

    xbt = din("xbt", [DM, L])
    xres = din("xres", [DM, T])
    win_t = din("win_t", [DM, 2 * CH])
    convw = din("convw", [CH, 4])
    convb = din("convb", [CH, 1])
    wx_t = din("wx_t", [CH, 96])
    wdt_t = din("wdt_t", [64, CH])
    bdt = din("bdt", [CH, 1])
    alog = din("alog", [CH, NST])
    dpv = din("dpv", [CH, 1])
    wout_t = din("wout_t", [CH, DM])
    wq_t = din("wq_t", [DM, DM])
    wk_t = din("wk_t", [DM, DM])
    wv_t = din("wv_t", [DM, DM])
    wo_t = din("wo_t", [DM, DM])
    gw_t = din("gw_t", [DM, NE])
    gup = din("gup", [NE, DM, 2048])
    dwn = din("dwn", [NE, 1024, DM])
    iden = din("iden", [128, 128])
    cosr = din("cosr", [128, T])
    sinr = din("sinr", [128, T])
    mask8 = din("mask8", [128, 8 * 512])
    oh8 = din("oh8", [8, NE * 128])

    yout = nc.dram_tensor("yout", [T, DM], F32, kind="ExternalOutput").ap()
    sdump = (nc.dram_tensor("sdump", [8, 128, T], F32, kind="ExternalOutput").ap()
             if dbg else None)

    dk = dict(kind="ExternalOutput") if dbg else {}
    u_hbm = nc.dram_tensor("u_hbm", [CH, L], F32, **dk).ap()
    rs_hbm = nc.dram_tensor("rs_hbm", [CH, L], F32, **dk).ap()
    dl_hbm = nc.dram_tensor("dl_hbm", [CH, L], F32, **dk).ap()
    xdbl_in = nc.dram_tensor("xdbl_in", [4, 96, 1024], F32).ap()
    xdbl_out = nc.dram_tensor("xdbl_out", [4, 96, 1024], F32).ap()
    rsin = nc.dram_tensor("rsin", [8, 2, DM, 256], F32).ap()
    rsout = nc.dram_tensor("rsout", [8, DM, 256], F32).ap()
    gup_bf = nc.dram_tensor("gup_bf", [NE, 128, 16384], BF16).ap()
    dwn_bf = nc.dram_tensor("dwn_bf", [NE, 128, 8192], BF16).ap()
    wsg_hbm = nc.dram_tensor("wsg_hbm", [NE, 2048], F32).ap()
    kag_in = nc.dram_tensor("kag_in", [128, 16384], BF16).ap()
    kag_out = nc.dram_tensor("kag_out", [256, 16384], BF16).ap()
    vag_in = nc.dram_tensor("vag_in", [T, NH * 65], BF16).ap()
    vag_out = nc.dram_tensor("vag_out", [2 * T, NH * 65], BF16).ap()
    negc_hbm = nc.dram_tensor("negc_hbm", [NH, T], F32).ap()
    ybf_dbg = (nc.dram_tensor("ybf_dbg", [CH, L], BF16, kind="ExternalOutput").ap()
               if dbg else None)
    xs_dbg = (nc.dram_tensor("xs_dbg", [4, 128, 4096], BF16, kind="ExternalOutput").ap()
              if dbg else None)
    rt_dbg = (nc.dram_tensor("rt_dbg", [8, 2048], F32, kind="ExternalOutput").ap()
              if dbg else None)
    da_dbg = (nc.dram_tensor("da_dbg", [4, 128, 4096], F32, kind="ExternalOutput").ap()
              if dbg else None)
    lg_dbg = (nc.dram_tensor("lg_dbg", [128, 8], F32, kind="ExternalOutput").ap()
              if dbg else None)
    xs2_dbg = (nc.dram_tensor("xs2_dbg", [2, 128, 8192], BF16, kind="ExternalOutput").ap()
               if dbg else None)
    da2_dbg = (nc.dram_tensor("da2_dbg", [2, 128, 8192], F32, kind="ExternalOutput").ap()
               if dbg else None)
    r8_dbg = (nc.dram_tensor("r8_dbg", [128, 48], F32, kind="ExternalOutput").ap()
              if dbg else None)
    row_dbg = (nc.dram_tensor("row_dbg", [4, 512], F32, kind="ExternalOutput").ap()
               if dbg else None)
    dbg_gate = [0]
    rso_dbg = (nc.dram_tensor("rso_dbg", [8, DM, 256], F32, kind="ExternalOutput").ap()
               if dbg else None)
    rsi_dbg = (nc.dram_tensor("rsi_dbg", [8, 2, DM, 256], F32, kind="ExternalOutput").ap()
               if dbg else None)
    diag_hbm = nc.dram_tensor("diag_hbm", [NH, T], F32).ap()
    qn2_hbm = nc.dram_tensor("qn2_hbm", [NH, T], F32).ap()

    Q = {"tensor": [], "vector": [], "scalar": [], "gpsimd": [], "sync": []}

    def on(eng):
        def deco(fn):
            Q[eng].append(fn)
            return fn
        return deco


    def ld3(sy_or_g, sb_ap, hbm_ap, nchunk, sem, p=128, slow=False):
        sy_or_g.dma_start(
            out=sb_ap.rearrange("p (c m) -> p c m", c=nchunk),
            in_=hbm_ap.rearrange("(c p) m -> p c m", p=p),
            allow_slow_non_contiguous=slow).then_inc(sem, 16)

    _sem_guards = []
    _all_sems = []

    def mksem(name):
        cm = nc.semaphore(name)
        _sem_guards.append(cm)
        s = Sem(cm.__enter__())
        _all_sems.append(s)
        return s

    sW = mksem("sW")
    sX = mksem("sX")
    sT = mksem("sT")
    sV = mksem("sV")
    sS = mksem("sS")
    sG = mksem("sG")
    sC = mksem("sC")
    sY = mksem("sY")
    sPH = mksem("sPH")
    sPst = mksem("sPst")
    sXs = mksem("sXs")
    sxp = [mksem("sxp0"), mksem("sxp1")]
    slp = [mksem("slp0"), mksem("slp1")]
    sbp = [mksem("sbp0"), mksem("sbp1"), mksem("sbp2"), mksem("sbp3")]
    scp = [mksem("scp0"), mksem("scp1")]
    sup = [mksem("sup0"), mksem("sup1")]
    srp = [mksem("srp0"), mksem("srp1")]
    sdst = mksem("sdst")
    sxd = mksem("sxd")
    swt = mksem("swt")
    swin = mksem("swin")
    smw = mksem("smw")
    sdl1 = mksem("sdl1")
    sCar = mksem("sCar")
    sCrs = mksem("sCrs")

    nphase = [0]
    _bar_waited = {}

    def barrier():
        nphase[0] += 1
        tgt = 5 * nphase[0]
        drains = []
        for s in _all_sems:
            if s is sPH:
                continue
            if s.n > _bar_waited.get(id(s), 0):
                drains.append((s.h, s.n))
                _bar_waited[id(s)] = s.n
        for eng in Q:
            @on(eng)
            def _(e, tgt=tgt, drains=drains, eng=eng):
                if eng == "sync":
                    for h, n in drains:
                        e.wait_ge(h, n)
                e.nop().then_inc(sPH.h, 1)
                e.wait_ge(sPH.h, tgt)
        sPH.n = tgt

    with (
        nc.sbuf_tensor("stream", [128, 8 * T], F32) as stream,
        nc.sbuf_tensor("iden_sb", [128, 128], F32) as iden_sb,
        nc.sbuf_tensor("onesc", [128, 1], F32) as onesc,
        nc.sbuf_tensor("onescb", [128, 1], BF16) as onescb,
        nc.sbuf_tensor("onesr", [1, 128], F32) as onesr,
        nc.sbuf_tensor("rows", [1, 1536], F32) as rows,
        nc.sbuf_tensor("a_sb", [128, 128], F32) as a_sb,
        nc.sbuf_tensor("ptmp", [128, 128], F32) as ptmp,
        nc.sbuf_tensor("hstate", [128, 128], F32) as hstate,
        nc.sbuf_tensor("convw_sb", [128, 32], F32) as convw_sb,
        nc.sbuf_tensor("convb_sb", [128, 8], F32) as convb_sb,
        nc.sbuf_tensor("bdt_sb", [128, 8], F32) as bdt_sb,
        nc.sbuf_tensor("dp_sb", [128, 8], F32) as dp_sb,
        nc.Block() as block,
    ):
        r_mu = rows[0:1, 0:512]
        r_inv = rows[0:1, 512:1024]
        r_muinv = rows[0:1, 1024:1536]

        @on("sync")
        def _(sy):
            sy.dma_start(out=iden_sb[:, :], in_=iden).then_inc(sW.h, 16)
            ld3(sy, convw_sb[:, :], convw, 8, sW.h)
            ld3(sy, convb_sb[:, :], convb, 8, sW.h, slow=True)
            ld3(sy, bdt_sb[:, :], bdt, 8, sW.h, slow=True)
            ld3(sy, dp_sb[:, :], dpv, 8, sW.h, slow=True)
            ld3(sy, a_sb[:, :], alog, 8, sW.h)
        w_const = sW.inc(16 * 6)

        @on("vector")
        def _(v, w=w_const):
            v.wait_ge(sW.h, w)
            if dbg:
                v.memset(stream[:, :], 0.0)
            v.memset(onesc[:, :], 1.0)
            v.memset(onescb[:, :], 1.0)
            v.memset(onesr[:, :], 1.0)
            v.memset(hstate[:, :], 0.0)
            v.engine_nop().then_inc(sV.h, 1)
        sV.inc()

        @on("scalar")
        def _(s, w=w_const):
            s.wait_ge(sW.h, w)
            s.activation(ptmp[:, :], a_sb[:, :], AF.Exp, bias=0.0,
                         scale=1.0).then_inc(sS.h, 1)
        s_expA = sS.inc()

        @on("vector")
        def _(v, w=s_expA):
            v.wait_ge(sS.h, w)
            v.tensor_scalar(out=a_sb[:, :], in0=ptmp[:, :], scalar1=-1.0, scalar2=None,
                            op0=ALU.mult)
            v.engine_nop().then_inc(sV.h, 1)
        V_INIT = sV.inc()

        @on("tensor")
        def _(t, w=V_INIT):
            t.wait_ge(sV.h, w)

        @on("scalar")
        def _(s, w=V_INIT):
            s.wait_ge(sV.h, w)

        with nc.sbuf_tensor("stg", [128, 16384], BF16) as stg:
            @on("gpsimd")
            def _(g, base=sG.n):
                k = 0
                for e in range(NE):
                    ld3(g, stg[:, :], gup[e], 8, sG.h)
                    k += 1
                    g.wait_ge(sG.h, base + 16 * k)
                    g.dma_start(out=gup_bf[e], in_=stg[:, :]).then_inc(sG.h, 16)
                    k += 1
                    g.wait_ge(sG.h, base + 16 * k)
                for e in range(NE):
                    ld3(g, stg[:, 0:8192], dwn[e], 8, sG.h)
                    k += 1
                    g.wait_ge(sG.h, base + 16 * k)
                    g.dma_start(out=dwn_bf[e], in_=stg[:, 0:8192]).then_inc(sG.h, 16)
                    k += 1
                    g.wait_ge(sG.h, base + 16 * k)
            sG.inc(16 * 32)
            barrier()

        with contextlib.ExitStack() as _mst:
            m_win = _mst.enter_context(nc.sbuf_tensor("m_win", [128, 8 * 1024], BF16))
            m_wx = _mst.enter_context(nc.sbuf_tensor("m_wx", [128, 8 * 96], BF16))
            m_wdt = _mst.enter_context(nc.sbuf_tensor("m_wdt", [64, 1024], BF16))
            m_wout = _mst.enter_context(nc.sbuf_tensor("m_wout", [128, 8 * 1024], BF16))
            m_wrow = _mst.enter_context(nc.sbuf_tensor("m_wrow", [1, 2048], F32))
            m_x = _mst.enter_context(nc.sbuf_tensor("m_x", [128, 2 * 512], F32))
            m_sq = _mst.enter_context(nc.sbuf_tensor("m_sq", [128, 512], F32))
            m_xs = _mst.enter_context(nc.sbuf_tensor("m_xs", [128, 8 * 512], BF16))
            m_uext = _mst.enter_context(nc.sbuf_tensor("m_uext", [128, 2 * 516], F32))
            m_halo = _mst.enter_context(nc.sbuf_tensor("m_halo", [128, 8 * 3], F32))
            m_ucb = _mst.enter_context(nc.sbuf_tensor("m_ucb", [128, 512], F32))
            m_ubf = _mst.enter_context(nc.sbuf_tensor("m_ubf", [128, 2 * 512], BF16))
            m_t1 = _mst.enter_context(nc.sbuf_tensor("m_t1", [128, 512], F32))
            m_t2 = _mst.enter_context(nc.sbuf_tensor("m_t2", [128, 512], F32))
            m_t3 = _mst.enter_context(nc.sbuf_tensor("m_t3", [128, 512], F32))
            m_xdbl = _mst.enter_context(nc.sbuf_tensor("m_xdbl", [96, 1024], F32))
            m_dlt = _mst.enter_context(nc.sbuf_tensor("m_dlt", [64, 1024], BF16))
            m_res = _mst.enter_context(nc.sbuf_tensor("m_res", [128, 512], F32))
            b_dl = _mst.enter_context(nc.sbuf_tensor("b_dl", [128, 2 * 512], F32))
            b_u = _mst.enter_context(nc.sbuf_tensor("b_u", [128, 2 * 512], F32))
            b_rs = _mst.enter_context(nc.sbuf_tensor("b_rs", [128, 2 * 512], F32))
            b_du = _mst.enter_context(nc.sbuf_tensor("b_du", [128, 512], F32))
            b_dA = _mst.enter_context(nc.sbuf_tensor("b_dA", [128, 2 * 512], F32))
            b_dBu = _mst.enter_context(nc.sbuf_tensor("b_dBu", [128, 512], F32))
            b_h = _mst.enter_context(nc.sbuf_tensor("b_h", [128, 512], F32))
            b_hC = _mst.enter_context(nc.sbuf_tensor("b_hC", [128, 512], F32))
            b_y = _mst.enter_context(nc.sbuf_tensor("b_y", [128, 512], F32))
            b_bm = _mst.enter_context(nc.sbuf_tensor("b_bm", [128, 4 * 512], BF16))
            b_cm = _mst.enter_context(nc.sbuf_tensor("b_cm", [128, 4 * 512], BF16))
            b_ybf = _mst.enter_context(nc.sbuf_tensor("b_ybf", [128, 2 * 8 * 512], BF16))
            c_wo = _mst.enter_context(nc.sbuf_tensor("c_wo", [128, 512], F32))
            c_rs = _mst.enter_context(nc.sbuf_tensor("c_rs", [128, 256], F32))
            c_xr = _mst.enter_context(nc.sbuf_tensor("c_xr", [128, 256], F32))
            p_st = _mst.enter_context(nc.psum_tensor("p_st", [1, 1024], F32))
            p_inv = _mst.enter_context(nc.psum_tensor("p_inv", [128, 512], F32))
            p_mm = _mst.enter_context(nc.psum_tensor("p_mm", [128, 2 * 512], F32))
            p_wx = _mst.enter_context(nc.psum_tensor("p_wx", [96, 512], F32))
            p_dl = _mst.enter_context(nc.psum_tensor("p_dl", [128, 2 * 512], F32))
            _dummy_ctx = True

            @on("gpsimd")
            def _(g):
                ld3(g, m_wx[:, :], wx_t, 8, smw.h)
                g.dma_start(out=m_wdt[:, :], in_=wdt_t).then_inc(smw.h, 16)
                ld3(g, m_wout[:, :], wout_t, 8, smw.h)
            w_mw = smw.inc(16 * 3)

            def load_win(half):
                t_prev = sT.n

                @on("gpsimd")
                def _(g, half=half, t_prev=t_prev):
                    g.wait_ge(sT.h, t_prev)
                    ld3(g, m_win[:, :], win_t[:, half * CH:(half + 1) * CH], 8, swin.h)
                return swin.inc(16)

            def emit_ws(half, wtag):
                marks = []
                for grp in range(2):
                    @on("tensor")
                    def _(t, wtag=wtag, grp=grp, vprev=(marks[-1] if marks else 0)):
                        t.wait_ge(swin.h, wtag)
                        if vprev:
                            t.wait_ge(sV.h, vprev)
                        for k in range(7):
                            t.matmul(p_dl[0:1, 0:512], onescb[:, :],
                                     m_win[:, k * 1024 + grp * 512:k * 1024 + grp * 512 + 512],
                                     start=(k == 0), stop=False)
                        t.matmul(p_dl[0:1, 0:512], onescb[:, :],
                                 m_win[:, 7 * 1024 + grp * 512:7 * 1024 + grp * 512 + 512],
                                 start=False, stop=True).then_inc(sT.h, 1)
                    tmk = sT.inc()

                    @on("vector")
                    def _(v, half=half, grp=grp, tmk=tmk):
                        v.wait_ge(sT.h, tmk)
                        v.tensor_scalar(
                            out=m_wrow[0:1,
                                       half * 1024 + grp * 512: half * 1024 + grp * 512 + 512],
                            in0=p_dl[0:1, 0:512], scalar1=-1.0, scalar2=None, op0=ALU.mult)
                        v.engine_nop().then_inc(sV.h, 1)
                    marks.append(sV.inc())
                return marks[-1]

            ws_done = [0, 0]
            ci_cnt = [0]
            prev_xs = [0]
            pmm_free = {0: 0, 1: 0}

            def emit_stats_xs(c):
                ci = ci_cnt[0]
                cols = slice(c * 512, (c + 1) * 512)
                for k in range(8):
                    @on("sync")
                    def _(sy, k=k, c=c, cols=cols, tmark=sT.n, smark=sS.n,
                          vmark=prev_xs[0]):
                        if k >= 2:
                            sy.wait_ge(sT.h, tmark)
                            sy.wait_ge(sS.h, smark)
                        elif vmark:
                            sy.wait_ge(sV.h, vmark)
                        sy.dma_start(out=m_x[:, (k % 2) * 512:(k % 2) * 512 + 512],
                                     in_=xbt[k * 128:(k + 1) * 128, cols]
                                     ).then_inc(sxp[k % 2].h, 16)
                    xl = sxp[k % 2].inc(16)

                    @on("scalar")
                    def _(s, k=k, xl=xl, tmark=sT.n):
                        s.wait_ge(sxp[k % 2].h, xl)
                        s.wait_ge(sT.h, tmark)
                        s.activation(m_sq[:, :], m_x[:, (k % 2) * 512:(k % 2) * 512 + 512],
                                     AF.Square, bias=0.0, scale=1.0).then_inc(sS.h, 1)
                    sqd = sS.inc()

                    @on("tensor")
                    def _(t, k=k, xl=xl, sqd=sqd, ci=ci):
                        t.wait_ge(sxp[k % 2].h, xl)
                        if k == 0:
                            t.wait_ge(sPst.h, ci)
                        t.matmul(p_st[0:1, 0:512], onesc[:, :],
                                 m_x[:, (k % 2) * 512:(k % 2) * 512 + 512],
                                 start=(k == 0), stop=(k == 7))
                        t.wait_ge(sS.h, sqd)
                        t.matmul(p_st[0:1, 512:1024], onesc[:, :], m_sq[:, :],
                                 start=(k == 0), stop=(k == 7)).then_inc(sT.h, 1)
                    sT.inc()
                t_stats = sT.n

                @on("vector")
                def _(v, t_stats=t_stats):
                    v.wait_ge(sT.h, t_stats)
                    v.tensor_scalar(out=r_mu, in0=p_st[0:1, 0:512], scalar1=1.0 / DM,
                                    scalar2=None, op0=ALU.mult)
                    v.tensor_scalar(out=m_sq[0:1, :], in0=p_st[0:1, 512:1024],
                                    scalar1=1.0 / DM, scalar2=None, op0=ALU.mult)
                    v.tensor_tensor(m_t1[0:1, :], r_mu, r_mu, ALU.mult)
                    v.tensor_tensor(m_sq[0:1, :], m_sq[0:1, :], m_t1[0:1, :], ALU.subtract)
                    v.tensor_scalar(out=m_sq[0:1, :], in0=m_sq[0:1, :], scalar1=EPS_LN,
                                    scalar2=None, op0=ALU.add)
                    v.engine_nop().then_inc(sV.h, 1)
                    v.engine_nop().then_inc(sPst.h, 1)
                v_rm = sV.inc()
                sPst.inc()

                @on("scalar")
                def _(s, v_rm=v_rm):
                    s.wait_ge(sV.h, v_rm)
                    s.activation(m_t1[0:1, :], m_sq[0:1, :], AF.Ln, bias=0.0, scale=1.0)
                    s.activation(r_inv, m_t1[0:1, :], AF.Exp, bias=0.0,
                                 scale=-0.5).then_inc(sS.h, 1)
                s_inv = sS.inc()

                @on("vector")
                def _(v, s_inv=s_inv):
                    v.wait_ge(sS.h, s_inv)
                    v.tensor_tensor(r_muinv, r_mu, r_inv, ALU.mult)
                    v.engine_nop().then_inc(sV.h, 1)
                v_muinv = sV.inc()

                @on("tensor")
                def _(t, v_muinv=v_muinv, ci=ci):
                    t.wait_ge(sV.h, v_muinv)
                    t.wait_ge(sXs.h, ci)
                    t.matmul(p_inv[:, :], onesr[:, :], r_inv, start=True, stop=True).then_inc(sT.h, 1)
                t_invrep = sT.inc()

                for k in range(8):
                    @on("sync")
                    def _(sy, k=k, cols=cols, vmark=sV.n, t_stats=t_stats, smark=sS.n):
                        if k >= 2:
                            sy.wait_ge(sV.h, vmark)
                        else:
                            sy.wait_ge(sT.h, t_stats)
                            sy.wait_ge(sS.h, smark)
                        sy.dma_start(out=m_x[:, (k % 2) * 512:(k % 2) * 512 + 512],
                                     in_=xbt[k * 128:(k + 1) * 128, cols]
                                     ).then_inc(sxp[k % 2].h, 16)
                    xl = sxp[k % 2].inc(16)

                    @on("vector")
                    def _(v, k=k, xl=xl, t_invrep=t_invrep):
                        v.wait_ge(sxp[k % 2].h, xl)
                        v.wait_ge(sT.h, t_invrep)
                        v.tensor_tensor(m_xs[:, k * 512:(k + 1) * 512],
                                        m_x[:, (k % 2) * 512:(k % 2) * 512 + 512],
                                        p_inv[:, :], ALU.mult)
                        v.engine_nop().then_inc(sV.h, 1)
                        if k == 7:
                            v.engine_nop().then_inc(sXs.h, 1)
                    sV.inc()
                sXs.inc()
                ci_cnt[0] += 1
                prev_xs[0] = sV.n
                return sV.n

            def emit_win_half(c, half, wtag, v_xs):
                cols = slice(c * 512, (c + 1) * 512)
                for m in range(8):
                    pm = m % 2

                    @on("tensor")
                    def _(t, m=m, pm=pm, v_xs=v_xs, wtag=wtag, half=half,
                          smark=pmm_free[pm], wsv=ws_done[half]):
                        t.wait_ge(sV.h, max(v_xs, wsv))
                        t.wait_ge(swin.h, wtag)
                        if smark:
                            t.wait_ge(sS.h, smark)
                        for k in range(8):
                            t.matmul(p_mm[:, pm * 512:pm * 512 + 512],
                                     m_win[:, k * 1024 + m * 128:k * 1024 + m * 128 + 128],
                                     m_xs[:, k * 512:(k + 1) * 512],
                                     start=(k == 0), stop=False)
                        t.matmul(p_mm[:, pm * 512:pm * 512 + 512],
                                 m_wrow[0:1, half * 1024 + m * 128:half * 1024 + m * 128 + 128],
                                 r_muinv, start=False, stop=True).then_inc(sT.h, 1)
                    t_m = sT.inc()

                    if half == 0:
                        @on("vector")
                        def _(v, m=m, pm=pm, t_m=t_m, c=c, smark=sS.n):
                            v.wait_ge(sT.h, t_m)
                            v.wait_ge(sS.h, smark)
                            if c == 0:
                                v.memset(m_uext[:, pm * 516:pm * 516 + 3], 0.0)
                            else:
                                v.tensor_copy(m_uext[:, pm * 516:pm * 516 + 3],
                                              m_halo[:, m * 3:(m + 1) * 3])
                            v.engine_nop().then_inc(sV.h, 1)
                        v_halo = sV.inc()

                        @on("scalar")
                        def _(s, m=m, pm=pm, t_m=t_m, v_halo=v_halo):
                            s.wait_ge(sT.h, t_m)
                            s.wait_ge(sV.h, v_halo)
                            s.activation(m_uext[:, pm * 516 + 3:pm * 516 + 515],
                                         p_mm[:, pm * 512:pm * 512 + 512],
                                         AF.Copy, bias=0.0, scale=1.0).then_inc(sS.h, 1)
                        s_ev = sS.inc()
                        pmm_free[pm] = s_ev

                        @on("vector")
                        def _(v, m=m, pm=pm, s_ev=s_ev):
                            v.wait_ge(sS.h, s_ev)
                            v.tensor_copy(m_halo[:, m * 3:(m + 1) * 3],
                                          m_uext[:, pm * 516 + 512:pm * 516 + 515])
                            v.tensor_scalar(out=m_t1[:, :],
                                            in0=m_uext[:, pm * 516:pm * 516 + 512],
                                            scalar1=convw_sb[:, m * 4:m * 4 + 1],
                                            scalar2=None, op0=ALU.mult)
                            for kk in range(1, 4):
                                v.scalar_tensor_tensor(
                                    m_t1[:, :],
                                    m_uext[:, pm * 516 + kk:pm * 516 + kk + 512],
                                    convw_sb[:, m * 4 + kk:m * 4 + kk + 1],
                                    m_t1[:, :], ALU.mult, ALU.add)
                            v.engine_nop().then_inc(sV.h, 1)
                        v_conv = sV.inc()

                        @on("scalar")
                        def _(s, m=m, pm=pm, v_conv=v_conv, ymark=sup[0].n):
                            s.wait_ge(sV.h, v_conv)
                            s.wait_ge(sup[0].h, ymark)
                            s.activation(m_ucb[:, :], m_t1[:, :],
                                         AF.Silu, bias=convb_sb[:, m:m + 1],
                                         scale=1.0).then_inc(sS.h, 1)
                        s_silu = sS.inc()

                        @on("vector")
                        def _(v, pm=pm, s_silu=s_silu):
                            v.wait_ge(sS.h, s_silu)
                            v.tensor_copy(m_ubf[:, pm * 512:pm * 512 + 512], m_ucb[:, :])
                            v.engine_nop().then_inc(sV.h, 1)
                        v_ubf = sV.inc()

                        @on("sync")
                        def _(sy, m=m, pm=pm, cols=cols, s_silu=s_silu):
                            sy.wait_ge(sS.h, s_silu)
                            sy.dma_start(out=u_hbm[m * 128:(m + 1) * 128, cols],
                                         in_=m_ucb[:, :]).then_inc(sup[0].h, 16)
                        sup[0].inc(16)

                        @on("tensor")
                        def _(t, m=m, pm=pm, v_ubf=v_ubf, w_mw=w_mw):
                            t.wait_ge(smw.h, w_mw)
                            t.wait_ge(sV.h, v_ubf)
                            t.matmul(p_wx[0:96, :], m_wx[:, m * 96:(m + 1) * 96],
                                     m_ubf[:, pm * 512:pm * 512 + 512],
                                     start=(m == 0), stop=(m == 7)).then_inc(sT.h, 1)
                        sT.inc()
                    else:
                        @on("scalar")
                        def _(s, m=m, pm=pm, t_m=t_m, ymark=srp[0].n):
                            s.wait_ge(sT.h, t_m)
                            s.wait_ge(srp[0].h, ymark)
                            s.activation(m_res[:, :],
                                         p_mm[:, pm * 512:pm * 512 + 512],
                                         AF.Silu, bias=0.0, scale=1.0).then_inc(sS.h, 1)
                        s_rs = sS.inc()
                        pmm_free[pm] = s_rs

                        @on("sync")
                        def _(sy, m=m, pm=pm, cols=cols, s_rs=s_rs):
                            sy.wait_ge(sS.h, s_rs)
                            sy.dma_start(out=rs_hbm[m * 128:(m + 1) * 128, cols],
                                         in_=m_res[:, :]).then_inc(srp[0].h, 16)
                        srp[0].inc(16)

                if half == 0:
                    t_wx = sT.n

                    @on("vector")
                    def _(v, c=c, t_wx=t_wx, ymark=sxd.n):
                        v.wait_ge(sT.h, t_wx)
                        v.wait_ge(sxd.h, ymark)
                        v.tensor_copy(m_xdbl[0:96, (c % 2) * 512:(c % 2) * 512 + 512],
                                      p_wx[0:96, :])
                        v.engine_nop().then_inc(sV.h, 1)
                    v_wx = sV.inc()

                    @on("sync")
                    def _(sy, c=c, v_wx=v_wx):
                        sy.wait_ge(sV.h, v_wx)
                        sy.dma_start(out=xdbl_in[c // 2][:, (c % 2) * 512:(c % 2) * 512 + 512],
                                     in_=m_xdbl[0:96, (c % 2) * 512:(c % 2) * 512 + 512]
                                     ).then_inc(sxd.h, 16)
                    sxd.inc(16)

            def emit_delta(sc, c_ar, v_ws0):
                @on("gpsimd")
                def _(g, sc=sc, c_ar=c_ar):
                    g.wait_ge(sCar.h, c_ar)
                    g.dma_start(out=m_dlt[:, :], in_=xdbl_out[sc][0:64, :]).then_inc(sdl1.h, 16)
                dl_ld = sdl1.inc(16)

                dmarks = {}
                spmk = {0: 0, 1: 0}
                vdmk = {0: 0, 1: 0}
                stmk = {0: 0, 1: 0}
                for cb in range(8):
                    for tc in range(2):
                        pp = (cb * 2 + tc) % 2

                        @on("tensor")
                        def _(t, cb=cb, tc=tc, pp=pp, dl_ld=dl_ld, w_mw=w_mw,
                              sm=spmk[pp], vm=vdmk[pp], vmark=sV.n, v_ws0=v_ws0):
                            t.wait_ge(sdl1.h, dl_ld)
                            t.wait_ge(smw.h, w_mw)
                            if sm:
                                t.wait_ge(sS.h, sm)
                            if vm:
                                t.wait_ge(sV.h, vm)
                            if not (cb or tc):
                                t.wait_ge(sV.h, max(vmark, v_ws0))
                            t.matmul(p_dl[:, pp * 512:pp * 512 + 512],
                                     m_wdt[:, cb * 128:(cb + 1) * 128],
                                     m_dlt[:, tc * 512:(tc + 1) * 512],
                                     start=True, stop=True).then_inc(sT.h, 1)
                        t_d = sT.inc()

                        dX = m_t2 if pp == 0 else m_t1
                        dY = m_t3 if pp == 0 else m_res

                        @on("scalar")
                        def _(s, cb=cb, pp=pp, dX=dX, dY=dY, t_d=t_d, vm=vdmk[pp]):
                            s.wait_ge(sT.h, t_d)
                            if vm:
                                s.wait_ge(sV.h, vm)
                            s.activation(dX[:, :],
                                         p_dl[:, pp * 512:pp * 512 + 512], AF.Abs,
                                         bias=bdt_sb[:, cb:cb + 1], scale=1.0)
                            s.activation(dY[:, :], dX[:, :],
                                         AF.Exp, bias=0.0, scale=-1.0)
                            s.activation(dX[:, :], dY[:, :],
                                         AF.Ln, bias=1.0,
                                         scale=1.0).then_inc(sS.h, 1)
                        s_sp = sS.inc()
                        spmk[pp] = s_sp

                        @on("vector")
                        def _(v, cb=cb, tc=tc, pp=pp, dX=dX, dY=dY, s_sp=s_sp,
                              ym=stmk[pp], rp0=srp[0].n, rp1=srp[1].n):
                            v.wait_ge(sS.h, s_sp)
                            if ym:
                                v.wait_ge(sdst.h, ym)
                            elif pp == 1:
                                v.wait_ge(srp[0].h, rp0)
                                v.wait_ge(srp[1].h, rp1)
                            v.tensor_scalar(out=dY[:, :],
                                            in0=p_dl[:, pp * 512:pp * 512 + 512],
                                            scalar1=bdt_sb[:, cb:cb + 1], scalar2=0.0,
                                            op0=ALU.add, op1=ALU.max)
                            v.tensor_tensor(dY[:, :], dY[:, :], dX[:, :], ALU.add)
                            v.engine_nop().then_inc(sV.h, 1)
                        v_d = sV.inc()
                        vdmk[pp] = v_d

                        @on("sync")
                        def _(sy, sc=sc, cb=cb, tc=tc, dY=dY, v_d=v_d):
                            sy.wait_ge(sV.h, v_d)
                            sy.dma_start(
                                out=dl_hbm[cb * 128:(cb + 1) * 128,
                                           sc * 1024 + tc * 512:sc * 1024 + tc * 512 + 512],
                                in_=dY[:, :]).then_inc(sdst.h, 16)
                        dmarks[(cb, tc)] = sdst.inc(16)
                        stmk[pp] = dmarks[(cb, tc)]
                return dmarks

            scan_marks = []
            cb_marks = []
            passc_t = {}

            def emit_passB(hsc, y_dl):
                sc = hsc // 2
                half = hsc % 2
                yb = (hsc % 2) * 4096
                colg = slice(sc * 1024 + half * 512, sc * 1024 + half * 512 + 512)
                tprev = passc_t.get(hsc - 2, 0)
                for cb in range(8):
                    pb = cb % 2
                    slab_free = cb_marks[-2] if len(cb_marks) >= 2 else 0

                    @on("sync")
                    def _(sy, cb=cb, pb=pb, colg=colg, y_dl=y_dl, vmark=slab_free,
                          dmk=y_dl[4][(cb, half)]):
                        sy.wait_ge(sup[0].h, y_dl[0])
                        sy.wait_ge(sup[1].h, y_dl[1])
                        sy.wait_ge(srp[0].h, y_dl[2])
                        sy.wait_ge(srp[1].h, y_dl[3])
                        sy.wait_ge(sdst.h, dmk)
                        if vmark:
                            sy.wait_ge(sV.h, vmark)
                        sy.dma_start(out=b_dl[:, pb * 512:pb * 512 + 512],
                                     in_=dl_hbm[cb * 128:(cb + 1) * 128, colg]
                                     ).then_inc(slp[pb].h, 16)
                        sy.dma_start(out=b_u[:, pb * 512:pb * 512 + 512],
                                     in_=u_hbm[cb * 128:(cb + 1) * 128, colg]
                                     ).then_inc(slp[pb].h, 16)

                    @on("scalar")
                    def _(s, cb=cb, pb=pb, colg=colg, y_dl=y_dl, vmark=slab_free,
                          dmk=y_dl[4][(cb, half)]):
                        s.wait_ge(srp[0].h, y_dl[2])
                        s.wait_ge(srp[1].h, y_dl[3])
                        s.wait_ge(sdst.h, dmk)
                        if vmark:
                            s.wait_ge(sV.h, vmark)
                        s.dma_start(out=b_rs[:, pb * 512:pb * 512 + 512],
                                    in_=rs_hbm[cb * 128:(cb + 1) * 128, colg]
                                    ).then_inc(slp[pb].h, 16)
                    x_sl = slp[pb].inc(16 * 3)

                    @on("vector")
                    def _(v, cb=cb, pb=pb, x_sl=x_sl):
                        v.wait_ge(slp[pb].h, x_sl)
                        v.tensor_tensor(b_du[:, :], b_dl[:, pb * 512:pb * 512 + 512],
                                        b_u[:, pb * 512:pb * 512 + 512], ALU.mult)
                        v.engine_nop().then_inc(sV.h, 1)
                    sV.inc()

                    for n in range(NST):
                        pn = n % 4
                        bm_free = scan_marks[-4] if len(scan_marks) >= 4 else 0
                        da_free = scan_marks[-2] if len(scan_marks) >= 2 else 0

                        @on("gpsimd")
                        def _(g, sc=sc, half=half, n=n, pn=pn, vfree=bm_free):
                            if vfree:
                                g.wait_ge(sV.h, vfree)
                            g.dma_start(
                                out=b_bm[:, pn * 512:pn * 512 + 512],
                                in_=xdbl_out[sc][64 + n:65 + n, half * 512:half * 512 + 512]
                                .broadcast_to([128, 512])).then_inc(sbp[pn].h, 16)
                            g.dma_start(
                                out=b_cm[:, pn * 512:pn * 512 + 512],
                                in_=xdbl_out[sc][80 + n:81 + n, half * 512:half * 512 + 512]
                                .broadcast_to([128, 512])).then_inc(sbp[pn].h, 16)
                        g_bc = sbp[pn].inc(32)

                        @on("scalar")
                        def _(s, cb=cb, pb=pb, n=n, x_sl=x_sl, vfree=da_free):
                            s.wait_ge(slp[pb].h, x_sl)
                            if vfree:
                                s.wait_ge(sV.h, vfree)
                            s.activation(b_dA[:, (n % 2) * 512:(n % 2) * 512 + 512],
                                         b_dl[:, pb * 512:pb * 512 + 512], AF.Exp, bias=0.0,
                                         scale=a_sb[:, cb * 16 + n:cb * 16 + n + 1]
                                         ).then_inc(sS.h, 1)
                        s_dA = sS.inc()

                        @on("vector")
                        def _(v, cb=cb, n=n, pn=pn, g_bc=g_bc, s_dA=s_dA):
                            v.wait_ge(sbp[pn].h, g_bc)
                            v.tensor_tensor(b_dBu[:, :], b_du[:, :],
                                            b_bm[:, pn * 512:pn * 512 + 512], ALU.mult)
                            v.wait_ge(sS.h, s_dA)
                            v.tensor_tensor_scan(
                                b_h[:, :], b_dA[:, (n % 2) * 512:(n % 2) * 512 + 512],
                                b_dBu[:, :],
                                hstate[:, cb * 16 + n:cb * 16 + n + 1],
                                ALU.mult, ALU.add)
                            v.tensor_copy(hstate[:, cb * 16 + n:cb * 16 + n + 1],
                                          b_h[:, 511:512])
                            v.tensor_tensor(b_hC[:, :], b_h[:, :],
                                            b_cm[:, pn * 512:pn * 512 + 512], ALU.mult)
                            if n == 0:
                                v.tensor_copy(b_y[:, :], b_hC[:, :])
                            else:
                                v.tensor_tensor(b_y[:, :], b_y[:, :], b_hC[:, :], ALU.add)
                            v.engine_nop().then_inc(sV.h, 1)
                        scan_marks.append(sV.inc())

                    @on("vector")
                    def _(v, cb=cb, pb=pb, yb=yb, tprev=tprev):
                        if tprev:
                            v.wait_ge(sT.h, tprev)
                        v.scalar_tensor_tensor(b_y[:, :], b_u[:, pb * 512:pb * 512 + 512],
                                               dp_sb[:, cb:cb + 1], b_y[:, :],
                                               ALU.mult, ALU.add)
                        v.tensor_tensor(b_ybf[:, yb + cb * 512:yb + (cb + 1) * 512],
                                        b_y[:, :],
                                        b_rs[:, pb * 512:pb * 512 + 512], ALU.mult)
                        v.engine_nop().then_inc(sV.h, 1)
                    cb_marks.append(sV.inc())
                    if dbg:
                        @on("sync")
                        def _(sy, hsc=hsc, cb=cb, yb=yb, colg=colg, vm=cb_marks[-1]):
                            sy.wait_ge(sV.h, vm)
                            sy.dma_start(
                                out=ybf_dbg[cb * 128:(cb + 1) * 128, colg],
                                in_=b_ybf[:, yb + cb * 512:yb + (cb + 1) * 512]
                                ).then_inc(sY.h, 16)
                        sY.inc(16)
                return sV.n

            def emit_passC(hsc, v_ybf):
                yb = (hsc % 2) * 4096
                for dm in range(8):
                    pm = dm % 2

                    @on("tensor")
                    def _(t, dm=dm, pm=pm, v_ybf=v_ybf, w_mw=w_mw,
                          smark=pmm_free[pm], yb=yb):
                        t.wait_ge(sV.h, v_ybf)
                        t.wait_ge(smw.h, w_mw)
                        if smark:
                            t.wait_ge(sS.h, smark)
                        for cb in range(7):
                            t.matmul(p_mm[:, pm * 512:pm * 512 + 512],
                                     m_wout[:, cb * 1024 + dm * 128:cb * 1024 + dm * 128 + 128],
                                     b_ybf[:, yb + cb * 512:yb + (cb + 1) * 512],
                                     start=(cb == 0), stop=False)
                        t.matmul(p_mm[:, pm * 512:pm * 512 + 512],
                                 m_wout[:, 7 * 1024 + dm * 128:7 * 1024 + dm * 128 + 128],
                                 b_ybf[:, yb + 7 * 512:yb + 8 * 512],
                                 start=False, stop=True).then_inc(sT.h, 1)
                    t_w = sT.inc()

                    @on("scalar")
                    def _(s, pm=pm, t_w=t_w, ymark=swt.n):
                        s.wait_ge(sT.h, t_w)
                        s.wait_ge(swt.h, ymark)
                        s.activation(c_wo[:, :], p_mm[:, pm * 512:pm * 512 + 512],
                                     AF.Copy, bias=0.0, scale=1.0).then_inc(sS.h, 1)
                    s_w = sS.inc()
                    pmm_free[pm] = s_w

                    @on("sync")
                    def _(sy, hsc=hsc, dm=dm, s_w=s_w):
                        sy.wait_ge(sS.h, s_w)
                        src = c_wo[:, :].rearrange("p (a b) -> p a b", b=128)
                        sy.dma_start(
                            out=rsin[hsc][0][dm * 128:(dm + 1) * 128, :]
                            .rearrange("p (a b) -> p a b", b=128),
                            in_=src[:, 0:4:2, :]).then_inc(swt.h, 16)
                        sy.dma_start(
                            out=rsin[hsc][1][dm * 128:(dm + 1) * 128, :]
                            .rearrange("p (a b) -> p a b", b=128),
                            in_=src[:, 1:4:2, :]).then_inc(swt.h, 16)
                    swt.inc(32)
                y_w = swt.n

                passc_t[hsc] = sT.n

                @on("gpsimd")
                def _(g, hsc=hsc, y_w=y_w):
                    g.wait_ge(swt.h, y_w)
                    g.collective_compute(
                        "ReduceScatter", ALU.add,
                        replica_groups=[[0, 1], [2, 3], [4, 5], [6, 7]],
                        ins=[rsin[hsc]], outs=[rsout[hsc]],
                    ).then_inc(sCrs.h, 1)
                return sCrs.inc()

            def emit_stream_write(hsc, c_rs_done):
                if dbg:
                    @on("gpsimd")
                    def _(g, hsc=hsc, c_rs_done=c_rs_done):
                        g.wait_ge(sCrs.h, c_rs_done)
                        g.dma_start(out=rso_dbg[hsc], in_=rsout[hsc]).then_inc(sG.h, 16)
                        g.dma_start(out=rsi_dbg[hsc], in_=rsin[hsc]).then_inc(sG.h, 16)
                    sG.inc(32)
                for dm in range(8):
                    pm = dm % 2

                    @on("sync")
                    def _(sy, hsc=hsc, dm=dm, c_rs_done=c_rs_done, vmark=sV.n):
                        sy.wait_ge(sCrs.h, c_rs_done)
                        sy.wait_ge(sV.h, vmark)
                        sy.dma_start(out=c_rs[:, :],
                                     in_=rsout[hsc][dm * 128:(dm + 1) * 128, :]
                                     ).then_inc(scp[0].h, 16)
                        sy.dma_start(out=c_xr[:, :],
                                     in_=xres[dm * 128:(dm + 1) * 128,
                                              hsc * 256:(hsc + 1) * 256]).then_inc(scp[0].h, 16)
                    x_c = scp[0].inc(32)

                    @on("vector")
                    def _(v, hsc=hsc, dm=dm, x_c=x_c):
                        v.wait_ge(scp[0].h, x_c)
                        v.tensor_tensor(
                            stream[:, dm * T + hsc * 256:dm * T + hsc * 256 + 256],
                            c_rs[:, :], c_xr[:, :], ALU.add)
                        v.engine_nop().then_inc(sV.h, 1)
                    sV.inc()
                return sV.n

            pend_passC = []
            pend_rs = []
            stream_done = [0]
            for sc in range(_NSC):
                for c in (2 * sc, 2 * sc + 1):
                    w_u = load_win(0)
                    if c == 0:
                        ws_done[0] = emit_ws(0, w_u)
                    v_xs = emit_stats_xs(c)
                    emit_win_half(c, 0, w_u, v_xs)
                    w_r = load_win(1)
                    if c == 0:
                        ws_done[1] = emit_ws(1, w_r)
                    emit_win_half(c, 1, w_r, v_xs)

                y_xdbl = sxd.n

                @on("gpsimd")
                def _(g, sc=sc, y_xdbl=y_xdbl):
                    g.wait_ge(sxd.h, y_xdbl)
                    g.collective_compute(
                        "AllReduce", ALU.add,
                        replica_groups=[[0, 1], [2, 3], [4, 5], [6, 7]],
                        ins=[xdbl_in[sc]], outs=[xdbl_out[sc]],
                    ).then_inc(sCar.h, 1)
                c_ar = sCar.inc()

                dmarks = emit_delta(sc, c_ar, max(ws_done))
                y_dl = (sup[0].n, sup[1].n, srp[0].n, srp[1].n, dmarks)

                for half in range(2):
                    hsc = 2 * sc + half
                    v_ybf = emit_passB(hsc, y_dl)
                    pend_passC.append((hsc, v_ybf))
                    while pend_passC and pend_passC[0][0] <= hsc - 1:
                        h0, vy = pend_passC.pop(0)
                        pend_rs.append((h0, emit_passC(h0, vy)))
                while pend_rs and pend_rs[0][0] <= 2 * sc - 4:
                    h0, cd = pend_rs.pop(0)
                    stream_done[0] = emit_stream_write(h0, cd)

            for h0, vy in pend_passC:
                pend_rs.append((h0, emit_passC(h0, vy)))
            for h0, cd in pend_rs:
                stream_done[0] = emit_stream_write(h0, cd)

            if stage == "mamba":
                @on("sync")
                def _(sy, v=stream_done[0], tgt=sY.n + 16 * 8):
                    sy.wait_ge(sV.h, v)
                    for cc in range(8):
                        sy.dma_start(out=sdump[cc],
                                     in_=stream[:, cc * T:(cc + 1) * T]).then_inc(sY.h, 16)
                    sy.wait_ge(sY.h, tgt)
                sY.inc(16 * 8)

        barrier()

        def emit_ffn(call):
            with contextlib.ExitStack() as _fst:
                f_xs = _fst.enter_context(nc.sbuf_tensor(f"f_xs{call}", [128, 8 * 1024], BF16))
                f_mrow = _fst.enter_context(nc.sbuf_tensor(f"f_mrow{call}", [1, 1024], F32))
                f_gup = _fst.enter_context(nc.sbuf_tensor(f"f_gup{call}", [128, 16384], BF16))
                f_dwn = _fst.enter_context(nc.sbuf_tensor(f"f_dwn{call}", [128, 8192], BF16))
                f_act = _fst.enter_context(nc.sbuf_tensor(f"f_act{call}", [128, 8 * 512], BF16))
                f_dacc = _fst.enter_context(nc.sbuf_tensor(f"f_dacc{call}", [128, 8 * 1024], F32))
                f_sg = _fst.enter_context(nc.sbuf_tensor(f"f_sg{call}", [128, 512], F32))
                f_a1 = _fst.enter_context(nc.sbuf_tensor(f"f_a1{call}", [128, 512], F32))
                f_sq = _fst.enter_context(nc.sbuf_tensor(f"f_sq{call}", [128, 512], F32))
                f_rt = _fst.enter_context(nc.sbuf_tensor(f"f_rt{call}", [8, 2048], F32))
                f_gw = _fst.enter_context(nc.sbuf_tensor(f"f_gw{call}", [128, 64], BF16))
                f_oh = _fst.enter_context(nc.sbuf_tensor(f"f_oh{call}", [8, 1024], F32))
                f_r8 = _fst.enter_context(nc.sbuf_tensor(f"f_r8{call}", [128, 48], F32))
                pf_st = _fst.enter_context(nc.psum_tensor(f"pf_st{call}", [1, 512], F32))
                pf_ms = _fst.enter_context(nc.psum_tensor(f"pf_ms{call}", [128, 512], F32))
                pf_g = _fst.enter_context(nc.psum_tensor(f"pf_g{call}", [128, 2 * 512], F32))
                pf_u = _fst.enter_context(nc.psum_tensor(f"pf_u{call}", [128, 2 * 512], F32))
                pf_d = _fst.enter_context(nc.psum_tensor(f"pf_d{call}", [128, 2 * 512], F32))

                sfg = mksem(f"sfg{call}")
                sdp = mksem(f"sdp{call}")
                sfw = mksem(f"sfw{call}")

                @on("gpsimd")
                def _(g):
                    ld3(g, f_gw[:, :], gw_t, 8, sfw.h)
                w_gw = sfw.inc(16)

                @on("sync")
                def _(sy):
                    sy.dma_start(out=f_oh[:, :], in_=oh8).then_inc(sW.h, 16)
                w_oh = sW.inc(16)

                v_rw = 0
                t_gu_last = {}
                t_dn_last = {}
                v_end = sV.n

                for H in range(2):
                    @on("sync")
                    def _(sy, H=H, tprev=t_gu_last.get((H - 1, NE - 1), 0)):
                        if tprev:
                            sy.wait_ge(sT.h, tprev)
                        sy.dma_start(out=f_gup[:, :], in_=gup_bf[0]).then_inc(sfg.h, 16)
                    gup_marks = {0: sfg.inc(16)}

                    @on("gpsimd")
                    def _(g, H=H, tprev=t_dn_last.get((H - 1, NE - 1), 0)):
                        if tprev:
                            g.wait_ge(sT.h, tprev)
                        g.dma_start(out=f_dwn[:, :], in_=dwn_bf[0]).then_inc(sdp.h, 16)
                    dwn_marks = {0: sdp.inc(16)}

                    for fl in range(2):
                        f = 2 * H + fl
                        scol = H * 1024 + fl * 512
                        for k in range(8):
                            @on("scalar")
                            def _(s, k=k, scol=scol, tmark=sT.n, vmark=sV.n):
                                s.wait_ge(sT.h, tmark)
                                if k == 0:
                                    s.wait_ge(sV.h, vmark)
                                s.activation(f_sq[:, :], stream[:, k * T + scol:k * T + scol + 512],
                                             AF.Square, bias=0.0, scale=1.0).then_inc(sS.h, 1)
                            sqd = sS.inc()

                            @on("tensor")
                            def _(t, k=k, scol=scol, sqd=sqd, vmark=sV.n):
                                if k == 0:
                                    t.wait_ge(sV.h, vmark)
                                t.matmul(pf_st[0:1, 0:512], onesc[:, :],
                                         stream[:, k * T + scol:k * T + scol + 512],
                                         start=(k == 0), stop=(k == 7))
                                t.wait_ge(sS.h, sqd)
                                t.matmul(pf_d[0:1, 512:1024], onesc[:, :], f_sq[:, :],
                                         start=(k == 0), stop=(k == 7)).then_inc(sT.h, 1)
                            sT.inc()
                        t_stats = sT.n

                        @on("vector")
                        def _(v, t_stats=t_stats):
                            v.wait_ge(sT.h, t_stats)
                            v.tensor_scalar(out=r_mu, in0=pf_st[0:1, 0:512], scalar1=1.0 / DM,
                                            scalar2=None, op0=ALU.mult)
                            v.tensor_scalar(out=f_sq[0:1, :], in0=pf_d[0:1, 512:1024],
                                            scalar1=1.0 / DM, scalar2=None, op0=ALU.mult)
                            v.tensor_tensor(r_inv, r_mu, r_mu, ALU.mult)
                            v.tensor_tensor(f_sq[0:1, :], f_sq[0:1, :], r_inv, ALU.subtract)
                            v.tensor_scalar(out=f_sq[0:1, :], in0=f_sq[0:1, :], scalar1=EPS_LN,
                                            scalar2=None, op0=ALU.add)
                            v.engine_nop().then_inc(sV.h, 1)
                        v_rm = sV.inc()

                        @on("scalar")
                        def _(s, v_rm=v_rm):
                            s.wait_ge(sV.h, v_rm)
                            s.activation(f_a1[0:1, :], f_sq[0:1, :], AF.Ln, bias=0.0, scale=1.0)
                            s.activation(r_inv, f_a1[0:1, :], AF.Exp, bias=0.0,
                                         scale=-0.5).then_inc(sS.h, 1)
                        s_inv = sS.inc()

                        @on("vector")
                        def _(v, fl=fl, s_inv=s_inv):
                            v.wait_ge(sS.h, s_inv)
                            v.tensor_tensor(r_muinv, r_mu, r_inv, ALU.mult)
                            v.tensor_copy(f_mrow[0:1, fl * 512:(fl + 1) * 512], r_muinv)
                            v.engine_nop().then_inc(sV.h, 1)
                        v_muinv = sV.inc()

                        @on("tensor")
                        def _(t, v_muinv=v_muinv, vmark=sV.n):
                            t.wait_ge(sV.h, v_muinv)
                            t.wait_ge(sV.h, vmark)
                            t.matmul(pf_ms[:, :], onesr[:, :], r_inv, start=True,
                                     stop=True)
                            t.matmul(pf_u[:, 0:512], onesr[:, :], r_muinv, start=True,
                                     stop=True).then_inc(sT.h, 1)
                        t_invrep = sT.inc()

                        for k in range(8):
                            @on("vector")
                            def _(v, k=k, fl=fl, scol=scol, H=H, t_invrep=t_invrep,
                                  tprev=t_gu_last.get((H - 1, NE - 1), 0)):
                                v.wait_ge(sT.h, t_invrep)
                                if k == 0 and tprev:
                                    v.wait_ge(sT.h, tprev)
                                v.tensor_tensor(f_a1[:, :],
                                                stream[:, k * T + scol:k * T + scol + 512],
                                                pf_ms[:, :], ALU.mult)
                                v.tensor_tensor(f_xs[:, k * 1024 + fl * 512:
                                                     k * 1024 + fl * 512 + 512],
                                                f_a1[:, :], pf_u[:, 0:512], ALU.subtract)
                                v.engine_nop().then_inc(sV.h, 1)
                            sV.inc()
                        v_xs = sV.n
                        if dbg and call == 0 and fl == 1:
                            @on("sync")
                            def _(sy, H=H, v_xs=v_xs):
                                sy.wait_ge(sV.h, v_xs)
                                sy.dma_start(out=xs2_dbg[H],
                                             in_=f_xs[:, :]).then_inc(sY.h, 16)
                            sY.inc(16)

                        for tt_ in range(4):
                            @on("tensor")
                            def _(t, tt_=tt_, fl=fl, v_xs=v_xs, w_gw=w_gw,
                                  v_rw=v_rw, vmark=sV.n):
                                t.wait_ge(sV.h, max(v_xs, v_rw))
                                t.wait_ge(sfw.h, w_gw)
                                if tt_ > 0:
                                    t.wait_ge(sV.h, vmark)
                                for k in range(7):
                                    t.matmul(pf_d[:, 0:8],
                                             f_xs[:, k * 1024 + fl * 512 + tt_ * 128:
                                                  k * 1024 + fl * 512 + tt_ * 128 + 128],
                                             f_gw[:, k * 8:(k + 1) * 8],
                                             start=(k == 0), stop=False)
                                t.matmul(pf_d[:, 0:8],
                                         f_xs[:, 7 * 1024 + fl * 512 + tt_ * 128:
                                              7 * 1024 + fl * 512 + tt_ * 128 + 128],
                                         f_gw[:, 56:64],
                                         start=False, stop=True).then_inc(sT.h, 1)
                            t_lg = sT.inc()

                            @on("vector")
                            def _(v, t_lg=t_lg):
                                v.wait_ge(sT.h, t_lg)
                                v.reduce_max(f_r8[:, 8:9], pf_d[:, 0:8], axis=AX)
                                v.drain()
                                v.tensor_scalar(out=f_r8[:, 9:10], in0=f_r8[:, 8:9], scalar1=-1.0,
                                                scalar2=None, op0=ALU.mult)
                                v.drain()
                                v.engine_nop().then_inc(sV.h, 1)
                            v_m8 = sV.inc()

                            @on("scalar")
                            def _(s, v_m8=v_m8):
                                s.wait_ge(sV.h, v_m8)
                                s.activation(f_r8[:, 16:24], pf_d[:, 0:8], AF.Exp,
                                             bias=f_r8[:, 9:10], scale=1.0,
                                             accum_out=f_r8[:, 10:11]).then_inc(sS.h, 1)
                            s_p8 = sS.inc()

                            @on("vector")
                            def _(v, s_p8=s_p8):
                                v.wait_ge(sS.h, s_p8)
                                v.reciprocal(f_r8[:, 11:12], f_r8[:, 10:11])
                                v.drain()
                                v.tensor_scalar(out=f_r8[:, 16:24], in0=f_r8[:, 16:24],
                                                scalar1=f_r8[:, 11:12], scalar2=None, op0=ALU.mult)
                                v.drain()
                                v.reduce_max(f_r8[:, 12:13], f_r8[:, 16:24], axis=AX)
                                v.drain()
                                v.tensor_scalar(out=f_r8[:, 24:32], in0=f_r8[:, 16:24],
                                                scalar1=f_r8[:, 12:13], scalar2=None, op0=ALU.is_ge)
                                v.drain()
                                v.scalar_tensor_tensor(f_r8[:, 32:40], f_r8[:, 24:32], -1e30,
                                                       f_r8[:, 16:24], ALU.mult, ALU.add)
                                v.drain()
                                v.reduce_max(f_r8[:, 13:14], f_r8[:, 32:40], axis=AX)
                                v.drain()
                                v.tensor_scalar(out=f_r8[:, 32:40], in0=f_r8[:, 32:40],
                                                scalar1=f_r8[:, 13:14], scalar2=None, op0=ALU.is_ge)
                                v.tensor_tensor(f_r8[:, 14:15], f_r8[:, 12:13], f_r8[:, 13:14],
                                                ALU.add)
                                v.drain()
                                v.tensor_scalar(out=f_r8[:, 15:16], in0=f_r8[:, 14:15],
                                                scalar1=EPS_R, scalar2=None, op0=ALU.add)
                                v.drain()
                                v.reciprocal(f_r8[:, 40:41], f_r8[:, 15:16])
                                v.drain()
                                v.tensor_tensor(f_r8[:, 41:42], f_r8[:, 14:15], f_r8[:, 40:41],
                                                ALU.mult)
                                v.drain()
                                v.tensor_scalar(out=f_r8[:, 41:42], in0=f_r8[:, 41:42],
                                                scalar1=EPS_R, scalar2=None, op0=ALU.add)
                                v.drain()
                                v.reciprocal(f_r8[:, 42:43], f_r8[:, 41:42])
                                v.drain()
                                v.tensor_tensor(f_r8[:, 43:44], f_r8[:, 40:41], f_r8[:, 42:43],
                                                ALU.mult)
                                v.drain()
                                v.tensor_tensor(f_r8[:, 44:45], f_r8[:, 12:13], f_r8[:, 43:44],
                                                ALU.mult)
                                v.tensor_tensor(f_r8[:, 45:46], f_r8[:, 13:14], f_r8[:, 43:44],
                                                ALU.mult)
                                v.drain()
                                v.tensor_scalar(out=f_r8[:, 0:8], in0=f_r8[:, 24:32],
                                                scalar1=f_r8[:, 44:45], scalar2=None, op0=ALU.mult)
                                v.drain()
                                v.scalar_tensor_tensor(f_r8[:, 0:8], f_r8[:, 32:40],
                                                       f_r8[:, 45:46], f_r8[:, 0:8],
                                                       ALU.mult, ALU.add)
                                v.engine_nop().then_inc(sV.h, 1)
                            v_rtd = sV.inc()

                            @on("tensor")
                            def _(t, v_rtd=v_rtd, w_oh=w_oh):
                                t.wait_ge(sV.h, v_rtd)
                                t.wait_ge(sW.h, w_oh)
                                t.transpose(pf_ms[0:8, 0:128], f_r8[:, 0:8],
                                            iden_sb[:, :]).then_inc(sT.h, 1)
                            t_tr = sT.inc()

                            @on("vector")
                            def _(v, tt_=tt_, f=f, t_tr=t_tr):
                                v.wait_ge(sT.h, t_tr)
                                v.tensor_copy(f_rt[0:8, f * 512 + tt_ * 128:
                                                   f * 512 + tt_ * 128 + 128],
                                              pf_ms[0:8, 0:128])
                                v.engine_nop().then_inc(sV.h, 1)
                            sV.inc()
                    v_rt = sV.n

                    silu_marks = {0: 0, 1: 0}
                    gating_marks = {0: 0, 1: 0}
                    evac_marks = {0: 0, 1: 0}
                    for e in range(NE):
                        w_g = gup_marks[e]
                        w_d = dwn_marks[e]

                        for tch in range(2):
                            @on("tensor")
                            def _(t, e=e, H=H, tch=tch, v_rt=v_rt, w_oh=w_oh, vmark=sV.n):
                                t.wait_ge(sV.h, max(v_rt, vmark))
                                t.wait_ge(sW.h, w_oh)
                                t.matmul(pf_ms[:, :], f_oh[0:8, e * 128:(e + 1) * 128],
                                         f_rt[0:8, H * 1024 + tch * 512:
                                              H * 1024 + tch * 512 + 512],
                                         start=True, stop=True).then_inc(sT.h, 1)
                            t_rep = sT.inc()

                            for qi in range(8):
                                pp = qi % 2

                                @on("tensor")
                                def _(t, e=e, qi=qi, pp=pp, tch=tch, w_g=w_g,
                                      sm=silu_marks[pp], gm=gating_marks[pp]):
                                    t.wait_ge(sfg.h, w_g)
                                    if sm:
                                        t.wait_ge(sS.h, sm)
                                    if gm:
                                        t.wait_ge(sV.h, gm)
                                    for kc in range(8):
                                        t.matmul(pf_g[:, pp * 512:pp * 512 + 512],
                                                 f_gup[:, kc * 2048 + qi * 128:
                                                       kc * 2048 + qi * 128 + 128],
                                                 f_xs[:, kc * 1024 + tch * 512:
                                                      kc * 1024 + tch * 512 + 512],
                                                 start=(kc == 0), stop=(kc == 7))
                                    for kc in range(7):
                                        t.matmul(pf_u[:, pp * 512:pp * 512 + 512],
                                                 f_gup[:, kc * 2048 + 1024 + qi * 128:
                                                       kc * 2048 + 1024 + qi * 128 + 128],
                                                 f_xs[:, kc * 1024 + tch * 512:
                                                      kc * 1024 + tch * 512 + 512],
                                                 start=(kc == 0), stop=False)
                                    t.matmul(pf_u[:, pp * 512:pp * 512 + 512],
                                             f_gup[:, 7 * 2048 + 1024 + qi * 128:
                                                   7 * 2048 + 1024 + qi * 128 + 128],
                                             f_xs[:, 7 * 1024 + tch * 512:
                                                  7 * 1024 + tch * 512 + 512],
                                             start=False, stop=True).then_inc(sT.h, 1)
                                t_gu = sT.inc()

                                @on("scalar")
                                def _(s, pp=pp, t_gu=t_gu, gm=gating_marks[1 - pp], vmark=sV.n):
                                    s.wait_ge(sT.h, t_gu)
                                    s.wait_ge(sV.h, vmark)
                                    s.activation(f_sg[:, :], pf_g[:, pp * 512:pp * 512 + 512],
                                                 AF.Silu, bias=0.0, scale=1.0).then_inc(sS.h, 1)
                                silu_marks[pp] = sS.inc()

                                @on("vector")
                                def _(v, e=e, qi=qi, pp=pp, H=H, s_sg=silu_marks[pp],
                                      t_rep=t_rep,
                                      tdn=t_dn_last.get((H, e - 1),
                                                        t_dn_last.get((H - 1, NE - 1), 0))):
                                    v.wait_ge(sS.h, s_sg)
                                    v.wait_ge(sT.h, t_rep)
                                    if tdn:
                                        v.wait_ge(sT.h, tdn)
                                    v.tensor_tensor(f_a1[:, :], f_sg[:, :],
                                                    pf_u[:, pp * 512:pp * 512 + 512], ALU.mult)
                                    v.tensor_tensor(f_act[:, qi * 512:(qi + 1) * 512],
                                                    f_a1[:, :], pf_ms[:, :], ALU.mult)
                                    v.engine_nop().then_inc(sV.h, 1)
                                gating_marks[pp] = sV.inc()
                            v_act = gating_marks[(NE - 1) % 2]
                            t_gu_last[(H, e)] = sT.n

                            for oi in range(8):
                                pd = oi % 2

                                @on("tensor")
                                def _(t, e=e, oi=oi, pd=pd, w_d=w_d, v_act=v_act,
                                      em=evac_marks[pd]):
                                    t.wait_ge(sdp.h, w_d)
                                    t.wait_ge(sV.h, max(v_act, em))
                                    for kc in range(8):
                                        t.matmul(pf_d[:, pd * 512:pd * 512 + 512],
                                                 f_dwn[:, kc * 1024 + oi * 128:
                                                       kc * 1024 + oi * 128 + 128],
                                                 f_act[:, kc * 512:(kc + 1) * 512],
                                                 start=(kc == 0), stop=(kc == 7)).then_inc(sT.h, 1)
                                t_dn = sT.inc(8)

                                @on("vector")
                                def _(v, e=e, oi=oi, pd=pd, tch=tch, t_dn=t_dn):
                                    v.wait_ge(sT.h, t_dn)
                                    dst = f_dacc[:, oi * 1024 + tch * 512:
                                                 oi * 1024 + tch * 512 + 512]
                                    if e == 0:
                                        v.tensor_copy(dst, pf_d[:, pd * 512:pd * 512 + 512])
                                    else:
                                        v.tensor_tensor(dst, dst,
                                                        pf_d[:, pd * 512:pd * 512 + 512],
                                                        ALU.add)
                                    v.engine_nop().then_inc(sV.h, 1)
                                evac_marks[pd] = sV.inc()
                            t_dn_last[(H, e)] = sT.n

                        if e + 1 < NE:
                            @on("sync")
                            def _(sy, e=e, tm=t_gu_last[(H, e)]):
                                sy.wait_ge(sT.h, tm)
                                sy.dma_start(out=f_gup[:, :],
                                             in_=gup_bf[e + 1]).then_inc(sfg.h, 16)
                            gup_marks[e + 1] = sfg.inc(16)

                            @on("gpsimd")
                            def _(g, e=e, tm=t_dn_last[(H, e)]):
                                g.wait_ge(sT.h, tm)
                                g.dma_start(out=f_dwn[:, :],
                                            in_=dwn_bf[e + 1]).then_inc(sdp.h, 16)
                            dwn_marks[e + 1] = sdp.inc(16)

                    last_evac = max(evac_marks.values())
                    if dbg and call == 0:
                        @on("sync")
                        def _(sy, H=H, last_evac=last_evac, v_rt=v_rt):
                            sy.wait_ge(sV.h, last_evac)
                            sy.dma_start(out=da2_dbg[H],
                                         in_=f_dacc[:, :]).then_inc(sY.h, 16)
                            if H == 1:
                                sy.dma_start(out=rt_dbg,
                                             in_=f_rt[0:8, :]).then_inc(sY.h, 16)
                        sY.inc(16 * (2 if H == 1 else 1))
                    for k in range(8):
                        @on("vector")
                        def _(v, k=k, H=H, last_evac=last_evac):
                            if k == 0:
                                v.wait_ge(sV.h, last_evac)
                            v.tensor_tensor(stream[:, k * T + H * 1024:k * T + H * 1024 + 1024],
                                            stream[:, k * T + H * 1024:k * T + H * 1024 + 1024],
                                            f_dacc[:, k * 1024:(k + 1) * 1024], ALU.add)
                            v.engine_nop().then_inc(sV.h, 1)
                        sV.inc()
                    v_end = sV.n
                return v_end


        def emit_attention():
            sqk = mksem("sqk")
            srot = mksem("srot")
            sks = mksem("sks")
            sCag = mksem("sCag")
            skt = [mksem("skt0"), mksem("skt1"), mksem("skt2"), mksem("skt3")]
            svt = [mksem("svt0"), mksem("svt1"), mksem("svt2"), mksem("svt3")]
            sat = mksem("sat")
            sprep = mksem("sprep")

            with contextlib.ExitStack() as _qst:
                a_q = _qst.enter_context(nc.sbuf_tensor("a_q", [128, 8 * 2048], BF16))

                with contextlib.ExitStack() as _ast:
                    a_wq = _ast.enter_context(nc.sbuf_tensor("a_wq", [128, 8 * 1024], BF16))
                    a_wk = _ast.enter_context(nc.sbuf_tensor("a_wk", [128, 8 * 1024], BF16))
                    a_wv = _ast.enter_context(nc.sbuf_tensor("a_wv", [128, 8 * 1024], BF16))
                    a_xs = _ast.enter_context(nc.sbuf_tensor("a_xs", [128, 8 * 512], BF16))
                    a_sq = _ast.enter_context(nc.sbuf_tensor("a_sq", [128, 512], F32))
                    a_cos = _ast.enter_context(nc.sbuf_tensor("a_cos", [128, 2048], F32))
                    a_sin = _ast.enter_context(nc.sbuf_tensor("a_sin", [128, 2048], F32))
                    a_qf = _ast.enter_context(nc.sbuf_tensor("a_qf", [128, 2 * 512], F32))
                    a_qr = _ast.enter_context(nc.sbuf_tensor("a_qr", [128, 2 * 512], F32))
                    a_t1 = _ast.enter_context(nc.sbuf_tensor("a_t1", [128, 512], F32))
                    a_t2 = _ast.enter_context(nc.sbuf_tensor("a_t2", [128, 512], F32))
                    a_kb = _ast.enter_context(nc.sbuf_tensor("a_kb", [128, 512], BF16))
                    a_vs = _ast.enter_context(nc.sbuf_tensor("a_vs", [128, 1040], BF16))
                    pa_st = _ast.enter_context(nc.psum_tensor("pa_st", [1, 1024], F32))
                    pa_ms = _ast.enter_context(nc.psum_tensor("pa_ms", [128, 512], F32))
                    pa_q = _ast.enter_context(nc.psum_tensor("pa_q", [128, 2 * 512], F32))
                    pa_v = _ast.enter_context(nc.psum_tensor("pa_v", [128, 2 * 512], F32))

                    @on("gpsimd")
                    def _(g):
                        ld3(g, a_wq[:, :], wq_t, 8, sqk.h)
                        ld3(g, a_wk[:, :], wk_t, 8, sqk.h)
                        ld3(g, a_wv[:, :], wv_t, 8, sqk.h)
                    w_qkv = sqk.inc(16 * 3)

                    @on("sync")
                    def _(sy):
                        sy.dma_start(out=a_cos[:, :], in_=cosr).then_inc(sW.h, 16)
                        sy.dma_start(out=a_sin[:, :], in_=sinr).then_inc(sW.h, 16)
                    w_cs = sW.inc(32)

                    v_ws = 0

                    rot_free = {}
                    rope_free = {}
                    rpc = [0]
                    for f in range(4):
                        xs_off = 0
                        scol = f * 512
                        for k in range(8):
                            @on("scalar")
                            def _(s, k=k, scol=scol, tmark=sT.n, vmark=sV.n):
                                s.wait_ge(sT.h, tmark)
                                if k == 0:
                                    s.wait_ge(sV.h, vmark)
                                s.activation(a_sq[:, :],
                                             stream[:, k * T + scol:k * T + scol + 512],
                                             AF.Square, bias=0.0, scale=1.0).then_inc(sS.h, 1)
                            sqd = sS.inc()

                            @on("tensor")
                            def _(t, k=k, scol=scol, sqd=sqd, vmark=sV.n):
                                if k == 0:
                                    t.wait_ge(sV.h, vmark)
                                t.matmul(pa_st[0:1, 0:512], onesc[:, :],
                                         stream[:, k * T + scol:k * T + scol + 512],
                                         start=(k == 0), stop=(k == 7))
                                t.wait_ge(sS.h, sqd)
                                t.matmul(pa_st[0:1, 512:1024], onesc[:, :], a_sq[:, :],
                                         start=(k == 0), stop=(k == 7)).then_inc(sT.h, 1)
                            sT.inc()
                        t_stats = sT.n

                        @on("vector")
                        def _(v, t_stats=t_stats):
                            v.wait_ge(sT.h, t_stats)
                            v.tensor_scalar(out=r_mu, in0=pa_st[0:1, 0:512], scalar1=1.0 / DM,
                                            scalar2=None, op0=ALU.mult)
                            v.tensor_scalar(out=a_sq[0:1, :], in0=pa_st[0:1, 512:1024],
                                            scalar1=1.0 / DM, scalar2=None, op0=ALU.mult)
                            v.tensor_tensor(r_inv, r_mu, r_mu, ALU.mult)
                            v.tensor_tensor(a_sq[0:1, :], a_sq[0:1, :], r_inv,
                                            ALU.subtract)
                            v.tensor_scalar(out=a_sq[0:1, :], in0=a_sq[0:1, :],
                                            scalar1=EPS_LN, scalar2=None, op0=ALU.add)
                            v.engine_nop().then_inc(sV.h, 1)
                        v_rm = sV.inc()

                        @on("scalar")
                        def _(s, v_rm=v_rm):
                            s.wait_ge(sV.h, v_rm)
                            s.activation(a_t1[0:1, :], a_sq[0:1, :], AF.Ln, bias=0.0,
                                         scale=1.0)
                            s.activation(r_inv, a_t1[0:1, :], AF.Exp, bias=0.0,
                                         scale=-0.5).then_inc(sS.h, 1)
                        s_inv = sS.inc()

                        @on("vector")
                        def _(v, s_inv=s_inv):
                            v.wait_ge(sS.h, s_inv)
                            v.tensor_tensor(r_muinv, r_mu, r_inv, ALU.mult)
                            v.engine_nop().then_inc(sV.h, 1)
                        v_muinv = sV.inc()

                        @on("tensor")
                        def _(t, v_muinv=v_muinv, smark=sS.n):
                            t.wait_ge(sV.h, v_muinv)
                            t.wait_ge(sS.h, smark)
                            t.matmul(pa_ms[:, :], onesr[:, :], r_inv, start=True,
                                     stop=True)
                            t.matmul(pa_q[:, 0:512], onesr[:, :], r_muinv, start=True,
                                     stop=True).then_inc(sT.h, 1)
                        t_invrep = sT.inc()

                        for k in range(8):
                            @on("vector")
                            def _(v, k=k, scol=scol, xs_off=xs_off, t_invrep=t_invrep,
                                  tmark=sT.n):
                                v.wait_ge(sT.h, t_invrep)
                                if k == 0:
                                    t_invrep and v.wait_ge(sT.h, tmark)
                                v.tensor_tensor(
                                    a_t1[:, :],
                                    stream[:, k * T + scol:k * T + scol + 512],
                                    pa_ms[:, :], ALU.mult)
                                v.tensor_tensor(
                                    a_xs[:, xs_off + k * 512:xs_off + (k + 1) * 512],
                                    a_t1[:, :], pa_q[:, 0:512], ALU.subtract)
                                v.engine_nop().then_inc(sV.h, 1)
                            sV.inc()
                        v_xs = sV.n

                        for which, wsb, wsoff in ((0, a_wq, 0), (1, a_wk, 1024)):
                            for m in range(8):
                                pm = m % 2

                                @on("tensor")
                                def _(t, wsb=wsb, wsoff=wsoff, m=m, pm=pm, xs_off=xs_off,
                                      v_xs=v_xs, v_ws=v_ws, w_qkv=w_qkv, smark=sS.n):
                                    t.wait_ge(sV.h, max(v_xs, v_ws))
                                    t.wait_ge(sqk.h, w_qkv)
                                    t.wait_ge(sS.h, smark)
                                    for k in range(7):
                                        t.matmul(pa_q[:, pm * 512:pm * 512 + 512],
                                                 wsb[:, k * 1024 + m * 128:
                                                     k * 1024 + m * 128 + 128],
                                                 a_xs[:, xs_off + k * 512:
                                                      xs_off + (k + 1) * 512],
                                                 start=(k == 0), stop=False)
                                    t.matmul(pa_q[:, pm * 512:pm * 512 + 512],
                                             wsb[:, 7 * 1024 + m * 128:
                                                 7 * 1024 + m * 128 + 128],
                                             a_xs[:, xs_off + 7 * 512:xs_off + 8 * 512],
                                             start=False, stop=True).then_inc(sT.h, 1)
                                t_m = sT.inc()

                                rp = rpc[0] % 2
                                rpc[0] += 1

                                @on("scalar")
                                def _(s, m=m, pm=pm, rp=rp, t_m=t_m,
                                      rmark=rot_free.get(rp, 0)):
                                    s.wait_ge(sT.h, t_m)
                                    if rmark:
                                        s.wait_ge(srot.h, rmark)
                                    s.activation(a_qf[:, rp * 512:rp * 512 + 512],
                                                 pa_q[:, pm * 512:pm * 512 + 512],
                                                 AF.Copy, bias=0.0,
                                                 scale=1.0).then_inc(sS.h, 1)
                                s_ev = sS.inc()

                                @on("sync")
                                def _(sy, s_ev=s_ev, rp=rp, vmark=rope_free.get(rp, 0)):
                                    sy.wait_ge(sS.h, s_ev)
                                    if vmark:
                                        sy.wait_ge(sV.h, vmark)
                                    for hh in range(2):
                                        b = hh * 64
                                        sy.dma_start(out=a_qr[b:b + 32,
                                                             rp * 512:rp * 512 + 512],
                                                     in_=a_qf[b + 32:b + 64,
                                                              rp * 512:rp * 512 + 512]
                                                     ).then_inc(srot.h, 16)
                                        sy.dma_start(out=a_qr[b + 32:b + 64,
                                                              rp * 512:rp * 512 + 512],
                                                     in_=a_qf[b:b + 32,
                                                              rp * 512:rp * 512 + 512]
                                                     ).then_inc(srot.h, 16)
                                r_rot = srot.inc(64)
                                rot_free[rp] = r_rot

                                @on("vector")
                                def _(v, which=which, m=m, rp=rp, scol=scol, r_rot=r_rot,
                                      w_cs=w_cs, smark=sks.n):
                                    v.wait_ge(srot.h, r_rot)
                                    v.wait_ge(sW.h, w_cs)
                                    v.wait_ge(sks.h, smark)
                                    v.tensor_tensor(a_t1[:, :],
                                                    a_qf[:, rp * 512:rp * 512 + 512],
                                                    a_cos[:, scol:scol + 512], ALU.mult)
                                    v.tensor_tensor(a_t2[:, :],
                                                    a_qr[:, rp * 512:rp * 512 + 512],
                                                    a_sin[:, scol:scol + 512], ALU.mult)
                                    v.tensor_tensor(a_t1[:, :], a_t1[:, :], a_t2[:, :],
                                                    ALU.add)
                                    if which == 0:
                                        v.tensor_scalar(
                                            out=a_q[:, m * 2048 + scol:m * 2048 + scol + 512],
                                            in0=a_t1[:, :], scalar1=0.125, scalar2=None,
                                            op0=ALU.mult)
                                    else:
                                        v.tensor_copy(a_kb[:, :], a_t1[:, :])
                                    v.engine_nop().then_inc(sV.h, 1)
                                v_rope = sV.inc()
                                rope_free[rp] = v_rope

                                if which == 1:
                                    @on("vector")
                                    def _(v, m=m, scol=scol, v_rope=v_rope):
                                        v.tensor_tensor(
                                            a_t2[:, :],
                                            a_q[:, m * 2048 + scol:m * 2048 + scol + 512],
                                            a_kb[:, :], ALU.mult)
                                        v.engine_nop().then_inc(sV.h, 1)
                                    v_dg = sV.inc()
                                else:
                                    @on("scalar")
                                    def _(s, m=m, scol=scol, v_rope=v_rope):
                                        s.wait_ge(sV.h, v_rope)
                                        s.activation(
                                            a_t2[:, :],
                                            a_q[:, m * 2048 + scol:m * 2048 + scol + 512],
                                            AF.Square, bias=0.0, scale=1.0).then_inc(sS.h, 1)
                                    s_q2 = sS.inc()
                                    v_dg = None

                                @on("tensor")
                                def _(t, m=m, which=which, v_dg=v_dg,
                                      smark=sS.n, vmark=sV.n):
                                    if which == 1:
                                        t.wait_ge(sV.h, v_dg)
                                    else:
                                        t.wait_ge(sS.h, smark)
                                    t.wait_ge(sV.h, vmark)
                                    t.matmul(pa_st[0:1, 512:1024], onesc[0:64, :],
                                             a_t2[0:64, :], start=True, stop=True)
                                    t.matmul(pa_st[0:1, 0:512], onesc[64:128, :],
                                             a_t2[64:128, :], start=True,
                                             stop=True).then_inc(sT.h, 1)
                                t_rr = sT.inc()

                                @on("vector")
                                def _(v, t_rr=t_rr, ymark=sks.n):
                                    v.wait_ge(sT.h, t_rr)
                                    v.wait_ge(sks.h, ymark)
                                    v.tensor_copy(rows[0:1, 0:512], pa_st[0:1, 512:1024])
                                    v.tensor_copy(rows[0:1, 512:1024], pa_st[0:1, 0:512])
                                    v.engine_nop().then_inc(sV.h, 1)
                                v_rr = sV.inc()

                                @on("sync")
                                def _(sy, which=which, m=m, scol=scol, v_rr=v_rr):
                                    sy.wait_ge(sV.h, v_rr)
                                    dst = diag_hbm if which == 1 else qn2_hbm
                                    sy.dma_start(out=dst[2 * m:2 * m + 1, scol:scol + 512],
                                                 in_=rows[0:1, 0:512]).then_inc(sks.h, 16)
                                    sy.dma_start(out=dst[2 * m + 1:2 * m + 2, scol:scol + 512],
                                                 in_=rows[0:1, 512:1024]).then_inc(sks.h, 16)
                                    if which == 1:
                                        off = (m // 2) * 256 + (m % 2) * 128
                                        sy.dma_start(
                                            out=kag_in
                                            .rearrange("p (t x) -> p t x", x=1024)
                                            [:, (scol // 128):(scol // 128) + 4,
                                             off:off + 128],
                                            in_=a_kb[:, :]
                                            .rearrange("p (t c) -> p t c", c=128)
                                            ).then_inc(sks.h, 16)
                                sks.inc(48 if which == 1 else 32)

                        for tt_ in range(4):
                            for ng in range(2):
                                @on("tensor")
                                def _(t, tt_=tt_, ng=ng, xs_off=xs_off, v_xs=v_xs,
                                      v_ws=v_ws, w_qkv=w_qkv, vmark=sV.n):
                                    t.wait_ge(sV.h, max(v_xs, v_ws, vmark))
                                    t.wait_ge(sqk.h, w_qkv)
                                    for k in range(7):
                                        t.matmul(pa_v[:, ng * 512:ng * 512 + 512],
                                                 a_xs[:, xs_off + k * 512 + tt_ * 128:
                                                      xs_off + k * 512 + tt_ * 128 + 128],
                                                 a_wv[:, k * 1024 + ng * 512:
                                                      k * 1024 + ng * 512 + 512],
                                                 start=(k == 0), stop=False)
                                    t.matmul(pa_v[:, ng * 512:ng * 512 + 512],
                                             a_xs[:, xs_off + 7 * 512 + tt_ * 128:
                                                  xs_off + 7 * 512 + tt_ * 128 + 128],
                                             a_wv[:, 7 * 1024 + ng * 512:
                                                  7 * 1024 + ng * 512 + 512],
                                             start=False, stop=True).then_inc(sT.h, 1)
                                sT.inc()
                            t_v = sT.n

                            @on("vector")
                            def _(v, tt_=tt_, t_v=t_v, smark=sks.n):
                                v.wait_ge(sT.h, t_v)
                                v.wait_ge(sks.h, smark)
                                v.memset(a_vs[:, :], 1.0)
                                for ng in range(2):
                                    v.tensor_copy(
                                        a_vs[:, ng * 520:(ng + 1) * 520]
                                        .rearrange("p (h c) -> p h c", h=8)[:, :, 0:64],
                                        pa_v[:, ng * 512:ng * 512 + 512]
                                        .rearrange("p (h c) -> p h c", h=8))
                                v.engine_nop().then_inc(sV.h, 1)
                            v_vs = sV.inc()

                            @on("sync")
                            def _(sy, f=f, tt_=tt_, v_vs=v_vs):
                                sy.wait_ge(sV.h, v_vs)
                                sy.dma_start(out=vag_in[(f * 4 + tt_) * 128:
                                                        (f * 4 + tt_ + 1) * 128, :],
                                             in_=a_vs[:, :]).then_inc(sks.h, 16)
                            sks.inc(16)

                    y_kv = sks.n

                    @on("gpsimd")
                    def _(g, y_kv=y_kv):
                        g.wait_ge(sks.h, y_kv)
                        g.collective_compute(
                            "AllGather", ALU.bypass,
                            replica_groups=[[0, 1], [2, 3], [4, 5], [6, 7]],
                            ins=[kag_in], outs=[kag_out],
                        ).then_inc(sCag.h, 1)
                        g.collective_compute(
                            "AllGather", ALU.bypass,
                            replica_groups=[[0, 1], [2, 3], [4, 5], [6, 7]],
                            ins=[vag_in], outs=[vag_out],
                        ).then_inc(sCag.h, 1)
                    c_ag = sCag.inc(2)

                barrier()

                with contextlib.ExitStack() as _bst:
                    b_at = _bst.enter_context(nc.sbuf_tensor("b_at", [128, 8 * 2048], BF16))
                    b_wo = _bst.enter_context(nc.sbuf_tensor("b_wo", [128, 8 * 1024], BF16))
                    b_kt = _bst.enter_context(nc.sbuf_tensor("b_kt", [128, 4 * 256], BF16))
                    b_vt = _bst.enter_context(nc.sbuf_tensor("b_vt", [128, 4 * 260], BF16))
                    b_p = _bst.enter_context(nc.sbuf_tensor("b_p", [128, 2 * 2048], BF16))
                    b_msk = _bst.enter_context(nc.sbuf_tensor("b_msk", [128, 8 * 512], BF16))
                    b_kh = _bst.enter_context(nc.sbuf_tensor("b_kh", [64, 2 * 512], BF16))
                    b_k2 = _bst.enter_context(nc.sbuf_tensor("b_k2", [64, 512], F32))
                    b_den = _bst.enter_context(nc.sbuf_tensor("b_den", [65, 512], F32))
                    b_o65 = _bst.enter_context(nc.sbuf_tensor("b_o65", [65, 128], F32))
                    crep = _bst.enter_context(nc.sbuf_tensor("crep", [128, 2048], F32))
                    b_ps = _bst.enter_context(nc.sbuf_tensor("b_ps", [128, 2 * 1024], F32))
                    b_nl = _bst.enter_context(nc.sbuf_tensor("b_nl", [1, 1024], F32))
                    b_tmp = _bst.enter_context(nc.sbuf_tensor("b_tmp", [64, 512], BF16))
                    b_av = _bst.enter_context(nc.sbuf_tensor("b_av", [64, 512], F32))
                    b_kmx = _bst.enter_context(nc.sbuf_tensor("b_kmx", [1, 64], F32))
                    pb_s = _bst.enter_context(nc.psum_tensor("pb_s", [128, 2048], F32))
                    pb_a0 = _bst.enter_context(nc.psum_tensor("pb_a0", [65, 512], F32))
                    pb_a1 = _bst.enter_context(nc.psum_tensor("pb_a1", [65, 512], F32))
                    pb_a2 = _bst.enter_context(nc.psum_tensor("pb_a2", [65, 512], F32))
                    pb_a3 = _bst.enter_context(nc.psum_tensor("pb_a3", [65, 512], F32))
                    PB_A = [pb_a0, pb_a1, pb_a2, pb_a3]

                    @on("gpsimd")
                    def _(g):
                        ld3(g, b_wo[:, :], wo_t, 8, sqk.h)
                        g.dma_start(out=b_msk[:, :], in_=mask8).then_inc(sqk.h, 16)
                    w_wo = sqk.inc(32)

                    @on("vector")
                    def _(v):
                        v.memset(b_o65[64:65, :], 1.0)
                        v.engine_nop().then_inc(sV.h, 1)
                    sV.inc()

                    for h in range(NH):
                        for kc in range(8):
                            pk = kc % 2

                            @on("sync")
                            def _(sy, h=h, kc=kc, pk=pk, c_ag=c_ag, smark=sS.n):
                                sy.wait_ge(sCag.h, c_ag)
                                sy.wait_ge(sS.h, smark)
                                hb_, hl_ = h // 4, h % 4
                                pbase = (kc % 2) * 128 + (hl_ % 2) * 64
                                xoff = hb_ * 256 + (hl_ // 2) * 128
                                sy.dma_start(
                                    out=b_kh[:, pk * 512:pk * 512 + 512]
                                    .rearrange("p (t c) -> p t c", c=128),
                                    in_=kag_out
                                    .rearrange("q (t x) -> q t x", x=1024)
                                    [pbase:pbase + 64,
                                     (kc // 2) * 4:(kc // 2) * 4 + 4,
                                     xoff:xoff + 128]
                                    ).then_inc(sprep.h, 16)
                            k_ld = sprep.inc(16)

                            @on("scalar")
                            def _(s, h=h, kc=kc, pk=pk, k_ld=k_ld, tmark=sT.n):
                                s.wait_ge(sprep.h, k_ld)
                                s.wait_ge(sT.h, tmark)
                                s.activation(b_k2[:, :], b_kh[:, pk * 512:pk * 512 + 512],
                                             AF.Square, bias=0.0, scale=1.0).then_inc(sS.h, 1)
                            s_k2 = sS.inc()

                            @on("tensor")
                            def _(t, h=h, kc=kc, s_k2=s_k2, vmark=sV.n):
                                t.wait_ge(sS.h, s_k2)
                                if kc == 0:
                                    t.wait_ge(sV.h, vmark)
                                t.matmul(pb_s[0:1, 0:512], onesc[0:64, :], b_k2[:, :],
                                         start=(kc == 0), stop=(kc == 7)).then_inc(sT.h, 1)
                            sT.inc()
                        t_k2 = sT.n

                        @on("vector")
                        def _(v, h=h, t_k2=t_k2):
                            v.wait_ge(sT.h, t_k2)
                            v.reduce_max(b_kmx[0:1, h:h + 1], pb_s[0:1, 0:512], axis=AX)
                            v.engine_nop().then_inc(sV.h, 1)
                        sV.inc()
                    v_kmx = sV.n

                    @on("scalar")
                    def _(s, v_kmx=v_kmx):
                        s.wait_ge(sV.h, v_kmx)
                        s.activation(b_kmx[0:1, 16:32], b_kmx[0:1, 0:16], AF.Ln,
                                     bias=0.0, scale=1.0)
                        s.drain()
                        s.activation(b_kmx[0:1, 32:48], b_kmx[0:1, 16:32], AF.Exp,
                                     bias=0.0, scale=0.5).then_inc(sS.h, 1)
                    s_kmx = sS.inc()

                    for h in range(NH):
                        for gq in range(4):
                            @on("sync")
                            def _(sy, h=h, gq=gq, vmark=sV.n):
                                sy.wait_ge(sV.h, vmark)
                                sy.dma_start(out=b_nl[0:1, 0:512],
                                             in_=qn2_hbm[h:h + 1, gq * 512:gq * 512 + 512]
                                             ).then_inc(sprep.h, 16)
                                sy.dma_start(out=b_nl[0:1, 512:1024],
                                             in_=diag_hbm[h:h + 1, gq * 512:gq * 512 + 512]
                                             ).then_inc(sprep.h, 16)
                            l_nc = sprep.inc(32)

                            @on("scalar")
                            def _(s, h=h, gq=gq, l_nc=l_nc, s_kmx=s_kmx):
                                s.wait_ge(sprep.h, l_nc)
                                s.wait_ge(sS.h, s_kmx)
                                s.activation(rows[0:1, 0:512], b_nl[0:1, 0:512], AF.Ln,
                                             bias=0.0, scale=1.0)
                                s.activation(rows[0:1, 512:1024], rows[0:1, 0:512], AF.Exp,
                                             bias=0.0, scale=0.5).then_inc(sS.h, 1)
                            s_qn = sS.inc()

                            @on("vector")
                            def _(v, h=h, gq=gq, s_qn=s_qn, ymark=sks.n):
                                v.wait_ge(sS.h, s_qn)
                                v.wait_ge(sks.h, ymark)
                                v.tensor_scalar(out=rows[0:1, 0:512], in0=rows[0:1, 512:1024],
                                                scalar1=b_kmx[0:1, 32 + h:33 + h],
                                                scalar2=-CAP, op0=ALU.mult, op1=ALU.add)
                                v.tensor_tensor(rows[0:1, 0:512], rows[0:1, 0:512],
                                                b_nl[0:1, 512:1024], ALU.max)
                                v.tensor_scalar(out=rows[0:1, 512:1024], in0=rows[0:1, 0:512],
                                                scalar1=-1.0, scalar2=None, op0=ALU.mult)
                                v.engine_nop().then_inc(sV.h, 1)
                            v_nc = sV.inc()

                            @on("sync")
                            def _(sy, h=h, gq=gq, v_nc=v_nc):
                                sy.wait_ge(sV.h, v_nc)
                                sy.dma_start(out=negc_hbm[h:h + 1, gq * 512:gq * 512 + 512],
                                             in_=rows[0:1, 512:1024]).then_inc(sks.h, 16)
                            sks.inc(16)
                    y_nc = sks.n

                    kt_free = {}
                    vt_free = {}
                    bp_free = {}
                    nc_free = {}
                    sub_free = {}
                    sub_marks = {}
                    bps_free = {}
                    pbs_free = [0]
                    scrp = mksem("scrp")
                    for gq in range(4):
                        njj = 8 * gq + 8
                        for hb in range(4):
                            ib = gq * 4 + hb
                            nb = ib % 2
                            @on("gpsimd")
                            def _(g, gq=gq, hb=hb, y_nc=y_nc,
                                  nf=nc_free.get(0, 0)):
                                g.wait_ge(sks.h, y_nc)
                                if nf:
                                    g.wait_ge(sV.h, nf)
                                for hl in range(4):
                                    h = hb * 4 + hl
                                    g.dma_start(out=crep[:, hl * 512:(hl + 1) * 512],
                                                in_=negc_hbm[h:h + 1,
                                                             gq * 512:gq * 512 + 512]
                                                .broadcast_to([128, 512])
                                                ).then_inc(scrp.h, 16)
                            l_ncb = scrp.inc(64)

                            av_marks = {}
                            exp_marks = {}
                            for jj in range(njj):
                                pj = jj % 4
                                bp = jj % 2

                                @on("sync")
                                def _(sy, gq=gq, hb=hb, jj=jj, pj=pj, c_ag=c_ag,
                                      kf=kt_free.get(pj, 0)):
                                    sy.wait_ge(sCag.h, c_ag)
                                    if kf:
                                        sy.wait_ge(sT.h, kf)
                                    sy.dma_start(
                                        out=b_kt[:, pj * 256:pj * 256 + 256],
                                        in_=kag_out[(jj % 2) * 128:(jj % 2) * 128 + 128,
                                                    ((jj // 2) * 4 + hb) * 256:
                                                    ((jj // 2) * 4 + hb) * 256 + 256]
                                        ).then_inc(skt[pj].h, 16)
                                l_kt = skt[pj].inc(16)

                                @on("gpsimd")
                                def _(g, gq=gq, hb=hb, jj=jj, pj=pj, c_ag=c_ag,
                                      vf=vt_free.get(pj, 0)):
                                    g.wait_ge(sCag.h, c_ag)
                                    if vf:
                                        g.wait_ge(sT.h, vf)
                                    g.dma_start(
                                        out=b_vt[:, pj * 260:pj * 260 + 260],
                                        in_=vag_out[(jj % 2) * 2048 + (jj // 2) * 128:
                                                    (jj % 2) * 2048 + (jj // 2) * 128 + 128,
                                                    hb * 260:(hb + 1) * 260]
                                        ).then_inc(svt[pj].h, 16)
                                l_vt = svt[pj].inc(16)

                                for hl in range(4):
                                    @on("tensor")
                                    def _(t, gq=gq, hb=hb, jj=jj, pj=pj, hl=hl,
                                          l_kt=l_kt,
                                          sf0=sub_free.get(0, 0),
                                          sf1=sub_free.get(1, 0),
                                          pbs0=pbs_free[0]):
                                        h = hb * 4 + hl
                                        t.wait_ge(skt[pj].h, l_kt)
                                        if hl == 0 and sf0:
                                            t.wait_ge(sV.h, sf0)
                                        if hl == 2 and sf1:
                                            t.wait_ge(sV.h, sf1)
                                        if hl == 0 and jj == 0 and pbs0:
                                            t.wait_ge(sV.h, pbs0)
                                        bq = (hl % 2) * 64
                                        t.matmul(pb_s[:, hl * 512:hl * 512 + 512],
                                                 b_kt[bq:bq + 64,
                                                      pj * 256 + (hl // 2) * 128:
                                                      pj * 256 + (hl // 2) * 128 + 128],
                                                 a_q[bq:bq + 64,
                                                     (h // 2) * 2048 + gq * 512:
                                                     (h // 2) * 2048 + gq * 512 + 512],
                                                 start=True, stop=True).then_inc(sT.h, 1)
                                    sT.inc()
                                    if hl == 1:
                                        t_s2 = sT.n
                                t_s4 = sT.n
                                kt_free[pj] = t_s4

                                for half in range(2):
                                    @on("vector")
                                    def _(v, jj=jj, bp=bp, nb=nb, half=half,
                                          ts=(t_s2 if half == 0 else t_s4),
                                          l_ncb=l_ncb,
                                          bpsf=bps_free.get(half, 0)):
                                        v.wait_ge(sT.h, ts)
                                        v.wait_ge(scrp.h, l_ncb)
                                        if bpsf:
                                            v.wait_ge(sS.h, bpsf)
                                        v.tensor_tensor(
                                            b_ps[:, half * 1024:half * 1024 + 1024],
                                            pb_s[:, half * 1024:half * 1024 + 1024],
                                            crep[:, half * 1024:half * 1024 + 1024],
                                            ALU.add)
                                        v.engine_nop().then_inc(sV.h, 1)
                                    sub_marks[(jj, half)] = sV.inc()
                                    sub_free[half] = sub_marks[(jj, half)]
                                nc_free[0] = sub_marks[(jj, 1)]

                                for half in range(2):
                                    @on("scalar")
                                    def _(s, jj=jj, bp=bp, half=half,
                                          sv=sub_marks[(jj, half)],
                                          bpf=bp_free.get(bp, 0)):
                                        s.wait_ge(sV.h, sv)
                                        if half == 0 and bpf:
                                            s.wait_ge(sT.h, bpf)
                                        s.activation(b_p[:, bp * 2048 + half * 1024:
                                                         bp * 2048 + half * 1024 + 1024],
                                                     b_ps[:, half * 1024:half * 1024 + 1024],
                                                     AF.Exp, bias=0.0,
                                                     scale=1.0).then_inc(sS.h, 1)
                                    exp_marks[(jj, half)] = sS.inc()
                                    bps_free[half] = exp_marks[(jj, half)]

                                r = jj - 8 * gq
                                msk_marks = [None, None]
                                if r >= 0:
                                    for half in range(2):
                                        @on("vector")
                                        def _(v, r=r, bp=bp, half=half, w_wo=w_wo,
                                              se=exp_marks[(jj, half)]):
                                            v.wait_ge(sS.h, se)
                                            v.wait_ge(sqk.h, w_wo)
                                            v.tensor_tensor(
                                                b_p[:, bp * 2048 + half * 1024:
                                                    bp * 2048 + (half + 1) * 1024]
                                                .rearrange("p (q c) -> p q c", q=2),
                                                b_p[:, bp * 2048 + half * 1024:
                                                    bp * 2048 + (half + 1) * 1024]
                                                .rearrange("p (q c) -> p q c", q=2),
                                                b_msk[0:128, r * 512:(r + 1) * 512]
                                                .unsqueeze(1)
                                                .broadcast_to([128, 2, 512]), ALU.mult)
                                            v.engine_nop().then_inc(sV.h, 1)
                                        msk_marks[half] = sV.inc()

                                for hl in range(4):
                                    @on("tensor")
                                    def _(t, gq=gq, hb=hb, jj=jj, pj=pj, bp=bp, hl=hl,
                                          njj=njj, l_vt=l_vt,
                                          se=exp_marks[(jj, hl // 2)],
                                          vm=msk_marks[hl // 2], vmark=sV.n):
                                        t.wait_ge(svt[pj].h, l_vt)
                                        if vm is not None:
                                            t.wait_ge(sV.h, vm)
                                        else:
                                            t.wait_ge(sS.h, se)
                                        if jj == 0:
                                            t.wait_ge(sV.h, vmark)
                                        t.matmul(PB_A[hl][:, :],
                                                 b_vt[:, pj * 260 + hl * 65:
                                                      pj * 260 + hl * 65 + 65],
                                                 b_p[:, bp * 2048 + hl * 512:
                                                     bp * 2048 + (hl + 1) * 512],
                                                 start=(jj == 0),
                                                 stop=(jj == njj - 1)).then_inc(sT.h, 1)
                                    sT.inc()
                                av_marks[jj] = sT.n
                                vt_free[pj] = av_marks[jj]
                                bp_free[bp] = av_marks[jj]

                            for hl in range(4):
                                @on("vector")
                                def _(v, gq=gq, hb=hb, hl=hl, av=av_marks[njj - 1],
                                      ymark=sat.n):
                                    v.wait_ge(sT.h, av)
                                    v.wait_ge(sat.h, ymark)
                                    v.tensor_copy(b_den[64:65, :], PB_A[hl][64:65, :])
                                    v.tensor_scalar(out=b_den[64:65, :], in0=b_den[64:65, :],
                                                    scalar1=1e-38, scalar2=None, op0=ALU.max)
                                    v.reciprocal(b_den[64:65, :], b_den[64:65, :])
                                    v.engine_nop().then_inc(sV.h, 1)
                                v_den = sV.inc()

                                @on("tensor")
                                def _(t, hl=hl, v_den=v_den, vmark=sV.n):
                                    t.wait_ge(sV.h, v_den)
                                    t.matmul(pb_s[:, 0:512], b_o65[64:65, :],
                                             b_den[64:65, :], start=True,
                                             stop=True).then_inc(sT.h, 1)
                                t_dr = sT.inc()

                                @on("vector")
                                def _(v, hl=hl, t_dr=t_dr):
                                    v.wait_ge(sT.h, t_dr)
                                    v.tensor_copy(b_av[:, :], PB_A[hl][0:64, :])
                                    v.tensor_tensor(b_tmp[:, :], b_av[:, :],
                                                    pb_s[0:64, 0:512], ALU.mult)
                                    v.engine_nop().then_inc(sV.h, 1)
                                v_tmp = sV.inc()

                                @on("sync")
                                def _(sy, gq=gq, hb=hb, hl=hl, v_tmp=v_tmp):
                                    sy.wait_ge(sV.h, v_tmp)
                                    h = hb * 4 + hl
                                    sy.dma_start(
                                        out=b_at[(h % 2) * 64:(h % 2) * 64 + 64,
                                                 (h // 2) * 2048 + gq * 512:
                                                 (h // 2) * 2048 + gq * 512 + 512],
                                        in_=b_tmp[:, :]).then_inc(sat.h, 16)
                                sat.inc(16)
                            pbs_free[0] = sV.n
                    y_at = sat.n

                    for m in range(8):
                        for tc in range(4):
                            pq = (m * 4 + tc) % 2

                            @on("tensor")
                            def _(t, m=m, tc=tc, pq=pq, y_at=y_at, w_wo=w_wo, vmark=sV.n):
                                t.wait_ge(sat.h, y_at)
                                t.wait_ge(sqk.h, w_wo)
                                t.wait_ge(sV.h, vmark)
                                for k in range(7):
                                    t.matmul(pb_s[:, pq * 512 + 1024:pq * 512 + 1536],
                                             b_wo[:, k * 1024 + m * 128:
                                                  k * 1024 + m * 128 + 128],
                                             b_at[:, k * 2048 + tc * 512:
                                                  k * 2048 + (tc + 1) * 512],
                                             start=(k == 0), stop=False)
                                t.matmul(pb_s[:, pq * 512 + 1024:pq * 512 + 1536],
                                         b_wo[:, 7 * 1024 + m * 128:7 * 1024 + m * 128 + 128],
                                         b_at[:, 7 * 2048 + tc * 512:7 * 2048 + (tc + 1) * 512],
                                         start=False, stop=True).then_inc(sT.h, 1)
                            t_wo = sT.inc()

                            @on("vector")
                            def _(v, m=m, tc=tc, pq=pq, t_wo=t_wo):
                                v.wait_ge(sT.h, t_wo)
                                v.tensor_tensor(
                                    stream[:, m * T + tc * 512:m * T + (tc + 1) * 512],
                                    stream[:, m * T + tc * 512:m * T + (tc + 1) * 512],
                                    pb_s[:, pq * 512 + 1024:pq * 512 + 1536], ALU.add)
                                v.engine_nop().then_inc(sV.h, 1)
                            sV.inc()
            return sV.n
        if stage != "mamba":
            v_ffn1 = emit_ffn(0)
            if stage == "ffn1":
                @on("sync")
                def _(sy, v=v_ffn1, tgt=sY.n + 16 * 8):
                    sy.wait_ge(sV.h, v)
                    for cc in range(8):
                        sy.dma_start(out=sdump[cc],
                                     in_=stream[:, cc * T:(cc + 1) * T]).then_inc(sY.h, 16)
                    sy.wait_ge(sY.h, tgt)
                sY.inc(16 * 8)
            barrier()

        if stage in ("attn", "full"):
            v_att = emit_attention()
            if stage == "attn":
                @on("sync")
                def _(sy, v=v_att, tgt=sY.n + 16 * 8):
                    sy.wait_ge(sV.h, v)
                    for cc in range(8):
                        sy.dma_start(out=sdump[cc],
                                     in_=stream[:, cc * T:(cc + 1) * T]).then_inc(sY.h, 16)
                    sy.wait_ge(sY.h, tgt)
                sY.inc(16 * 8)
            barrier()

        if stage == "full":
            v_ffn2 = emit_ffn(1)
            barrier()
            with contextlib.ExitStack() as _ost:
                o_sb = _ost.enter_context(nc.sbuf_tensor("o_sb", [128, 1024], F32))
                po_t = _ost.enter_context(nc.psum_tensor("po_t", [128, 1024], F32))
                for ttile in range(16):
                    @on("tensor")
                    def _(t, ttile=ttile, v_ffn2=v_ffn2, smark=sS.n):
                        t.wait_ge(sV.h, v_ffn2)
                        t.wait_ge(sS.h, smark)
                        for dmc in range(8):
                            t.transpose(po_t[:, dmc * 128:(dmc + 1) * 128],
                                        stream[:, dmc * T + ttile * 128:
                                               dmc * T + ttile * 128 + 128],
                                        iden_sb[:, :]).then_inc(sT.h, 1)
                    t_tr = sT.inc(8)

                    @on("scalar")
                    def _(s, ttile=ttile, t_tr=t_tr, ymark=sY.n):
                        s.wait_ge(sT.h, t_tr)
                        s.wait_ge(sY.h, ymark)
                        s.activation(o_sb[:, :], po_t[:, :], AF.Copy, bias=0.0,
                                     scale=1.0).then_inc(sS.h, 1)
                    s_o = sS.inc()

                    @on("sync")
                    def _(sy, ttile=ttile, s_o=s_o):
                        sy.wait_ge(sS.h, s_o)
                        sy.dma_start(out=yout[ttile * 128:(ttile + 1) * 128, :],
                                     in_=o_sb[:, :]).then_inc(sY.h, 16)
                    sY.inc(16)

                @on("sync")
                def _(sy, tgt=sY.n):
                    sy.wait_ge(sY.h, tgt)


        @block.tensor
        def _(t):
            for fn in Q["tensor"]:
                fn(t)

        @block.vector
        def _(v):
            for fn in Q["vector"]:
                fn(v)

        @block.scalar
        def _(s):
            for fn in Q["scalar"]:
                fn(s)

        @block.gpsimd
        def _(g):
            for fn in Q["gpsimd"]:
                fn(g)

        @block.sync
        def _(sy):
            for fn in Q["sync"]:
                fn(sy)

    return nc


def _rope_tables():
    half = HD // 2
    inv_freq = 1.0 / 10000.0 ** (np.arange(half, dtype=np.float32) / half)
    t = np.arange(L, dtype=np.float32)
    fr = t[:, None] * inv_freq[None, :]
    return np.cos(fr), np.sin(fr)


def _core_positions(j):
    tiles = np.arange(j, 32, 2)
    return (tiles[:, None] * 128 + np.arange(128)[None, :]).reshape(-1)


def _oh8():
    m = np.zeros((8, NE * 128), np.float32)
    for e in range(NE):
        m[e, e * 128:(e + 1) * 128] = 1.0
    return m


def _mask_tiles(j):
    m = np.zeros((128, 8, 4, 128), np.float32)
    kp = np.arange(128)[:, None]
    qp = np.arange(128)[None, :]
    for r in range(8):
        for cc in range(4):
            arel = 2 * cc + j
            if r < arel:
                m[:, r, cc, :] = 1.0
            elif r == arel:
                m[:, r, cc, :] = (kp <= qp).astype(np.float32)
    return np.ascontiguousarray(m.reshape(128, 8 * 512))


def _stage_inputs(inputs):
    f32 = np.float32
    x = np.asarray(inputs["L0_attn_a_x"], f32)
    W_in = np.asarray(inputs["L0_attn_a_W_in"], f32)
    conv_w = np.asarray(inputs["L0_attn_a_conv_w"], f32)
    conv_b = np.asarray(inputs["L0_attn_a_conv_b"], f32)
    W_x = np.asarray(inputs["L0_attn_a_W_x"], f32)
    W_dt = np.asarray(inputs["L0_attn_a_W_dt"], f32)
    b_dt = np.asarray(inputs["L0_attn_a_b_dt"], f32)
    A_log = np.asarray(inputs["L0_attn_a_A_log"], f32)
    Dp = np.asarray(inputs["L0_attn_a_Dp"], f32)
    W_out = np.asarray(inputs["L0_attn_a_W_out"], f32)
    w_qkv = np.asarray(inputs["L0_attn_b_w_qkv"], f32)
    w_out_attn = np.asarray(inputs["L0_attn_b_w_out"], f32)
    gate_w = np.asarray(inputs["L0_ffn_gate_w"], f32)
    gate_up = np.asarray(inputs["L0_ffn_gate_up_proj"], f32)
    down = np.asarray(inputs["L0_ffn_down_proj"], f32)

    cos_full, sin_full = _rope_tables()
    C = np.ascontiguousarray

    in_maps = []
    for c in range(8):
        b, j = c // 2, c % 2
        ch = slice(j * CH, (j + 1) * CH)
        pos = _core_positions(j)
        cosj = cos_full[pos]
        sinj = sin_full[pos]
        cosr_ = np.tile(cosj.T, (4, 1)).astype(f32)
        sinr_ = np.tile(sinj.T, (4, 1)).astype(f32)
        sinr_[0:32] *= -1.0
        sinr_[64:96] *= -1.0
        d = {
            "xbt": C(x[b].T),
            "xres": C(x[b].T[:, pos]),
            "win_t": C(np.concatenate(
                [W_in[ch], W_in[2048 + j * CH:2048 + (j + 1) * CH]], axis=0).T),
            "convw": C(conv_w[ch, 0, :]),
            "convb": C(conv_b[ch].reshape(CH, 1)),
            "wx_t": C(W_x[:, ch].T),
            "wdt_t": C(W_dt[ch].T),
            "bdt": C(b_dt[ch].reshape(CH, 1)),
            "alog": C(A_log[ch]),
            "dpv": C(Dp[ch].reshape(CH, 1)),
            "wout_t": C(W_out[:, ch].T),
            "wq_t": C(w_qkv[0:DM].T),
            "wk_t": C(w_qkv[DM:2 * DM].T),
            "wv_t": C(w_qkv[2 * DM:3 * DM].T),
            "wo_t": C(w_out_attn.T),
            "gw_t": C(gate_w.T),
            "gup": C(gate_up),
            "dwn": C(down),
            "iden": np.eye(128, dtype=f32),
            "cosr": C(cosr_),
            "sinr": C(sinr_),
            "mask8": _mask_tiles(j),
            "oh8": _oh8(),
        }
        in_maps.append(d)
    return in_maps


_NC_CACHE = {}


def kernel(**inputs):
    stage = _STAGE
    if stage not in _NC_CACHE:
        _NC_CACHE[stage] = build_program(stage)
    nc = _NC_CACHE[stage]
    in_maps = _stage_inputs(inputs)
    res = run_bass_kernel_spmd(nc, in_maps, core_ids=list(range(8)))
    if stage != "full":
        return res
    out = np.zeros((4, L, DM), np.float32)
    for c in range(8):
        b, j = c // 2, c % 2
        pos = _core_positions(j)
        out[b, pos, :] = res.results[c]["yout"]
    return out


# version 16
# speedup vs baseline: 1.0107x; 1.0107x over previous
import contextlib
import numpy as np
import ml_dtypes
import concourse.bass as bass
import concourse.mybir as mybir
from concourse.bass_utils import run_bass_kernel_spmd

F32 = mybir.dt.float32
BF16 = mybir.dt.bfloat16
AF = mybir.ActivationFunctionType
ALU = mybir.AluOpType
AX = mybir.AxisListType.X

DM = 1024
CH = 1024
NST = 16
L = 4096
T = 2048
NH = 16
NE = 8
HD = 64
EPS_LN = 1e-5
EPS_R = 1e-9
CAP = 78.0

_STAGE = "full"
_NSC = 4


class Sem:
    __slots__ = ("h", "n")

    def __init__(self, h):
        self.h = h
        self.n = 0

    def inc(self, k=1):
        self.n += k
        return self.n


def build_program(stage):
    nc = bass.Bass("TRN2", target_bir_lowering=False, debug=False,
                   detect_race_conditions=False)
    dbg = stage != "full"

    def din(name, shape, dt=F32):
        return nc.dram_tensor(name, shape, dt, kind="ExternalInput").ap()

    xbt = din("xbt", [DM, L])
    xres = din("xres", [DM, T])
    win_t = din("win_t", [DM, 2 * CH])
    convw = din("convw", [CH, 4])
    convb = din("convb", [CH, 1])
    wx_t = din("wx_t", [CH, 96])
    wdt_t = din("wdt_t", [64, CH])
    bdt = din("bdt", [CH, 1])
    alog = din("alog", [CH, NST])
    dpv = din("dpv", [CH, 1])
    wout_t = din("wout_t", [CH, DM])
    wq_t = din("wq_t", [DM, DM])
    wk_t = din("wk_t", [DM, DM])
    wv_t = din("wv_t", [DM, DM])
    wo_t = din("wo_t", [DM, DM])
    gw_t = din("gw_t", [DM, NE])
    gup = din("gup", [NE, DM, 2048])
    dwn = din("dwn", [NE, 1024, DM])
    iden = din("iden", [128, 128])
    cosr = din("cosr", [128, T])
    sinr = din("sinr", [128, T])
    mask8 = din("mask8", [128, 8 * 512])
    oh8 = din("oh8", [8, NE * 128])

    yout = nc.dram_tensor("yout", [T, DM], F32, kind="ExternalOutput").ap()
    sdump = (nc.dram_tensor("sdump", [8, 128, T], F32, kind="ExternalOutput").ap()
             if dbg else None)

    dk = dict(kind="ExternalOutput") if dbg else {}
    u_hbm = nc.dram_tensor("u_hbm", [CH, L], F32, **dk).ap()
    rs_hbm = nc.dram_tensor("rs_hbm", [CH, L], F32, **dk).ap()
    dl_hbm = nc.dram_tensor("dl_hbm", [CH, L], F32, **dk).ap()
    xdbl_in = nc.dram_tensor("xdbl_in", [4, 96, 1024], F32).ap()
    xdbl_out = nc.dram_tensor("xdbl_out", [4, 96, 1024], F32).ap()
    rsin = nc.dram_tensor("rsin", [8, 2, DM, 256], F32).ap()
    rsout = nc.dram_tensor("rsout", [8, DM, 256], F32).ap()
    gup_bf = nc.dram_tensor("gup_bf", [NE, 128, 16384], BF16).ap()
    dwn_bf = nc.dram_tensor("dwn_bf", [NE, 128, 8192], BF16).ap()
    wsg_hbm = nc.dram_tensor("wsg_hbm", [NE, 2048], F32).ap()
    kag_in = nc.dram_tensor("kag_in", [128, 16384], BF16).ap()
    kag_out = nc.dram_tensor("kag_out", [256, 16384], BF16).ap()
    vag_in = nc.dram_tensor("vag_in", [T, NH * 65], BF16).ap()
    vag_out = nc.dram_tensor("vag_out", [2 * T, NH * 65], BF16).ap()
    negc_hbm = nc.dram_tensor("negc_hbm", [NH, T], F32).ap()
    ybf_dbg = (nc.dram_tensor("ybf_dbg", [CH, L], BF16, kind="ExternalOutput").ap()
               if dbg else None)
    xs_dbg = (nc.dram_tensor("xs_dbg", [4, 128, 4096], BF16, kind="ExternalOutput").ap()
              if dbg else None)
    rt_dbg = (nc.dram_tensor("rt_dbg", [8, 2048], F32, kind="ExternalOutput").ap()
              if dbg else None)
    da_dbg = (nc.dram_tensor("da_dbg", [4, 128, 4096], F32, kind="ExternalOutput").ap()
              if dbg else None)
    lg_dbg = (nc.dram_tensor("lg_dbg", [128, 8], F32, kind="ExternalOutput").ap()
              if dbg else None)
    xs2_dbg = (nc.dram_tensor("xs2_dbg", [2, 128, 8192], BF16, kind="ExternalOutput").ap()
               if dbg else None)
    da2_dbg = (nc.dram_tensor("da2_dbg", [2, 128, 8192], F32, kind="ExternalOutput").ap()
               if dbg else None)
    r8_dbg = (nc.dram_tensor("r8_dbg", [128, 48], F32, kind="ExternalOutput").ap()
              if dbg else None)
    row_dbg = (nc.dram_tensor("row_dbg", [4, 512], F32, kind="ExternalOutput").ap()
               if dbg else None)
    dbg_gate = [0]
    rso_dbg = (nc.dram_tensor("rso_dbg", [8, DM, 256], F32, kind="ExternalOutput").ap()
               if dbg else None)
    rsi_dbg = (nc.dram_tensor("rsi_dbg", [8, 2, DM, 256], F32, kind="ExternalOutput").ap()
               if dbg else None)
    diag_hbm = nc.dram_tensor("diag_hbm", [NH, T], F32).ap()
    qn2_hbm = nc.dram_tensor("qn2_hbm", [NH, T], F32).ap()

    Q = {"tensor": [], "vector": [], "scalar": [], "gpsimd": [], "sync": []}

    def on(eng):
        def deco(fn):
            Q[eng].append(fn)
            return fn
        return deco


    def ld3(sy_or_g, sb_ap, hbm_ap, nchunk, sem, p=128, slow=False):
        sy_or_g.dma_start(
            out=sb_ap.rearrange("p (c m) -> p c m", c=nchunk),
            in_=hbm_ap.rearrange("(c p) m -> p c m", p=p),
            allow_slow_non_contiguous=slow).then_inc(sem, 16)

    _sem_guards = []
    _all_sems = []

    def mksem(name):
        cm = nc.semaphore(name)
        _sem_guards.append(cm)
        s = Sem(cm.__enter__())
        _all_sems.append(s)
        return s

    sW = mksem("sW")
    sX = mksem("sX")
    sT = mksem("sT")
    sV = mksem("sV")
    sS = mksem("sS")
    sG = mksem("sG")
    sC = mksem("sC")
    sY = mksem("sY")
    sPH = mksem("sPH")
    sPst = mksem("sPst")
    sXs = mksem("sXs")
    sxp = [mksem("sxp0"), mksem("sxp1")]
    slp = [mksem("slp0"), mksem("slp1")]
    sbp = [mksem("sbp0"), mksem("sbp1"), mksem("sbp2"), mksem("sbp3")]
    scp = [mksem("scp0"), mksem("scp1")]
    sup = [mksem("sup0"), mksem("sup1")]
    srp = [mksem("srp0"), mksem("srp1")]
    sdst = mksem("sdst")
    sxd = mksem("sxd")
    swt = mksem("swt")
    swin = mksem("swin")
    smw = mksem("smw")
    sdl1 = mksem("sdl1")
    sCar = mksem("sCar")
    sCrs = mksem("sCrs")

    nphase = [0]
    _bar_waited = {}

    def barrier():
        nphase[0] += 1
        tgt = 5 * nphase[0]
        drains = []
        for s in _all_sems:
            if s is sPH:
                continue
            if s.n > _bar_waited.get(id(s), 0):
                drains.append((s.h, s.n))
                _bar_waited[id(s)] = s.n
        for eng in Q:
            @on(eng)
            def _(e, tgt=tgt, drains=drains, eng=eng):
                if eng == "sync":
                    for h, n in drains:
                        e.wait_ge(h, n)
                e.nop().then_inc(sPH.h, 1)
                e.wait_ge(sPH.h, tgt)
        sPH.n = tgt

    with (
        nc.sbuf_tensor("stream", [128, 8 * T], F32) as stream,
        nc.sbuf_tensor("iden_sb", [128, 128], F32) as iden_sb,
        nc.sbuf_tensor("onesc", [128, 1], F32) as onesc,
        nc.sbuf_tensor("onescb", [128, 1], BF16) as onescb,
        nc.sbuf_tensor("onesr", [1, 128], F32) as onesr,
        nc.sbuf_tensor("rows", [1, 1536], F32) as rows,
        nc.sbuf_tensor("a_sb", [128, 128], F32) as a_sb,
        nc.sbuf_tensor("ptmp", [128, 128], F32) as ptmp,
        nc.sbuf_tensor("hstate", [128, 128], F32) as hstate,
        nc.sbuf_tensor("convw_sb", [128, 32], F32) as convw_sb,
        nc.sbuf_tensor("convb_sb", [128, 8], F32) as convb_sb,
        nc.sbuf_tensor("bdt_sb", [128, 8], F32) as bdt_sb,
        nc.sbuf_tensor("dp_sb", [128, 8], F32) as dp_sb,
        nc.Block() as block,
    ):
        r_mu = rows[0:1, 0:512]
        r_inv = rows[0:1, 512:1024]
        r_muinv = rows[0:1, 1024:1536]

        @on("sync")
        def _(sy):
            sy.dma_start(out=iden_sb[:, :], in_=iden).then_inc(sW.h, 16)
            ld3(sy, convw_sb[:, :], convw, 8, sW.h)
            ld3(sy, convb_sb[:, :], convb, 8, sW.h, slow=True)
            ld3(sy, bdt_sb[:, :], bdt, 8, sW.h, slow=True)
            ld3(sy, dp_sb[:, :], dpv, 8, sW.h, slow=True)
            ld3(sy, a_sb[:, :], alog, 8, sW.h)
        w_const = sW.inc(16 * 6)

        @on("vector")
        def _(v, w=w_const):
            v.wait_ge(sW.h, w)
            if dbg:
                v.memset(stream[:, :], 0.0)
            v.memset(onesc[:, :], 1.0)
            v.memset(onescb[:, :], 1.0)
            v.memset(onesr[:, :], 1.0)
            v.memset(hstate[:, :], 0.0)
            v.engine_nop().then_inc(sV.h, 1)
        sV.inc()

        @on("scalar")
        def _(s, w=w_const):
            s.wait_ge(sW.h, w)
            s.activation(ptmp[:, :], a_sb[:, :], AF.Exp, bias=0.0,
                         scale=1.0).then_inc(sS.h, 1)
        s_expA = sS.inc()

        @on("vector")
        def _(v, w=s_expA):
            v.wait_ge(sS.h, w)
            v.tensor_scalar(out=a_sb[:, :], in0=ptmp[:, :], scalar1=-1.0, scalar2=None,
                            op0=ALU.mult)
            v.engine_nop().then_inc(sV.h, 1)
        V_INIT = sV.inc()

        @on("tensor")
        def _(t, w=V_INIT):
            t.wait_ge(sV.h, w)

        @on("scalar")
        def _(s, w=V_INIT):
            s.wait_ge(sV.h, w)

        with nc.sbuf_tensor("stg", [128, 16384], BF16) as stg:
            @on("gpsimd")
            def _(g, base=sG.n):
                k = 0
                for e in range(NE):
                    ld3(g, stg[:, :], gup[e], 8, sG.h)
                    k += 1
                    g.wait_ge(sG.h, base + 16 * k)
                    g.dma_start(out=gup_bf[e], in_=stg[:, :]).then_inc(sG.h, 16)
                    k += 1
                    g.wait_ge(sG.h, base + 16 * k)
                for e in range(NE):
                    ld3(g, stg[:, 0:8192], dwn[e], 8, sG.h)
                    k += 1
                    g.wait_ge(sG.h, base + 16 * k)
                    g.dma_start(out=dwn_bf[e], in_=stg[:, 0:8192]).then_inc(sG.h, 16)
                    k += 1
                    g.wait_ge(sG.h, base + 16 * k)
            sG.inc(16 * 32)
            barrier()

        with contextlib.ExitStack() as _mst:
            m_win = _mst.enter_context(nc.sbuf_tensor("m_win", [128, 8 * 1024], BF16))
            m_wx = _mst.enter_context(nc.sbuf_tensor("m_wx", [128, 8 * 96], BF16))
            m_wdt = _mst.enter_context(nc.sbuf_tensor("m_wdt", [64, 1024], BF16))
            m_wout = _mst.enter_context(nc.sbuf_tensor("m_wout", [128, 8 * 1024], BF16))
            m_wrow = _mst.enter_context(nc.sbuf_tensor("m_wrow", [1, 2048], F32))
            m_x = _mst.enter_context(nc.sbuf_tensor("m_x", [128, 2 * 512], F32))
            m_sq = _mst.enter_context(nc.sbuf_tensor("m_sq", [128, 512], F32))
            m_xs = _mst.enter_context(nc.sbuf_tensor("m_xs", [128, 8 * 512], BF16))
            m_uext = _mst.enter_context(nc.sbuf_tensor("m_uext", [128, 2 * 516], F32))
            m_halo = _mst.enter_context(nc.sbuf_tensor("m_halo", [128, 8 * 3], F32))
            m_ucb = _mst.enter_context(nc.sbuf_tensor("m_ucb", [128, 512], F32))
            m_ubf = _mst.enter_context(nc.sbuf_tensor("m_ubf", [128, 2 * 512], BF16))
            m_t1 = _mst.enter_context(nc.sbuf_tensor("m_t1", [128, 512], F32))
            m_t2 = _mst.enter_context(nc.sbuf_tensor("m_t2", [128, 512], F32))
            m_t3 = _mst.enter_context(nc.sbuf_tensor("m_t3", [128, 512], F32))
            m_xdbl = _mst.enter_context(nc.sbuf_tensor("m_xdbl", [96, 1024], F32))
            m_dlt = _mst.enter_context(nc.sbuf_tensor("m_dlt", [64, 1024], BF16))
            m_res = _mst.enter_context(nc.sbuf_tensor("m_res", [128, 512], F32))
            b_dl = _mst.enter_context(nc.sbuf_tensor("b_dl", [128, 2 * 512], F32))
            b_u = _mst.enter_context(nc.sbuf_tensor("b_u", [128, 2 * 512], F32))
            b_rs = _mst.enter_context(nc.sbuf_tensor("b_rs", [128, 2 * 512], F32))
            b_du = _mst.enter_context(nc.sbuf_tensor("b_du", [128, 512], F32))
            b_dA = _mst.enter_context(nc.sbuf_tensor("b_dA", [128, 2 * 512], F32))
            b_dBu = _mst.enter_context(nc.sbuf_tensor("b_dBu", [128, 512], F32))
            b_h = _mst.enter_context(nc.sbuf_tensor("b_h", [128, 512], F32))
            b_hC = _mst.enter_context(nc.sbuf_tensor("b_hC", [128, 512], F32))
            b_y = _mst.enter_context(nc.sbuf_tensor("b_y", [128, 512], F32))
            b_bm = _mst.enter_context(nc.sbuf_tensor("b_bm", [128, 4 * 512], BF16))
            b_cm = _mst.enter_context(nc.sbuf_tensor("b_cm", [128, 4 * 512], BF16))
            b_ybf = _mst.enter_context(nc.sbuf_tensor("b_ybf", [128, 2 * 8 * 512], BF16))
            c_wo = _mst.enter_context(nc.sbuf_tensor("c_wo", [128, 512], F32))
            c_rs = _mst.enter_context(nc.sbuf_tensor("c_rs", [128, 256], F32))
            c_xr = _mst.enter_context(nc.sbuf_tensor("c_xr", [128, 256], F32))
            p_st = _mst.enter_context(nc.psum_tensor("p_st", [1, 1024], F32))
            p_inv = _mst.enter_context(nc.psum_tensor("p_inv", [128, 512], F32))
            p_mm = _mst.enter_context(nc.psum_tensor("p_mm", [128, 2 * 512], F32))
            p_wx = _mst.enter_context(nc.psum_tensor("p_wx", [96, 512], F32))
            p_dl = _mst.enter_context(nc.psum_tensor("p_dl", [128, 2 * 512], F32))
            _dummy_ctx = True

            @on("gpsimd")
            def _(g):
                ld3(g, m_wx[:, :], wx_t, 8, smw.h)
                g.dma_start(out=m_wdt[:, :], in_=wdt_t).then_inc(smw.h, 16)
                ld3(g, m_wout[:, :], wout_t, 8, smw.h)
            w_mw = smw.inc(16 * 3)

            def load_win(half):
                t_prev = sT.n

                @on("gpsimd")
                def _(g, half=half, t_prev=t_prev):
                    g.wait_ge(sT.h, t_prev)
                    ld3(g, m_win[:, :], win_t[:, half * CH:(half + 1) * CH], 8, swin.h)
                return swin.inc(16)

            def emit_ws(half, wtag):
                marks = []
                for grp in range(2):
                    @on("tensor")
                    def _(t, wtag=wtag, grp=grp, vprev=(marks[-1] if marks else 0)):
                        t.wait_ge(swin.h, wtag)
                        if vprev:
                            t.wait_ge(sV.h, vprev)
                        for k in range(7):
                            t.matmul(p_dl[0:1, 0:512], onescb[:, :],
                                     m_win[:, k * 1024 + grp * 512:k * 1024 + grp * 512 + 512],
                                     start=(k == 0), stop=False)
                        t.matmul(p_dl[0:1, 0:512], onescb[:, :],
                                 m_win[:, 7 * 1024 + grp * 512:7 * 1024 + grp * 512 + 512],
                                 start=False, stop=True).then_inc(sT.h, 1)
                    tmk = sT.inc()

                    @on("vector")
                    def _(v, half=half, grp=grp, tmk=tmk):
                        v.wait_ge(sT.h, tmk)
                        v.tensor_scalar(
                            out=m_wrow[0:1,
                                       half * 1024 + grp * 512: half * 1024 + grp * 512 + 512],
                            in0=p_dl[0:1, 0:512], scalar1=-1.0, scalar2=None, op0=ALU.mult)
                        v.engine_nop().then_inc(sV.h, 1)
                    marks.append(sV.inc())
                return marks[-1]

            ws_done = [0, 0]
            ci_cnt = [0]
            prev_xs = [0]
            pmm_free = {0: 0, 1: 0}

            def emit_stats_xs(c):
                ci = ci_cnt[0]
                cols = slice(c * 512, (c + 1) * 512)
                for k in range(8):
                    @on("sync" if k % 2 == 0 else "scalar")
                    def _(q, k=k, c=c, cols=cols, tmark=sT.n, smark=sS.n,
                          vmark=prev_xs[0]):
                        if k >= 2:
                            q.wait_ge(sT.h, tmark)
                            q.wait_ge(sS.h, smark)
                        elif vmark:
                            q.wait_ge(sV.h, vmark)
                        q.dma_start(out=m_x[:, (k % 2) * 512:(k % 2) * 512 + 512],
                                    in_=xbt[k * 128:(k + 1) * 128, cols]
                                    ).then_inc(sxp[k % 2].h, 16)
                    xl = sxp[k % 2].inc(16)

                    @on("scalar")
                    def _(s, k=k, xl=xl, tmark=sT.n):
                        s.wait_ge(sxp[k % 2].h, xl)
                        s.wait_ge(sT.h, tmark)
                        s.activation(m_sq[:, :], m_x[:, (k % 2) * 512:(k % 2) * 512 + 512],
                                     AF.Square, bias=0.0, scale=1.0).then_inc(sS.h, 1)
                    sqd = sS.inc()

                    @on("tensor")
                    def _(t, k=k, xl=xl, sqd=sqd, ci=ci):
                        t.wait_ge(sxp[k % 2].h, xl)
                        if k == 0:
                            t.wait_ge(sPst.h, ci)
                        t.matmul(p_st[0:1, 0:512], onesc[:, :],
                                 m_x[:, (k % 2) * 512:(k % 2) * 512 + 512],
                                 start=(k == 0), stop=(k == 7))
                        t.wait_ge(sS.h, sqd)
                        t.matmul(p_st[0:1, 512:1024], onesc[:, :], m_sq[:, :],
                                 start=(k == 0), stop=(k == 7)).then_inc(sT.h, 1)
                    sT.inc()
                t_stats = sT.n

                @on("vector")
                def _(v, t_stats=t_stats):
                    v.wait_ge(sT.h, t_stats)
                    v.tensor_scalar(out=r_mu, in0=p_st[0:1, 0:512], scalar1=1.0 / DM,
                                    scalar2=None, op0=ALU.mult)
                    v.tensor_scalar(out=m_sq[0:1, :], in0=p_st[0:1, 512:1024],
                                    scalar1=1.0 / DM, scalar2=None, op0=ALU.mult)
                    v.tensor_tensor(m_t1[0:1, :], r_mu, r_mu, ALU.mult)
                    v.tensor_tensor(m_sq[0:1, :], m_sq[0:1, :], m_t1[0:1, :], ALU.subtract)
                    v.tensor_scalar(out=m_sq[0:1, :], in0=m_sq[0:1, :], scalar1=EPS_LN,
                                    scalar2=None, op0=ALU.add)
                    v.engine_nop().then_inc(sV.h, 1)
                    v.engine_nop().then_inc(sPst.h, 1)
                v_rm = sV.inc()
                sPst.inc()

                @on("scalar")
                def _(s, v_rm=v_rm):
                    s.wait_ge(sV.h, v_rm)
                    s.activation(m_t1[0:1, :], m_sq[0:1, :], AF.Ln, bias=0.0, scale=1.0)
                    s.activation(r_inv, m_t1[0:1, :], AF.Exp, bias=0.0,
                                 scale=-0.5).then_inc(sS.h, 1)
                s_inv = sS.inc()

                @on("vector")
                def _(v, s_inv=s_inv):
                    v.wait_ge(sS.h, s_inv)
                    v.tensor_tensor(r_muinv, r_mu, r_inv, ALU.mult)
                    v.engine_nop().then_inc(sV.h, 1)
                v_muinv = sV.inc()

                @on("tensor")
                def _(t, v_muinv=v_muinv, ci=ci):
                    t.wait_ge(sV.h, v_muinv)
                    t.wait_ge(sXs.h, ci)
                    t.matmul(p_inv[:, :], onesr[:, :], r_inv, start=True, stop=True).then_inc(sT.h, 1)
                t_invrep = sT.inc()

                for k in range(8):
                    @on("sync" if k % 2 == 0 else "scalar")
                    def _(q, k=k, cols=cols, vmark=sV.n, t_stats=t_stats, smark=sS.n):
                        if k >= 2:
                            q.wait_ge(sV.h, vmark)
                        else:
                            q.wait_ge(sT.h, t_stats)
                            q.wait_ge(sS.h, smark)
                        q.dma_start(out=m_x[:, (k % 2) * 512:(k % 2) * 512 + 512],
                                    in_=xbt[k * 128:(k + 1) * 128, cols]
                                    ).then_inc(sxp[k % 2].h, 16)
                    xl = sxp[k % 2].inc(16)

                    @on("vector")
                    def _(v, k=k, xl=xl, t_invrep=t_invrep):
                        v.wait_ge(sxp[k % 2].h, xl)
                        v.wait_ge(sT.h, t_invrep)
                        v.tensor_tensor(m_xs[:, k * 512:(k + 1) * 512],
                                        m_x[:, (k % 2) * 512:(k % 2) * 512 + 512],
                                        p_inv[:, :], ALU.mult)
                        v.engine_nop().then_inc(sV.h, 1)
                        if k == 7:
                            v.engine_nop().then_inc(sXs.h, 1)
                    sV.inc()
                sXs.inc()
                ci_cnt[0] += 1
                prev_xs[0] = sV.n
                return sV.n

            def emit_win_half(c, half, wtag, v_xs):
                cols = slice(c * 512, (c + 1) * 512)
                for m in range(8):
                    pm = m % 2

                    @on("tensor")
                    def _(t, m=m, pm=pm, v_xs=v_xs, wtag=wtag, half=half,
                          smark=pmm_free[pm], wsv=ws_done[half]):
                        t.wait_ge(sV.h, max(v_xs, wsv))
                        t.wait_ge(swin.h, wtag)
                        if smark:
                            t.wait_ge(sS.h, smark)
                        for k in range(8):
                            t.matmul(p_mm[:, pm * 512:pm * 512 + 512],
                                     m_win[:, k * 1024 + m * 128:k * 1024 + m * 128 + 128],
                                     m_xs[:, k * 512:(k + 1) * 512],
                                     start=(k == 0), stop=False)
                        t.matmul(p_mm[:, pm * 512:pm * 512 + 512],
                                 m_wrow[0:1, half * 1024 + m * 128:half * 1024 + m * 128 + 128],
                                 r_muinv, start=False, stop=True).then_inc(sT.h, 1)
                    t_m = sT.inc()

                    if half == 0:
                        @on("vector")
                        def _(v, m=m, pm=pm, t_m=t_m, c=c, smark=sS.n):
                            v.wait_ge(sT.h, t_m)
                            v.wait_ge(sS.h, smark)
                            if c == 0:
                                v.memset(m_uext[:, pm * 516:pm * 516 + 3], 0.0)
                            else:
                                v.tensor_copy(m_uext[:, pm * 516:pm * 516 + 3],
                                              m_halo[:, m * 3:(m + 1) * 3])
                            v.engine_nop().then_inc(sV.h, 1)
                        v_halo = sV.inc()

                        @on("scalar")
                        def _(s, m=m, pm=pm, t_m=t_m, v_halo=v_halo):
                            s.wait_ge(sT.h, t_m)
                            s.wait_ge(sV.h, v_halo)
                            s.activation(m_uext[:, pm * 516 + 3:pm * 516 + 515],
                                         p_mm[:, pm * 512:pm * 512 + 512],
                                         AF.Copy, bias=0.0, scale=1.0).then_inc(sS.h, 1)
                        s_ev = sS.inc()
                        pmm_free[pm] = s_ev

                        @on("vector")
                        def _(v, m=m, pm=pm, s_ev=s_ev):
                            v.wait_ge(sS.h, s_ev)
                            v.tensor_copy(m_halo[:, m * 3:(m + 1) * 3],
                                          m_uext[:, pm * 516 + 512:pm * 516 + 515])
                            v.tensor_scalar(out=m_t1[:, :],
                                            in0=m_uext[:, pm * 516:pm * 516 + 512],
                                            scalar1=convw_sb[:, m * 4:m * 4 + 1],
                                            scalar2=None, op0=ALU.mult)
                            for kk in range(1, 4):
                                v.scalar_tensor_tensor(
                                    m_t1[:, :],
                                    m_uext[:, pm * 516 + kk:pm * 516 + kk + 512],
                                    convw_sb[:, m * 4 + kk:m * 4 + kk + 1],
                                    m_t1[:, :], ALU.mult, ALU.add)
                            v.engine_nop().then_inc(sV.h, 1)
                        v_conv = sV.inc()

                        @on("scalar")
                        def _(s, m=m, pm=pm, v_conv=v_conv, ymark=sup[0].n):
                            s.wait_ge(sV.h, v_conv)
                            s.wait_ge(sup[0].h, ymark)
                            s.activation(m_ucb[:, :], m_t1[:, :],
                                         AF.Silu, bias=convb_sb[:, m:m + 1],
                                         scale=1.0).then_inc(sS.h, 1)
                        s_silu = sS.inc()

                        @on("vector")
                        def _(v, pm=pm, s_silu=s_silu):
                            v.wait_ge(sS.h, s_silu)
                            v.tensor_copy(m_ubf[:, pm * 512:pm * 512 + 512], m_ucb[:, :])
                            v.engine_nop().then_inc(sV.h, 1)
                        v_ubf = sV.inc()

                        @on("sync")
                        def _(sy, m=m, pm=pm, cols=cols, s_silu=s_silu):
                            sy.wait_ge(sS.h, s_silu)
                            sy.dma_start(out=u_hbm[m * 128:(m + 1) * 128, cols],
                                         in_=m_ucb[:, :]).then_inc(sup[0].h, 16)
                        sup[0].inc(16)

                        @on("tensor")
                        def _(t, m=m, pm=pm, v_ubf=v_ubf, w_mw=w_mw):
                            t.wait_ge(smw.h, w_mw)
                            t.wait_ge(sV.h, v_ubf)
                            t.matmul(p_wx[0:96, :], m_wx[:, m * 96:(m + 1) * 96],
                                     m_ubf[:, pm * 512:pm * 512 + 512],
                                     start=(m == 0), stop=(m == 7)).then_inc(sT.h, 1)
                        sT.inc()
                    else:
                        @on("scalar")
                        def _(s, m=m, pm=pm, t_m=t_m, ymark=srp[0].n):
                            s.wait_ge(sT.h, t_m)
                            s.wait_ge(srp[0].h, ymark)
                            s.activation(m_res[:, :],
                                         p_mm[:, pm * 512:pm * 512 + 512],
                                         AF.Silu, bias=0.0, scale=1.0).then_inc(sS.h, 1)
                        s_rs = sS.inc()
                        pmm_free[pm] = s_rs

                        @on("sync")
                        def _(sy, m=m, pm=pm, cols=cols, s_rs=s_rs):
                            sy.wait_ge(sS.h, s_rs)
                            sy.dma_start(out=rs_hbm[m * 128:(m + 1) * 128, cols],
                                         in_=m_res[:, :]).then_inc(srp[0].h, 16)
                        srp[0].inc(16)

                if half == 0:
                    t_wx = sT.n

                    @on("vector")
                    def _(v, c=c, t_wx=t_wx, ymark=sxd.n):
                        v.wait_ge(sT.h, t_wx)
                        v.wait_ge(sxd.h, ymark)
                        v.tensor_copy(m_xdbl[0:96, (c % 2) * 512:(c % 2) * 512 + 512],
                                      p_wx[0:96, :])
                        v.engine_nop().then_inc(sV.h, 1)
                    v_wx = sV.inc()

                    @on("sync")
                    def _(sy, c=c, v_wx=v_wx):
                        sy.wait_ge(sV.h, v_wx)
                        sy.dma_start(out=xdbl_in[c // 2][:, (c % 2) * 512:(c % 2) * 512 + 512],
                                     in_=m_xdbl[0:96, (c % 2) * 512:(c % 2) * 512 + 512]
                                     ).then_inc(sxd.h, 16)
                    sxd.inc(16)

            def emit_delta(sc, c_ar, v_ws0):
                @on("gpsimd")
                def _(g, sc=sc, c_ar=c_ar):
                    g.wait_ge(sCar.h, c_ar)
                    g.dma_start(out=m_dlt[:, :], in_=xdbl_out[sc][0:64, :]).then_inc(sdl1.h, 16)
                dl_ld = sdl1.inc(16)

                dmarks = {}
                spmk = {0: 0, 1: 0}
                vdmk = {0: 0, 1: 0}
                stmk = {0: 0, 1: 0}
                for cb in range(8):
                    for tc in range(2):
                        pp = (cb * 2 + tc) % 2

                        @on("tensor")
                        def _(t, cb=cb, tc=tc, pp=pp, dl_ld=dl_ld, w_mw=w_mw,
                              sm=spmk[pp], vm=vdmk[pp], vmark=sV.n, v_ws0=v_ws0):
                            t.wait_ge(sdl1.h, dl_ld)
                            t.wait_ge(smw.h, w_mw)
                            if sm:
                                t.wait_ge(sS.h, sm)
                            if vm:
                                t.wait_ge(sV.h, vm)
                            if not (cb or tc):
                                t.wait_ge(sV.h, max(vmark, v_ws0))
                            t.matmul(p_dl[:, pp * 512:pp * 512 + 512],
                                     m_wdt[:, cb * 128:(cb + 1) * 128],
                                     m_dlt[:, tc * 512:(tc + 1) * 512],
                                     start=True, stop=True).then_inc(sT.h, 1)
                        t_d = sT.inc()

                        dX = m_t2 if pp == 0 else m_t1
                        dY = m_t3 if pp == 0 else m_res

                        @on("scalar")
                        def _(s, cb=cb, pp=pp, dX=dX, dY=dY, t_d=t_d, vm=vdmk[pp]):
                            s.wait_ge(sT.h, t_d)
                            if vm:
                                s.wait_ge(sV.h, vm)
                            s.activation(dX[:, :],
                                         p_dl[:, pp * 512:pp * 512 + 512], AF.Abs,
                                         bias=bdt_sb[:, cb:cb + 1], scale=1.0)
                            s.activation(dY[:, :], dX[:, :],
                                         AF.Exp, bias=0.0, scale=-1.0)
                            s.activation(dX[:, :], dY[:, :],
                                         AF.Ln, bias=1.0,
                                         scale=1.0).then_inc(sS.h, 1)
                        s_sp = sS.inc()
                        spmk[pp] = s_sp

                        @on("vector")
                        def _(v, cb=cb, tc=tc, pp=pp, dX=dX, dY=dY, s_sp=s_sp,
                              ym=stmk[pp], rp0=srp[0].n, rp1=srp[1].n):
                            v.wait_ge(sS.h, s_sp)
                            if ym:
                                v.wait_ge(sdst.h, ym)
                            elif pp == 1:
                                v.wait_ge(srp[0].h, rp0)
                                v.wait_ge(srp[1].h, rp1)
                            v.tensor_scalar(out=dY[:, :],
                                            in0=p_dl[:, pp * 512:pp * 512 + 512],
                                            scalar1=bdt_sb[:, cb:cb + 1], scalar2=0.0,
                                            op0=ALU.add, op1=ALU.max)
                            v.tensor_tensor(dY[:, :], dY[:, :], dX[:, :], ALU.add)
                            v.engine_nop().then_inc(sV.h, 1)
                        v_d = sV.inc()
                        vdmk[pp] = v_d

                        @on("sync")
                        def _(sy, sc=sc, cb=cb, tc=tc, dY=dY, v_d=v_d):
                            sy.wait_ge(sV.h, v_d)
                            sy.dma_start(
                                out=dl_hbm[cb * 128:(cb + 1) * 128,
                                           sc * 1024 + tc * 512:sc * 1024 + tc * 512 + 512],
                                in_=dY[:, :]).then_inc(sdst.h, 16)
                        dmarks[(cb, tc)] = sdst.inc(16)
                        stmk[pp] = dmarks[(cb, tc)]
                return dmarks

            scan_marks = []
            cb_marks = []
            passc_t = {}

            def emit_passB(hsc, y_dl):
                sc = hsc // 2
                half = hsc % 2
                yb = (hsc % 2) * 4096
                colg = slice(sc * 1024 + half * 512, sc * 1024 + half * 512 + 512)
                tprev = passc_t.get(hsc - 2, 0)
                for cb in range(8):
                    pb = cb % 2
                    slab_free = cb_marks[-2] if len(cb_marks) >= 2 else 0

                    @on("sync")
                    def _(sy, cb=cb, pb=pb, colg=colg, y_dl=y_dl, vmark=slab_free,
                          dmk=y_dl[4][(cb, half)]):
                        sy.wait_ge(sup[0].h, y_dl[0])
                        sy.wait_ge(sup[1].h, y_dl[1])
                        sy.wait_ge(srp[0].h, y_dl[2])
                        sy.wait_ge(srp[1].h, y_dl[3])
                        sy.wait_ge(sdst.h, dmk)
                        if vmark:
                            sy.wait_ge(sV.h, vmark)
                        sy.dma_start(out=b_dl[:, pb * 512:pb * 512 + 512],
                                     in_=dl_hbm[cb * 128:(cb + 1) * 128, colg]
                                     ).then_inc(slp[pb].h, 16)
                        sy.dma_start(out=b_u[:, pb * 512:pb * 512 + 512],
                                     in_=u_hbm[cb * 128:(cb + 1) * 128, colg]
                                     ).then_inc(slp[pb].h, 16)

                    @on("scalar")
                    def _(s, cb=cb, pb=pb, colg=colg, y_dl=y_dl, vmark=slab_free,
                          dmk=y_dl[4][(cb, half)]):
                        s.wait_ge(srp[0].h, y_dl[2])
                        s.wait_ge(srp[1].h, y_dl[3])
                        s.wait_ge(sdst.h, dmk)
                        if vmark:
                            s.wait_ge(sV.h, vmark)
                        s.dma_start(out=b_rs[:, pb * 512:pb * 512 + 512],
                                    in_=rs_hbm[cb * 128:(cb + 1) * 128, colg]
                                    ).then_inc(slp[pb].h, 16)
                    x_sl = slp[pb].inc(16 * 3)

                    @on("vector")
                    def _(v, cb=cb, pb=pb, x_sl=x_sl):
                        v.wait_ge(slp[pb].h, x_sl)
                        v.tensor_tensor(b_du[:, :], b_dl[:, pb * 512:pb * 512 + 512],
                                        b_u[:, pb * 512:pb * 512 + 512], ALU.mult)
                        v.engine_nop().then_inc(sV.h, 1)
                    sV.inc()

                    for n in range(NST):
                        pn = n % 4
                        bm_free = scan_marks[-4] if len(scan_marks) >= 4 else 0
                        da_free = scan_marks[-2] if len(scan_marks) >= 2 else 0

                        @on("gpsimd")
                        def _(g, sc=sc, half=half, n=n, pn=pn, vfree=bm_free):
                            if vfree:
                                g.wait_ge(sV.h, vfree)
                            g.dma_start(
                                out=b_bm[:, pn * 512:pn * 512 + 512],
                                in_=xdbl_out[sc][64 + n:65 + n, half * 512:half * 512 + 512]
                                .broadcast_to([128, 512])).then_inc(sbp[pn].h, 16)
                            g.dma_start(
                                out=b_cm[:, pn * 512:pn * 512 + 512],
                                in_=xdbl_out[sc][80 + n:81 + n, half * 512:half * 512 + 512]
                                .broadcast_to([128, 512])).then_inc(sbp[pn].h, 16)
                        g_bc = sbp[pn].inc(32)

                        @on("scalar")
                        def _(s, cb=cb, pb=pb, n=n, x_sl=x_sl, vfree=da_free):
                            s.wait_ge(slp[pb].h, x_sl)
                            if vfree:
                                s.wait_ge(sV.h, vfree)
                            s.activation(b_dA[:, (n % 2) * 512:(n % 2) * 512 + 512],
                                         b_dl[:, pb * 512:pb * 512 + 512], AF.Exp, bias=0.0,
                                         scale=a_sb[:, cb * 16 + n:cb * 16 + n + 1]
                                         ).then_inc(sS.h, 1)
                        s_dA = sS.inc()

                        @on("vector")
                        def _(v, cb=cb, n=n, pn=pn, g_bc=g_bc, s_dA=s_dA):
                            v.wait_ge(sbp[pn].h, g_bc)
                            v.tensor_tensor(b_dBu[:, :], b_du[:, :],
                                            b_bm[:, pn * 512:pn * 512 + 512], ALU.mult)
                            v.wait_ge(sS.h, s_dA)
                            v.tensor_tensor_scan(
                                b_h[:, :], b_dA[:, (n % 2) * 512:(n % 2) * 512 + 512],
                                b_dBu[:, :],
                                hstate[:, cb * 16 + n:cb * 16 + n + 1],
                                ALU.mult, ALU.add)
                            v.tensor_copy(hstate[:, cb * 16 + n:cb * 16 + n + 1],
                                          b_h[:, 511:512])
                            v.tensor_tensor(b_hC[:, :], b_h[:, :],
                                            b_cm[:, pn * 512:pn * 512 + 512], ALU.mult)
                            if n == 0:
                                v.tensor_copy(b_y[:, :], b_hC[:, :])
                            else:
                                v.tensor_tensor(b_y[:, :], b_y[:, :], b_hC[:, :], ALU.add)
                            v.engine_nop().then_inc(sV.h, 1)
                        scan_marks.append(sV.inc())

                    @on("vector")
                    def _(v, cb=cb, pb=pb, yb=yb, tprev=tprev):
                        if tprev:
                            v.wait_ge(sT.h, tprev)
                        v.scalar_tensor_tensor(b_y[:, :], b_u[:, pb * 512:pb * 512 + 512],
                                               dp_sb[:, cb:cb + 1], b_y[:, :],
                                               ALU.mult, ALU.add)
                        v.tensor_tensor(b_ybf[:, yb + cb * 512:yb + (cb + 1) * 512],
                                        b_y[:, :],
                                        b_rs[:, pb * 512:pb * 512 + 512], ALU.mult)
                        v.engine_nop().then_inc(sV.h, 1)
                    cb_marks.append(sV.inc())
                    if dbg:
                        @on("sync")
                        def _(sy, hsc=hsc, cb=cb, yb=yb, colg=colg, vm=cb_marks[-1]):
                            sy.wait_ge(sV.h, vm)
                            sy.dma_start(
                                out=ybf_dbg[cb * 128:(cb + 1) * 128, colg],
                                in_=b_ybf[:, yb + cb * 512:yb + (cb + 1) * 512]
                                ).then_inc(sY.h, 16)
                        sY.inc(16)
                return sV.n

            def emit_passC(hsc, v_ybf):
                yb = (hsc % 2) * 4096
                for dm in range(8):
                    pm = dm % 2

                    @on("tensor")
                    def _(t, dm=dm, pm=pm, v_ybf=v_ybf, w_mw=w_mw,
                          smark=pmm_free[pm], yb=yb):
                        t.wait_ge(sV.h, v_ybf)
                        t.wait_ge(smw.h, w_mw)
                        if smark:
                            t.wait_ge(sS.h, smark)
                        for cb in range(7):
                            t.matmul(p_mm[:, pm * 512:pm * 512 + 512],
                                     m_wout[:, cb * 1024 + dm * 128:cb * 1024 + dm * 128 + 128],
                                     b_ybf[:, yb + cb * 512:yb + (cb + 1) * 512],
                                     start=(cb == 0), stop=False)
                        t.matmul(p_mm[:, pm * 512:pm * 512 + 512],
                                 m_wout[:, 7 * 1024 + dm * 128:7 * 1024 + dm * 128 + 128],
                                 b_ybf[:, yb + 7 * 512:yb + 8 * 512],
                                 start=False, stop=True).then_inc(sT.h, 1)
                    t_w = sT.inc()

                    @on("scalar")
                    def _(s, pm=pm, t_w=t_w, ymark=swt.n):
                        s.wait_ge(sT.h, t_w)
                        s.wait_ge(swt.h, ymark)
                        s.activation(c_wo[:, :], p_mm[:, pm * 512:pm * 512 + 512],
                                     AF.Copy, bias=0.0, scale=1.0).then_inc(sS.h, 1)
                    s_w = sS.inc()
                    pmm_free[pm] = s_w

                    @on("sync")
                    def _(sy, hsc=hsc, dm=dm, s_w=s_w):
                        sy.wait_ge(sS.h, s_w)
                        src = c_wo[:, :].rearrange("p (a b) -> p a b", b=128)
                        sy.dma_start(
                            out=rsin[hsc][0][dm * 128:(dm + 1) * 128, :]
                            .rearrange("p (a b) -> p a b", b=128),
                            in_=src[:, 0:4:2, :]).then_inc(swt.h, 16)
                        sy.dma_start(
                            out=rsin[hsc][1][dm * 128:(dm + 1) * 128, :]
                            .rearrange("p (a b) -> p a b", b=128),
                            in_=src[:, 1:4:2, :]).then_inc(swt.h, 16)
                    swt.inc(32)
                y_w = swt.n

                passc_t[hsc] = sT.n

                @on("gpsimd")
                def _(g, hsc=hsc, y_w=y_w):
                    g.wait_ge(swt.h, y_w)
                    g.collective_compute(
                        "ReduceScatter", ALU.add,
                        replica_groups=[[0, 1], [2, 3], [4, 5], [6, 7]],
                        ins=[rsin[hsc]], outs=[rsout[hsc]],
                    ).then_inc(sCrs.h, 1)
                return sCrs.inc()

            def emit_stream_write(hsc, c_rs_done):
                if dbg:
                    @on("gpsimd")
                    def _(g, hsc=hsc, c_rs_done=c_rs_done):
                        g.wait_ge(sCrs.h, c_rs_done)
                        g.dma_start(out=rso_dbg[hsc], in_=rsout[hsc]).then_inc(sG.h, 16)
                        g.dma_start(out=rsi_dbg[hsc], in_=rsin[hsc]).then_inc(sG.h, 16)
                    sG.inc(32)
                for dm in range(8):
                    pm = dm % 2

                    @on("sync")
                    def _(sy, hsc=hsc, dm=dm, c_rs_done=c_rs_done, vmark=sV.n):
                        sy.wait_ge(sCrs.h, c_rs_done)
                        sy.wait_ge(sV.h, vmark)
                        sy.dma_start(out=c_rs[:, :],
                                     in_=rsout[hsc][dm * 128:(dm + 1) * 128, :]
                                     ).then_inc(scp[0].h, 16)
                        sy.dma_start(out=c_xr[:, :],
                                     in_=xres[dm * 128:(dm + 1) * 128,
                                              hsc * 256:(hsc + 1) * 256]).then_inc(scp[0].h, 16)
                    x_c = scp[0].inc(32)

                    @on("vector")
                    def _(v, hsc=hsc, dm=dm, x_c=x_c):
                        v.wait_ge(scp[0].h, x_c)
                        v.tensor_tensor(
                            stream[:, dm * T + hsc * 256:dm * T + hsc * 256 + 256],
                            c_rs[:, :], c_xr[:, :], ALU.add)
                        v.engine_nop().then_inc(sV.h, 1)
                    sV.inc()
                return sV.n

            pend_passC = []
            pend_rs = []
            stream_done = [0]
            for sc in range(_NSC):
                for c in (2 * sc, 2 * sc + 1):
                    w_u = load_win(0)
                    if c == 0:
                        ws_done[0] = emit_ws(0, w_u)
                    v_xs = emit_stats_xs(c)
                    emit_win_half(c, 0, w_u, v_xs)
                    w_r = load_win(1)
                    if c == 0:
                        ws_done[1] = emit_ws(1, w_r)
                    emit_win_half(c, 1, w_r, v_xs)

                y_xdbl = sxd.n

                @on("gpsimd")
                def _(g, sc=sc, y_xdbl=y_xdbl):
                    g.wait_ge(sxd.h, y_xdbl)
                    g.collective_compute(
                        "AllReduce", ALU.add,
                        replica_groups=[[0, 1], [2, 3], [4, 5], [6, 7]],
                        ins=[xdbl_in[sc]], outs=[xdbl_out[sc]],
                    ).then_inc(sCar.h, 1)
                c_ar = sCar.inc()

                dmarks = emit_delta(sc, c_ar, max(ws_done))
                y_dl = (sup[0].n, sup[1].n, srp[0].n, srp[1].n, dmarks)

                for half in range(2):
                    hsc = 2 * sc + half
                    v_ybf = emit_passB(hsc, y_dl)
                    pend_passC.append((hsc, v_ybf))
                    while pend_passC and pend_passC[0][0] <= hsc - 1:
                        h0, vy = pend_passC.pop(0)
                        pend_rs.append((h0, emit_passC(h0, vy)))
                while pend_rs and pend_rs[0][0] <= 2 * sc - 4:
                    h0, cd = pend_rs.pop(0)
                    stream_done[0] = emit_stream_write(h0, cd)

            for h0, vy in pend_passC:
                pend_rs.append((h0, emit_passC(h0, vy)))
            for h0, cd in pend_rs:
                stream_done[0] = emit_stream_write(h0, cd)

            if stage == "mamba":
                @on("sync")
                def _(sy, v=stream_done[0], tgt=sY.n + 16 * 8):
                    sy.wait_ge(sV.h, v)
                    for cc in range(8):
                        sy.dma_start(out=sdump[cc],
                                     in_=stream[:, cc * T:(cc + 1) * T]).then_inc(sY.h, 16)
                    sy.wait_ge(sY.h, tgt)
                sY.inc(16 * 8)

        barrier()

        def emit_ffn(call):
            with contextlib.ExitStack() as _fst:
                f_xs = _fst.enter_context(nc.sbuf_tensor(f"f_xs{call}", [128, 8 * 1024], BF16))
                f_mrow = _fst.enter_context(nc.sbuf_tensor(f"f_mrow{call}", [1, 1024], F32))
                f_gup = _fst.enter_context(nc.sbuf_tensor(f"f_gup{call}", [128, 16384], BF16))
                f_dwn = _fst.enter_context(nc.sbuf_tensor(f"f_dwn{call}", [128, 8192], BF16))
                f_act = _fst.enter_context(nc.sbuf_tensor(f"f_act{call}", [128, 8 * 512], BF16))
                f_dacc = _fst.enter_context(nc.sbuf_tensor(f"f_dacc{call}", [128, 8 * 1024], F32))
                f_sg = _fst.enter_context(nc.sbuf_tensor(f"f_sg{call}", [128, 512], F32))
                f_a1 = _fst.enter_context(nc.sbuf_tensor(f"f_a1{call}", [128, 512], F32))
                f_sq = _fst.enter_context(nc.sbuf_tensor(f"f_sq{call}", [128, 512], F32))
                f_rt = _fst.enter_context(nc.sbuf_tensor(f"f_rt{call}", [8, 2048], F32))
                f_gw = _fst.enter_context(nc.sbuf_tensor(f"f_gw{call}", [128, 64], BF16))
                f_oh = _fst.enter_context(nc.sbuf_tensor(f"f_oh{call}", [8, 1024], F32))
                f_r8 = _fst.enter_context(nc.sbuf_tensor(f"f_r8{call}", [128, 48], F32))
                pf_st = _fst.enter_context(nc.psum_tensor(f"pf_st{call}", [1, 512], F32))
                pf_ms = _fst.enter_context(nc.psum_tensor(f"pf_ms{call}", [128, 512], F32))
                pf_g = _fst.enter_context(nc.psum_tensor(f"pf_g{call}", [128, 2 * 512], F32))
                pf_u = _fst.enter_context(nc.psum_tensor(f"pf_u{call}", [128, 2 * 512], F32))
                pf_d = _fst.enter_context(nc.psum_tensor(f"pf_d{call}", [128, 2 * 512], F32))

                sfg = mksem(f"sfg{call}")
                sdp = mksem(f"sdp{call}")
                sfw = mksem(f"sfw{call}")

                @on("gpsimd")
                def _(g):
                    ld3(g, f_gw[:, :], gw_t, 8, sfw.h)
                w_gw = sfw.inc(16)

                @on("sync")
                def _(sy):
                    sy.dma_start(out=f_oh[:, :], in_=oh8).then_inc(sW.h, 16)
                w_oh = sW.inc(16)

                v_rw = 0
                t_gu_last = {}
                t_dn_last = {}
                v_end = sV.n

                for H in range(2):
                    @on("sync")
                    def _(sy, H=H, tprev=t_gu_last.get((H - 1, NE - 1), 0)):
                        if tprev:
                            sy.wait_ge(sT.h, tprev)
                        sy.dma_start(out=f_gup[:, :], in_=gup_bf[0]).then_inc(sfg.h, 16)
                    gup_marks = {0: sfg.inc(16)}

                    @on("gpsimd")
                    def _(g, H=H, tprev=t_dn_last.get((H - 1, NE - 1), 0)):
                        if tprev:
                            g.wait_ge(sT.h, tprev)
                        g.dma_start(out=f_dwn[:, :], in_=dwn_bf[0]).then_inc(sdp.h, 16)
                    dwn_marks = {0: sdp.inc(16)}

                    for fl in range(2):
                        f = 2 * H + fl
                        scol = H * 1024 + fl * 512
                        for k in range(8):
                            @on("scalar")
                            def _(s, k=k, scol=scol, tmark=sT.n, vmark=sV.n):
                                s.wait_ge(sT.h, tmark)
                                if k == 0:
                                    s.wait_ge(sV.h, vmark)
                                s.activation(f_sq[:, :], stream[:, k * T + scol:k * T + scol + 512],
                                             AF.Square, bias=0.0, scale=1.0).then_inc(sS.h, 1)
                            sqd = sS.inc()

                            @on("tensor")
                            def _(t, k=k, scol=scol, sqd=sqd, vmark=sV.n):
                                if k == 0:
                                    t.wait_ge(sV.h, vmark)
                                t.matmul(pf_st[0:1, 0:512], onesc[:, :],
                                         stream[:, k * T + scol:k * T + scol + 512],
                                         start=(k == 0), stop=(k == 7))
                                t.wait_ge(sS.h, sqd)
                                t.matmul(pf_d[0:1, 512:1024], onesc[:, :], f_sq[:, :],
                                         start=(k == 0), stop=(k == 7)).then_inc(sT.h, 1)
                            sT.inc()
                        t_stats = sT.n

                        @on("vector")
                        def _(v, t_stats=t_stats):
                            v.wait_ge(sT.h, t_stats)
                            v.tensor_scalar(out=r_mu, in0=pf_st[0:1, 0:512], scalar1=1.0 / DM,
                                            scalar2=None, op0=ALU.mult)
                            v.tensor_scalar(out=f_sq[0:1, :], in0=pf_d[0:1, 512:1024],
                                            scalar1=1.0 / DM, scalar2=None, op0=ALU.mult)
                            v.tensor_tensor(r_inv, r_mu, r_mu, ALU.mult)
                            v.tensor_tensor(f_sq[0:1, :], f_sq[0:1, :], r_inv, ALU.subtract)
                            v.tensor_scalar(out=f_sq[0:1, :], in0=f_sq[0:1, :], scalar1=EPS_LN,
                                            scalar2=None, op0=ALU.add)
                            v.engine_nop().then_inc(sV.h, 1)
                        v_rm = sV.inc()

                        @on("scalar")
                        def _(s, v_rm=v_rm):
                            s.wait_ge(sV.h, v_rm)
                            s.activation(f_a1[0:1, :], f_sq[0:1, :], AF.Ln, bias=0.0, scale=1.0)
                            s.activation(r_inv, f_a1[0:1, :], AF.Exp, bias=0.0,
                                         scale=-0.5).then_inc(sS.h, 1)
                        s_inv = sS.inc()

                        @on("vector")
                        def _(v, fl=fl, s_inv=s_inv):
                            v.wait_ge(sS.h, s_inv)
                            v.tensor_tensor(r_muinv, r_mu, r_inv, ALU.mult)
                            v.tensor_copy(f_mrow[0:1, fl * 512:(fl + 1) * 512], r_muinv)
                            v.engine_nop().then_inc(sV.h, 1)
                        v_muinv = sV.inc()

                        @on("tensor")
                        def _(t, v_muinv=v_muinv, vmark=sV.n):
                            t.wait_ge(sV.h, v_muinv)
                            t.wait_ge(sV.h, vmark)
                            t.matmul(pf_ms[:, :], onesr[:, :], r_inv, start=True,
                                     stop=True)
                            t.matmul(pf_u[:, 0:512], onesr[:, :], r_muinv, start=True,
                                     stop=True).then_inc(sT.h, 1)
                        t_invrep = sT.inc()

                        for k in range(8):
                            @on("vector")
                            def _(v, k=k, fl=fl, scol=scol, H=H, t_invrep=t_invrep,
                                  tprev=t_gu_last.get((H - 1, NE - 1), 0)):
                                v.wait_ge(sT.h, t_invrep)
                                if k == 0 and tprev:
                                    v.wait_ge(sT.h, tprev)
                                v.tensor_tensor(f_a1[:, :],
                                                stream[:, k * T + scol:k * T + scol + 512],
                                                pf_ms[:, :], ALU.mult)
                                v.tensor_tensor(f_xs[:, k * 1024 + fl * 512:
                                                     k * 1024 + fl * 512 + 512],
                                                f_a1[:, :], pf_u[:, 0:512], ALU.subtract)
                                v.engine_nop().then_inc(sV.h, 1)
                            sV.inc()
                        v_xs = sV.n
                        if dbg and call == 0 and fl == 1:
                            @on("sync")
                            def _(sy, H=H, v_xs=v_xs):
                                sy.wait_ge(sV.h, v_xs)
                                sy.dma_start(out=xs2_dbg[H],
                                             in_=f_xs[:, :]).then_inc(sY.h, 16)
                            sY.inc(16)

                        for tt_ in range(4):
                            @on("tensor")
                            def _(t, tt_=tt_, fl=fl, v_xs=v_xs, w_gw=w_gw,
                                  v_rw=v_rw, vmark=sV.n):
                                t.wait_ge(sV.h, max(v_xs, v_rw))
                                t.wait_ge(sfw.h, w_gw)
                                if tt_ > 0:
                                    t.wait_ge(sV.h, vmark)
                                for k in range(7):
                                    t.matmul(pf_d[:, 0:8],
                                             f_xs[:, k * 1024 + fl * 512 + tt_ * 128:
                                                  k * 1024 + fl * 512 + tt_ * 128 + 128],
                                             f_gw[:, k * 8:(k + 1) * 8],
                                             start=(k == 0), stop=False)
                                t.matmul(pf_d[:, 0:8],
                                         f_xs[:, 7 * 1024 + fl * 512 + tt_ * 128:
                                              7 * 1024 + fl * 512 + tt_ * 128 + 128],
                                         f_gw[:, 56:64],
                                         start=False, stop=True).then_inc(sT.h, 1)
                            t_lg = sT.inc()

                            @on("vector")
                            def _(v, t_lg=t_lg):
                                v.wait_ge(sT.h, t_lg)
                                v.reduce_max(f_r8[:, 8:9], pf_d[:, 0:8], axis=AX)
                                v.drain()
                                v.tensor_scalar(out=f_r8[:, 9:10], in0=f_r8[:, 8:9], scalar1=-1.0,
                                                scalar2=None, op0=ALU.mult)
                                v.drain()
                                v.engine_nop().then_inc(sV.h, 1)
                            v_m8 = sV.inc()

                            @on("scalar")
                            def _(s, v_m8=v_m8):
                                s.wait_ge(sV.h, v_m8)
                                s.activation(f_r8[:, 16:24], pf_d[:, 0:8], AF.Exp,
                                             bias=f_r8[:, 9:10], scale=1.0,
                                             accum_out=f_r8[:, 10:11]).then_inc(sS.h, 1)
                            s_p8 = sS.inc()

                            @on("vector")
                            def _(v, s_p8=s_p8):
                                v.wait_ge(sS.h, s_p8)
                                v.reciprocal(f_r8[:, 11:12], f_r8[:, 10:11])
                                v.drain()
                                v.tensor_scalar(out=f_r8[:, 16:24], in0=f_r8[:, 16:24],
                                                scalar1=f_r8[:, 11:12], scalar2=None, op0=ALU.mult)
                                v.drain()
                                v.reduce_max(f_r8[:, 12:13], f_r8[:, 16:24], axis=AX)
                                v.drain()
                                v.tensor_scalar(out=f_r8[:, 24:32], in0=f_r8[:, 16:24],
                                                scalar1=f_r8[:, 12:13], scalar2=None, op0=ALU.is_ge)
                                v.drain()
                                v.scalar_tensor_tensor(f_r8[:, 32:40], f_r8[:, 24:32], -1e30,
                                                       f_r8[:, 16:24], ALU.mult, ALU.add)
                                v.drain()
                                v.reduce_max(f_r8[:, 13:14], f_r8[:, 32:40], axis=AX)
                                v.drain()
                                v.tensor_scalar(out=f_r8[:, 32:40], in0=f_r8[:, 32:40],
                                                scalar1=f_r8[:, 13:14], scalar2=None, op0=ALU.is_ge)
                                v.tensor_tensor(f_r8[:, 14:15], f_r8[:, 12:13], f_r8[:, 13:14],
                                                ALU.add)
                                v.drain()
                                v.tensor_scalar(out=f_r8[:, 15:16], in0=f_r8[:, 14:15],
                                                scalar1=EPS_R, scalar2=None, op0=ALU.add)
                                v.drain()
                                v.reciprocal(f_r8[:, 40:41], f_r8[:, 15:16])
                                v.drain()
                                v.tensor_tensor(f_r8[:, 41:42], f_r8[:, 14:15], f_r8[:, 40:41],
                                                ALU.mult)
                                v.drain()
                                v.tensor_scalar(out=f_r8[:, 41:42], in0=f_r8[:, 41:42],
                                                scalar1=EPS_R, scalar2=None, op0=ALU.add)
                                v.drain()
                                v.reciprocal(f_r8[:, 42:43], f_r8[:, 41:42])
                                v.drain()
                                v.tensor_tensor(f_r8[:, 43:44], f_r8[:, 40:41], f_r8[:, 42:43],
                                                ALU.mult)
                                v.drain()
                                v.tensor_tensor(f_r8[:, 44:45], f_r8[:, 12:13], f_r8[:, 43:44],
                                                ALU.mult)
                                v.tensor_tensor(f_r8[:, 45:46], f_r8[:, 13:14], f_r8[:, 43:44],
                                                ALU.mult)
                                v.drain()
                                v.tensor_scalar(out=f_r8[:, 0:8], in0=f_r8[:, 24:32],
                                                scalar1=f_r8[:, 44:45], scalar2=None, op0=ALU.mult)
                                v.drain()
                                v.scalar_tensor_tensor(f_r8[:, 0:8], f_r8[:, 32:40],
                                                       f_r8[:, 45:46], f_r8[:, 0:8],
                                                       ALU.mult, ALU.add)
                                v.engine_nop().then_inc(sV.h, 1)
                            v_rtd = sV.inc()

                            @on("tensor")
                            def _(t, v_rtd=v_rtd, w_oh=w_oh):
                                t.wait_ge(sV.h, v_rtd)
                                t.wait_ge(sW.h, w_oh)
                                t.transpose(pf_ms[0:8, 0:128], f_r8[:, 0:8],
                                            iden_sb[:, :]).then_inc(sT.h, 1)
                            t_tr = sT.inc()

                            @on("vector")
                            def _(v, tt_=tt_, f=f, t_tr=t_tr):
                                v.wait_ge(sT.h, t_tr)
                                v.tensor_copy(f_rt[0:8, f * 512 + tt_ * 128:
                                                   f * 512 + tt_ * 128 + 128],
                                              pf_ms[0:8, 0:128])
                                v.engine_nop().then_inc(sV.h, 1)
                            sV.inc()
                    v_rt = sV.n

                    silu_marks = {0: 0, 1: 0}
                    gating_marks = {0: 0, 1: 0}
                    evac_marks = {0: 0, 1: 0}
                    for e in range(NE):
                        w_g = gup_marks[e]
                        w_d = dwn_marks[e]

                        for tch in range(2):
                            @on("tensor")
                            def _(t, e=e, H=H, tch=tch, v_rt=v_rt, w_oh=w_oh, vmark=sV.n):
                                t.wait_ge(sV.h, max(v_rt, vmark))
                                t.wait_ge(sW.h, w_oh)
                                t.matmul(pf_ms[:, :], f_oh[0:8, e * 128:(e + 1) * 128],
                                         f_rt[0:8, H * 1024 + tch * 512:
                                              H * 1024 + tch * 512 + 512],
                                         start=True, stop=True).then_inc(sT.h, 1)
                            t_rep = sT.inc()

                            for qi in range(8):
                                pp = qi % 2

                                @on("tensor")
                                def _(t, e=e, qi=qi, pp=pp, tch=tch, w_g=w_g,
                                      sm=silu_marks[pp], gm=gating_marks[pp]):
                                    t.wait_ge(sfg.h, w_g)
                                    if sm:
                                        t.wait_ge(sS.h, sm)
                                    if gm:
                                        t.wait_ge(sV.h, gm)
                                    for kc in range(8):
                                        t.matmul(pf_g[:, pp * 512:pp * 512 + 512],
                                                 f_gup[:, kc * 2048 + qi * 128:
                                                       kc * 2048 + qi * 128 + 128],
                                                 f_xs[:, kc * 1024 + tch * 512:
                                                      kc * 1024 + tch * 512 + 512],
                                                 start=(kc == 0), stop=(kc == 7))
                                    for kc in range(7):
                                        t.matmul(pf_u[:, pp * 512:pp * 512 + 512],
                                                 f_gup[:, kc * 2048 + 1024 + qi * 128:
                                                       kc * 2048 + 1024 + qi * 128 + 128],
                                                 f_xs[:, kc * 1024 + tch * 512:
                                                      kc * 1024 + tch * 512 + 512],
                                                 start=(kc == 0), stop=False)
                                    t.matmul(pf_u[:, pp * 512:pp * 512 + 512],
                                             f_gup[:, 7 * 2048 + 1024 + qi * 128:
                                                   7 * 2048 + 1024 + qi * 128 + 128],
                                             f_xs[:, 7 * 1024 + tch * 512:
                                                  7 * 1024 + tch * 512 + 512],
                                             start=False, stop=True).then_inc(sT.h, 1)
                                t_gu = sT.inc()

                                @on("scalar")
                                def _(s, pp=pp, t_gu=t_gu, gm=gating_marks[1 - pp], vmark=sV.n):
                                    s.wait_ge(sT.h, t_gu)
                                    s.wait_ge(sV.h, vmark)
                                    s.activation(f_sg[:, :], pf_g[:, pp * 512:pp * 512 + 512],
                                                 AF.Silu, bias=0.0, scale=1.0).then_inc(sS.h, 1)
                                silu_marks[pp] = sS.inc()

                                @on("vector")
                                def _(v, e=e, qi=qi, pp=pp, H=H, s_sg=silu_marks[pp],
                                      t_rep=t_rep,
                                      tdn=t_dn_last.get((H, e - 1),
                                                        t_dn_last.get((H - 1, NE - 1), 0))):
                                    v.wait_ge(sS.h, s_sg)
                                    v.wait_ge(sT.h, t_rep)
                                    if tdn:
                                        v.wait_ge(sT.h, tdn)
                                    v.tensor_tensor(f_a1[:, :], f_sg[:, :],
                                                    pf_u[:, pp * 512:pp * 512 + 512], ALU.mult)
                                    v.tensor_tensor(f_act[:, qi * 512:(qi + 1) * 512],
                                                    f_a1[:, :], pf_ms[:, :], ALU.mult)
                                    v.engine_nop().then_inc(sV.h, 1)
                                gating_marks[pp] = sV.inc()
                            v_act = gating_marks[(NE - 1) % 2]
                            t_gu_last[(H, e)] = sT.n

                            for oi in range(8):
                                pd = oi % 2

                                @on("tensor")
                                def _(t, e=e, oi=oi, pd=pd, w_d=w_d, v_act=v_act,
                                      em=evac_marks[pd]):
                                    t.wait_ge(sdp.h, w_d)
                                    t.wait_ge(sV.h, max(v_act, em))
                                    for kc in range(8):
                                        t.matmul(pf_d[:, pd * 512:pd * 512 + 512],
                                                 f_dwn[:, kc * 1024 + oi * 128:
                                                       kc * 1024 + oi * 128 + 128],
                                                 f_act[:, kc * 512:(kc + 1) * 512],
                                                 start=(kc == 0), stop=(kc == 7)).then_inc(sT.h, 1)
                                t_dn = sT.inc(8)

                                @on("vector")
                                def _(v, e=e, oi=oi, pd=pd, tch=tch, t_dn=t_dn):
                                    v.wait_ge(sT.h, t_dn)
                                    dst = f_dacc[:, oi * 1024 + tch * 512:
                                                 oi * 1024 + tch * 512 + 512]
                                    if e == 0:
                                        v.tensor_copy(dst, pf_d[:, pd * 512:pd * 512 + 512])
                                    else:
                                        v.tensor_tensor(dst, dst,
                                                        pf_d[:, pd * 512:pd * 512 + 512],
                                                        ALU.add)
                                    v.engine_nop().then_inc(sV.h, 1)
                                evac_marks[pd] = sV.inc()
                            t_dn_last[(H, e)] = sT.n

                        if e + 1 < NE:
                            @on("sync")
                            def _(sy, e=e, tm=t_gu_last[(H, e)]):
                                sy.wait_ge(sT.h, tm)
                                sy.dma_start(out=f_gup[:, :],
                                             in_=gup_bf[e + 1]).then_inc(sfg.h, 16)
                            gup_marks[e + 1] = sfg.inc(16)

                            @on("gpsimd")
                            def _(g, e=e, tm=t_dn_last[(H, e)]):
                                g.wait_ge(sT.h, tm)
                                g.dma_start(out=f_dwn[:, :],
                                            in_=dwn_bf[e + 1]).then_inc(sdp.h, 16)
                            dwn_marks[e + 1] = sdp.inc(16)

                    last_evac = max(evac_marks.values())
                    if dbg and call == 0:
                        @on("sync")
                        def _(sy, H=H, last_evac=last_evac, v_rt=v_rt):
                            sy.wait_ge(sV.h, last_evac)
                            sy.dma_start(out=da2_dbg[H],
                                         in_=f_dacc[:, :]).then_inc(sY.h, 16)
                            if H == 1:
                                sy.dma_start(out=rt_dbg,
                                             in_=f_rt[0:8, :]).then_inc(sY.h, 16)
                        sY.inc(16 * (2 if H == 1 else 1))
                    for k in range(8):
                        @on("vector")
                        def _(v, k=k, H=H, last_evac=last_evac):
                            if k == 0:
                                v.wait_ge(sV.h, last_evac)
                            v.tensor_tensor(stream[:, k * T + H * 1024:k * T + H * 1024 + 1024],
                                            stream[:, k * T + H * 1024:k * T + H * 1024 + 1024],
                                            f_dacc[:, k * 1024:(k + 1) * 1024], ALU.add)
                            v.engine_nop().then_inc(sV.h, 1)
                        sV.inc()
                    v_end = sV.n
                return v_end


        def emit_attention():
            sqk = mksem("sqk")
            srot = mksem("srot")
            sks = mksem("sks")
            sCag = mksem("sCag")
            skt = [mksem("skt0"), mksem("skt1"), mksem("skt2"), mksem("skt3")]
            svt = [mksem("svt0"), mksem("svt1"), mksem("svt2"), mksem("svt3")]
            sat = mksem("sat")
            sprep = mksem("sprep")

            with contextlib.ExitStack() as _qst:
                a_q = _qst.enter_context(nc.sbuf_tensor("a_q", [128, 8 * 2048], BF16))

                with contextlib.ExitStack() as _ast:
                    a_wq = _ast.enter_context(nc.sbuf_tensor("a_wq", [128, 8 * 1024], BF16))
                    a_wk = _ast.enter_context(nc.sbuf_tensor("a_wk", [128, 8 * 1024], BF16))
                    a_wv = _ast.enter_context(nc.sbuf_tensor("a_wv", [128, 8 * 1024], BF16))
                    a_xs = _ast.enter_context(nc.sbuf_tensor("a_xs", [128, 8 * 512], BF16))
                    a_sq = _ast.enter_context(nc.sbuf_tensor("a_sq", [128, 512], F32))
                    a_cos = _ast.enter_context(nc.sbuf_tensor("a_cos", [128, 2048], F32))
                    a_sin = _ast.enter_context(nc.sbuf_tensor("a_sin", [128, 2048], F32))
                    a_qf = _ast.enter_context(nc.sbuf_tensor("a_qf", [128, 2 * 512], F32))
                    a_qr = _ast.enter_context(nc.sbuf_tensor("a_qr", [128, 2 * 512], F32))
                    a_t1 = _ast.enter_context(nc.sbuf_tensor("a_t1", [128, 512], F32))
                    a_t2 = _ast.enter_context(nc.sbuf_tensor("a_t2", [128, 512], F32))
                    a_kb = _ast.enter_context(nc.sbuf_tensor("a_kb", [128, 512], BF16))
                    a_vs = _ast.enter_context(nc.sbuf_tensor("a_vs", [128, 1040], BF16))
                    pa_st = _ast.enter_context(nc.psum_tensor("pa_st", [1, 1024], F32))
                    pa_ms = _ast.enter_context(nc.psum_tensor("pa_ms", [128, 512], F32))
                    pa_q = _ast.enter_context(nc.psum_tensor("pa_q", [128, 2 * 512], F32))
                    pa_v = _ast.enter_context(nc.psum_tensor("pa_v", [128, 2 * 512], F32))

                    @on("gpsimd")
                    def _(g):
                        ld3(g, a_wq[:, :], wq_t, 8, sqk.h)
                        ld3(g, a_wk[:, :], wk_t, 8, sqk.h)
                        ld3(g, a_wv[:, :], wv_t, 8, sqk.h)
                    w_qkv = sqk.inc(16 * 3)

                    @on("sync")
                    def _(sy):
                        sy.dma_start(out=a_cos[:, :], in_=cosr).then_inc(sW.h, 16)
                        sy.dma_start(out=a_sin[:, :], in_=sinr).then_inc(sW.h, 16)
                    w_cs = sW.inc(32)

                    v_ws = 0

                    rot_free = {}
                    rope_free = {}
                    rpc = [0]
                    for f in range(4):
                        xs_off = 0
                        scol = f * 512
                        for k in range(8):
                            @on("scalar")
                            def _(s, k=k, scol=scol, tmark=sT.n, vmark=sV.n):
                                s.wait_ge(sT.h, tmark)
                                if k == 0:
                                    s.wait_ge(sV.h, vmark)
                                s.activation(a_sq[:, :],
                                             stream[:, k * T + scol:k * T + scol + 512],
                                             AF.Square, bias=0.0, scale=1.0).then_inc(sS.h, 1)
                            sqd = sS.inc()

                            @on("tensor")
                            def _(t, k=k, scol=scol, sqd=sqd, vmark=sV.n):
                                if k == 0:
                                    t.wait_ge(sV.h, vmark)
                                t.matmul(pa_st[0:1, 0:512], onesc[:, :],
                                         stream[:, k * T + scol:k * T + scol + 512],
                                         start=(k == 0), stop=(k == 7))
                                t.wait_ge(sS.h, sqd)
                                t.matmul(pa_st[0:1, 512:1024], onesc[:, :], a_sq[:, :],
                                         start=(k == 0), stop=(k == 7)).then_inc(sT.h, 1)
                            sT.inc()
                        t_stats = sT.n

                        @on("vector")
                        def _(v, t_stats=t_stats):
                            v.wait_ge(sT.h, t_stats)
                            v.tensor_scalar(out=r_mu, in0=pa_st[0:1, 0:512], scalar1=1.0 / DM,
                                            scalar2=None, op0=ALU.mult)
                            v.tensor_scalar(out=a_sq[0:1, :], in0=pa_st[0:1, 512:1024],
                                            scalar1=1.0 / DM, scalar2=None, op0=ALU.mult)
                            v.tensor_tensor(r_inv, r_mu, r_mu, ALU.mult)
                            v.tensor_tensor(a_sq[0:1, :], a_sq[0:1, :], r_inv,
                                            ALU.subtract)
                            v.tensor_scalar(out=a_sq[0:1, :], in0=a_sq[0:1, :],
                                            scalar1=EPS_LN, scalar2=None, op0=ALU.add)
                            v.engine_nop().then_inc(sV.h, 1)
                        v_rm = sV.inc()

                        @on("scalar")
                        def _(s, v_rm=v_rm):
                            s.wait_ge(sV.h, v_rm)
                            s.activation(a_t1[0:1, :], a_sq[0:1, :], AF.Ln, bias=0.0,
                                         scale=1.0)
                            s.activation(r_inv, a_t1[0:1, :], AF.Exp, bias=0.0,
                                         scale=-0.5).then_inc(sS.h, 1)
                        s_inv = sS.inc()

                        @on("vector")
                        def _(v, s_inv=s_inv):
                            v.wait_ge(sS.h, s_inv)
                            v.tensor_tensor(r_muinv, r_mu, r_inv, ALU.mult)
                            v.engine_nop().then_inc(sV.h, 1)
                        v_muinv = sV.inc()

                        @on("tensor")
                        def _(t, v_muinv=v_muinv, smark=sS.n):
                            t.wait_ge(sV.h, v_muinv)
                            t.wait_ge(sS.h, smark)
                            t.matmul(pa_ms[:, :], onesr[:, :], r_inv, start=True,
                                     stop=True)
                            t.matmul(pa_q[:, 0:512], onesr[:, :], r_muinv, start=True,
                                     stop=True).then_inc(sT.h, 1)
                        t_invrep = sT.inc()

                        for k in range(8):
                            @on("vector")
                            def _(v, k=k, scol=scol, xs_off=xs_off, t_invrep=t_invrep,
                                  tmark=sT.n):
                                v.wait_ge(sT.h, t_invrep)
                                if k == 0:
                                    t_invrep and v.wait_ge(sT.h, tmark)
                                v.tensor_tensor(
                                    a_t1[:, :],
                                    stream[:, k * T + scol:k * T + scol + 512],
                                    pa_ms[:, :], ALU.mult)
                                v.tensor_tensor(
                                    a_xs[:, xs_off + k * 512:xs_off + (k + 1) * 512],
                                    a_t1[:, :], pa_q[:, 0:512], ALU.subtract)
                                v.engine_nop().then_inc(sV.h, 1)
                            sV.inc()
                        v_xs = sV.n

                        for which, wsb, wsoff in ((0, a_wq, 0), (1, a_wk, 1024)):
                            for m in range(8):
                                pm = m % 2

                                @on("tensor")
                                def _(t, wsb=wsb, wsoff=wsoff, m=m, pm=pm, xs_off=xs_off,
                                      v_xs=v_xs, v_ws=v_ws, w_qkv=w_qkv, smark=sS.n):
                                    t.wait_ge(sV.h, max(v_xs, v_ws))
                                    t.wait_ge(sqk.h, w_qkv)
                                    t.wait_ge(sS.h, smark)
                                    for k in range(7):
                                        t.matmul(pa_q[:, pm * 512:pm * 512 + 512],
                                                 wsb[:, k * 1024 + m * 128:
                                                     k * 1024 + m * 128 + 128],
                                                 a_xs[:, xs_off + k * 512:
                                                      xs_off + (k + 1) * 512],
                                                 start=(k == 0), stop=False)
                                    t.matmul(pa_q[:, pm * 512:pm * 512 + 512],
                                             wsb[:, 7 * 1024 + m * 128:
                                                 7 * 1024 + m * 128 + 128],
                                             a_xs[:, xs_off + 7 * 512:xs_off + 8 * 512],
                                             start=False, stop=True).then_inc(sT.h, 1)
                                t_m = sT.inc()

                                rp = rpc[0] % 2
                                rpc[0] += 1

                                @on("scalar")
                                def _(s, m=m, pm=pm, rp=rp, t_m=t_m,
                                      rmark=rot_free.get(rp, 0)):
                                    s.wait_ge(sT.h, t_m)
                                    if rmark:
                                        s.wait_ge(srot.h, rmark)
                                    s.activation(a_qf[:, rp * 512:rp * 512 + 512],
                                                 pa_q[:, pm * 512:pm * 512 + 512],
                                                 AF.Copy, bias=0.0,
                                                 scale=1.0).then_inc(sS.h, 1)
                                s_ev = sS.inc()

                                @on("sync")
                                def _(sy, s_ev=s_ev, rp=rp, vmark=rope_free.get(rp, 0)):
                                    sy.wait_ge(sS.h, s_ev)
                                    if vmark:
                                        sy.wait_ge(sV.h, vmark)
                                    for hh in range(2):
                                        b = hh * 64
                                        sy.dma_start(out=a_qr[b:b + 32,
                                                             rp * 512:rp * 512 + 512],
                                                     in_=a_qf[b + 32:b + 64,
                                                              rp * 512:rp * 512 + 512]
                                                     ).then_inc(srot.h, 16)
                                        sy.dma_start(out=a_qr[b + 32:b + 64,
                                                              rp * 512:rp * 512 + 512],
                                                     in_=a_qf[b:b + 32,
                                                              rp * 512:rp * 512 + 512]
                                                     ).then_inc(srot.h, 16)
                                r_rot = srot.inc(64)
                                rot_free[rp] = r_rot

                                @on("vector")
                                def _(v, which=which, m=m, rp=rp, scol=scol, r_rot=r_rot,
                                      w_cs=w_cs, smark=sks.n):
                                    v.wait_ge(srot.h, r_rot)
                                    v.wait_ge(sW.h, w_cs)
                                    v.wait_ge(sks.h, smark)
                                    v.tensor_tensor(a_t1[:, :],
                                                    a_qf[:, rp * 512:rp * 512 + 512],
                                                    a_cos[:, scol:scol + 512], ALU.mult)
                                    v.tensor_tensor(a_t2[:, :],
                                                    a_qr[:, rp * 512:rp * 512 + 512],
                                                    a_sin[:, scol:scol + 512], ALU.mult)
                                    v.tensor_tensor(a_t1[:, :], a_t1[:, :], a_t2[:, :],
                                                    ALU.add)
                                    if which == 0:
                                        v.tensor_scalar(
                                            out=a_q[:, m * 2048 + scol:m * 2048 + scol + 512],
                                            in0=a_t1[:, :], scalar1=0.125, scalar2=None,
                                            op0=ALU.mult)
                                    else:
                                        v.tensor_copy(a_kb[:, :], a_t1[:, :])
                                    v.engine_nop().then_inc(sV.h, 1)
                                v_rope = sV.inc()
                                rope_free[rp] = v_rope

                                if which == 1:
                                    @on("vector")
                                    def _(v, m=m, scol=scol, v_rope=v_rope):
                                        v.tensor_tensor(
                                            a_t2[:, :],
                                            a_q[:, m * 2048 + scol:m * 2048 + scol + 512],
                                            a_kb[:, :], ALU.mult)
                                        v.engine_nop().then_inc(sV.h, 1)
                                    v_dg = sV.inc()
                                else:
                                    @on("scalar")
                                    def _(s, m=m, scol=scol, v_rope=v_rope):
                                        s.wait_ge(sV.h, v_rope)
                                        s.activation(
                                            a_t2[:, :],
                                            a_q[:, m * 2048 + scol:m * 2048 + scol + 512],
                                            AF.Square, bias=0.0, scale=1.0).then_inc(sS.h, 1)
                                    s_q2 = sS.inc()
                                    v_dg = None

                                @on("tensor")
                                def _(t, m=m, which=which, v_dg=v_dg,
                                      smark=sS.n, vmark=sV.n):
                                    if which == 1:
                                        t.wait_ge(sV.h, v_dg)
                                    else:
                                        t.wait_ge(sS.h, smark)
                                    t.wait_ge(sV.h, vmark)
                                    t.matmul(pa_st[0:1, 512:1024], onesc[0:64, :],
                                             a_t2[0:64, :], start=True, stop=True)
                                    t.matmul(pa_st[0:1, 0:512], onesc[64:128, :],
                                             a_t2[64:128, :], start=True,
                                             stop=True).then_inc(sT.h, 1)
                                t_rr = sT.inc()

                                @on("vector")
                                def _(v, t_rr=t_rr, ymark=sks.n):
                                    v.wait_ge(sT.h, t_rr)
                                    v.wait_ge(sks.h, ymark)
                                    v.tensor_copy(rows[0:1, 0:512], pa_st[0:1, 512:1024])
                                    v.tensor_copy(rows[0:1, 512:1024], pa_st[0:1, 0:512])
                                    v.engine_nop().then_inc(sV.h, 1)
                                v_rr = sV.inc()

                                @on("sync")
                                def _(sy, which=which, m=m, scol=scol, v_rr=v_rr):
                                    sy.wait_ge(sV.h, v_rr)
                                    dst = diag_hbm if which == 1 else qn2_hbm
                                    sy.dma_start(out=dst[2 * m:2 * m + 1, scol:scol + 512],
                                                 in_=rows[0:1, 0:512]).then_inc(sks.h, 16)
                                    sy.dma_start(out=dst[2 * m + 1:2 * m + 2, scol:scol + 512],
                                                 in_=rows[0:1, 512:1024]).then_inc(sks.h, 16)
                                    if which == 1:
                                        off = (m // 2) * 256 + (m % 2) * 128
                                        sy.dma_start(
                                            out=kag_in
                                            .rearrange("p (t x) -> p t x", x=1024)
                                            [:, (scol // 128):(scol // 128) + 4,
                                             off:off + 128],
                                            in_=a_kb[:, :]
                                            .rearrange("p (t c) -> p t c", c=128)
                                            ).then_inc(sks.h, 16)
                                sks.inc(48 if which == 1 else 32)

                        for tt_ in range(4):
                            for ng in range(2):
                                @on("tensor")
                                def _(t, tt_=tt_, ng=ng, xs_off=xs_off, v_xs=v_xs,
                                      v_ws=v_ws, w_qkv=w_qkv, vmark=sV.n):
                                    t.wait_ge(sV.h, max(v_xs, v_ws, vmark))
                                    t.wait_ge(sqk.h, w_qkv)
                                    for k in range(7):
                                        t.matmul(pa_v[:, ng * 512:ng * 512 + 512],
                                                 a_xs[:, xs_off + k * 512 + tt_ * 128:
                                                      xs_off + k * 512 + tt_ * 128 + 128],
                                                 a_wv[:, k * 1024 + ng * 512:
                                                      k * 1024 + ng * 512 + 512],
                                                 start=(k == 0), stop=False)
                                    t.matmul(pa_v[:, ng * 512:ng * 512 + 512],
                                             a_xs[:, xs_off + 7 * 512 + tt_ * 128:
                                                  xs_off + 7 * 512 + tt_ * 128 + 128],
                                             a_wv[:, 7 * 1024 + ng * 512:
                                                  7 * 1024 + ng * 512 + 512],
                                             start=False, stop=True).then_inc(sT.h, 1)
                                sT.inc()
                            t_v = sT.n

                            @on("vector")
                            def _(v, tt_=tt_, t_v=t_v, smark=sks.n):
                                v.wait_ge(sT.h, t_v)
                                v.wait_ge(sks.h, smark)
                                v.memset(a_vs[:, :], 1.0)
                                for ng in range(2):
                                    v.tensor_copy(
                                        a_vs[:, ng * 520:(ng + 1) * 520]
                                        .rearrange("p (h c) -> p h c", h=8)[:, :, 0:64],
                                        pa_v[:, ng * 512:ng * 512 + 512]
                                        .rearrange("p (h c) -> p h c", h=8))
                                v.engine_nop().then_inc(sV.h, 1)
                            v_vs = sV.inc()

                            @on("sync")
                            def _(sy, f=f, tt_=tt_, v_vs=v_vs):
                                sy.wait_ge(sV.h, v_vs)
                                sy.dma_start(out=vag_in[(f * 4 + tt_) * 128:
                                                        (f * 4 + tt_ + 1) * 128, :],
                                             in_=a_vs[:, :]).then_inc(sks.h, 16)
                            sks.inc(16)

                    y_kv = sks.n

                    @on("gpsimd")
                    def _(g, y_kv=y_kv):
                        g.wait_ge(sks.h, y_kv)
                        g.collective_compute(
                            "AllGather", ALU.bypass,
                            replica_groups=[[0, 1], [2, 3], [4, 5], [6, 7]],
                            ins=[kag_in], outs=[kag_out],
                        ).then_inc(sCag.h, 1)
                        g.collective_compute(
                            "AllGather", ALU.bypass,
                            replica_groups=[[0, 1], [2, 3], [4, 5], [6, 7]],
                            ins=[vag_in], outs=[vag_out],
                        ).then_inc(sCag.h, 1)
                    c_ag = sCag.inc(2)

                barrier()

                with contextlib.ExitStack() as _bst:
                    b_at = _bst.enter_context(nc.sbuf_tensor("b_at", [128, 8 * 2048], BF16))
                    b_wo = _bst.enter_context(nc.sbuf_tensor("b_wo", [128, 8 * 1024], BF16))
                    b_kt = _bst.enter_context(nc.sbuf_tensor("b_kt", [128, 4 * 256], BF16))
                    b_vt = _bst.enter_context(nc.sbuf_tensor("b_vt", [128, 4 * 260], BF16))
                    b_p = _bst.enter_context(nc.sbuf_tensor("b_p", [128, 2 * 2048], BF16))
                    b_msk = _bst.enter_context(nc.sbuf_tensor("b_msk", [128, 8 * 512], BF16))
                    b_kh = _bst.enter_context(nc.sbuf_tensor("b_kh", [64, 2 * 512], BF16))
                    b_k2 = _bst.enter_context(nc.sbuf_tensor("b_k2", [64, 512], F32))
                    b_den = _bst.enter_context(nc.sbuf_tensor("b_den", [65, 512], F32))
                    b_o65 = _bst.enter_context(nc.sbuf_tensor("b_o65", [65, 128], F32))
                    crep = _bst.enter_context(nc.sbuf_tensor("crep", [128, 2048], F32))
                    b_ps = _bst.enter_context(nc.sbuf_tensor("b_ps", [128, 2 * 1024], F32))
                    b_nl = _bst.enter_context(nc.sbuf_tensor("b_nl", [1, 1024], F32))
                    b_tmp = _bst.enter_context(nc.sbuf_tensor("b_tmp", [64, 512], BF16))
                    b_av = _bst.enter_context(nc.sbuf_tensor("b_av", [64, 512], F32))
                    b_kmx = _bst.enter_context(nc.sbuf_tensor("b_kmx", [1, 64], F32))
                    pb_s = _bst.enter_context(nc.psum_tensor("pb_s", [128, 2048], F32))
                    pb_a0 = _bst.enter_context(nc.psum_tensor("pb_a0", [65, 512], F32))
                    pb_a1 = _bst.enter_context(nc.psum_tensor("pb_a1", [65, 512], F32))
                    pb_a2 = _bst.enter_context(nc.psum_tensor("pb_a2", [65, 512], F32))
                    pb_a3 = _bst.enter_context(nc.psum_tensor("pb_a3", [65, 512], F32))
                    PB_A = [pb_a0, pb_a1, pb_a2, pb_a3]

                    @on("gpsimd")
                    def _(g):
                        ld3(g, b_wo[:, :], wo_t, 8, sqk.h)
                        g.dma_start(out=b_msk[:, :], in_=mask8).then_inc(sqk.h, 16)
                    w_wo = sqk.inc(32)

                    @on("vector")
                    def _(v):
                        v.memset(b_o65[64:65, :], 1.0)
                        v.engine_nop().then_inc(sV.h, 1)
                    sV.inc()

                    for h in range(NH):
                        for kc in range(8):
                            pk = kc % 2

                            @on("sync")
                            def _(sy, h=h, kc=kc, pk=pk, c_ag=c_ag, smark=sS.n):
                                sy.wait_ge(sCag.h, c_ag)
                                sy.wait_ge(sS.h, smark)
                                hb_, hl_ = h // 4, h % 4
                                pbase = (kc % 2) * 128 + (hl_ % 2) * 64
                                xoff = hb_ * 256 + (hl_ // 2) * 128
                                sy.dma_start(
                                    out=b_kh[:, pk * 512:pk * 512 + 512]
                                    .rearrange("p (t c) -> p t c", c=128),
                                    in_=kag_out
                                    .rearrange("q (t x) -> q t x", x=1024)
                                    [pbase:pbase + 64,
                                     (kc // 2) * 4:(kc // 2) * 4 + 4,
                                     xoff:xoff + 128]
                                    ).then_inc(sprep.h, 16)
                            k_ld = sprep.inc(16)

                            @on("scalar")
                            def _(s, h=h, kc=kc, pk=pk, k_ld=k_ld, tmark=sT.n):
                                s.wait_ge(sprep.h, k_ld)
                                s.wait_ge(sT.h, tmark)
                                s.activation(b_k2[:, :], b_kh[:, pk * 512:pk * 512 + 512],
                                             AF.Square, bias=0.0, scale=1.0).then_inc(sS.h, 1)
                            s_k2 = sS.inc()

                            @on("tensor")
                            def _(t, h=h, kc=kc, s_k2=s_k2, vmark=sV.n):
                                t.wait_ge(sS.h, s_k2)
                                if kc == 0:
                                    t.wait_ge(sV.h, vmark)
                                t.matmul(pb_s[0:1, 0:512], onesc[0:64, :], b_k2[:, :],
                                         start=(kc == 0), stop=(kc == 7)).then_inc(sT.h, 1)
                            sT.inc()
                        t_k2 = sT.n

                        @on("vector")
                        def _(v, h=h, t_k2=t_k2):
                            v.wait_ge(sT.h, t_k2)
                            v.reduce_max(b_kmx[0:1, h:h + 1], pb_s[0:1, 0:512], axis=AX)
                            v.engine_nop().then_inc(sV.h, 1)
                        sV.inc()
                    v_kmx = sV.n

                    @on("scalar")
                    def _(s, v_kmx=v_kmx):
                        s.wait_ge(sV.h, v_kmx)
                        s.activation(b_kmx[0:1, 16:32], b_kmx[0:1, 0:16], AF.Ln,
                                     bias=0.0, scale=1.0)
                        s.drain()
                        s.activation(b_kmx[0:1, 32:48], b_kmx[0:1, 16:32], AF.Exp,
                                     bias=0.0, scale=0.5).then_inc(sS.h, 1)
                    s_kmx = sS.inc()

                    for h in range(NH):
                        for gq in range(4):
                            @on("sync")
                            def _(sy, h=h, gq=gq, vmark=sV.n):
                                sy.wait_ge(sV.h, vmark)
                                sy.dma_start(out=b_nl[0:1, 0:512],
                                             in_=qn2_hbm[h:h + 1, gq * 512:gq * 512 + 512]
                                             ).then_inc(sprep.h, 16)
                                sy.dma_start(out=b_nl[0:1, 512:1024],
                                             in_=diag_hbm[h:h + 1, gq * 512:gq * 512 + 512]
                                             ).then_inc(sprep.h, 16)
                            l_nc = sprep.inc(32)

                            @on("scalar")
                            def _(s, h=h, gq=gq, l_nc=l_nc, s_kmx=s_kmx):
                                s.wait_ge(sprep.h, l_nc)
                                s.wait_ge(sS.h, s_kmx)
                                s.activation(rows[0:1, 0:512], b_nl[0:1, 0:512], AF.Ln,
                                             bias=0.0, scale=1.0)
                                s.activation(rows[0:1, 512:1024], rows[0:1, 0:512], AF.Exp,
                                             bias=0.0, scale=0.5).then_inc(sS.h, 1)
                            s_qn = sS.inc()

                            @on("vector")
                            def _(v, h=h, gq=gq, s_qn=s_qn, ymark=sks.n):
                                v.wait_ge(sS.h, s_qn)
                                v.wait_ge(sks.h, ymark)
                                v.tensor_scalar(out=rows[0:1, 0:512], in0=rows[0:1, 512:1024],
                                                scalar1=b_kmx[0:1, 32 + h:33 + h],
                                                scalar2=-CAP, op0=ALU.mult, op1=ALU.add)
                                v.tensor_tensor(rows[0:1, 0:512], rows[0:1, 0:512],
                                                b_nl[0:1, 512:1024], ALU.max)
                                v.tensor_scalar(out=rows[0:1, 512:1024], in0=rows[0:1, 0:512],
                                                scalar1=-1.0, scalar2=None, op0=ALU.mult)
                                v.engine_nop().then_inc(sV.h, 1)
                            v_nc = sV.inc()

                            @on("sync")
                            def _(sy, h=h, gq=gq, v_nc=v_nc):
                                sy.wait_ge(sV.h, v_nc)
                                sy.dma_start(out=negc_hbm[h:h + 1, gq * 512:gq * 512 + 512],
                                             in_=rows[0:1, 512:1024]).then_inc(sks.h, 16)
                            sks.inc(16)
                    y_nc = sks.n

                    kt_free = {}
                    vt_free = {}
                    bp_free = {}
                    nc_free = {}
                    sub_free = {}
                    sub_marks = {}
                    bps_free = {}
                    pbs_free = [0]
                    scrp = mksem("scrp")
                    for gq in range(4):
                        njj = 8 * gq + 8
                        for hb in range(4):
                            ib = gq * 4 + hb
                            nb = ib % 2
                            @on("gpsimd")
                            def _(g, gq=gq, hb=hb, y_nc=y_nc,
                                  nf=nc_free.get(0, 0)):
                                g.wait_ge(sks.h, y_nc)
                                if nf:
                                    g.wait_ge(sV.h, nf)
                                for hl in range(4):
                                    h = hb * 4 + hl
                                    g.dma_start(out=crep[:, hl * 512:(hl + 1) * 512],
                                                in_=negc_hbm[h:h + 1,
                                                             gq * 512:gq * 512 + 512]
                                                .broadcast_to([128, 512])
                                                ).then_inc(scrp.h, 16)
                            l_ncb = scrp.inc(64)

                            av_marks = {}
                            exp_marks = {}
                            for jj in range(njj):
                                pj = jj % 4
                                bp = jj % 2

                                @on("sync")
                                def _(sy, gq=gq, hb=hb, jj=jj, pj=pj, c_ag=c_ag,
                                      kf=kt_free.get(pj, 0)):
                                    sy.wait_ge(sCag.h, c_ag)
                                    if kf:
                                        sy.wait_ge(sT.h, kf)
                                    sy.dma_start(
                                        out=b_kt[:, pj * 256:pj * 256 + 256],
                                        in_=kag_out[(jj % 2) * 128:(jj % 2) * 128 + 128,
                                                    ((jj // 2) * 4 + hb) * 256:
                                                    ((jj // 2) * 4 + hb) * 256 + 256]
                                        ).then_inc(skt[pj].h, 16)
                                l_kt = skt[pj].inc(16)

                                @on("gpsimd")
                                def _(g, gq=gq, hb=hb, jj=jj, pj=pj, c_ag=c_ag,
                                      vf=vt_free.get(pj, 0)):
                                    g.wait_ge(sCag.h, c_ag)
                                    if vf:
                                        g.wait_ge(sT.h, vf)
                                    g.dma_start(
                                        out=b_vt[:, pj * 260:pj * 260 + 260],
                                        in_=vag_out[(jj % 2) * 2048 + (jj // 2) * 128:
                                                    (jj % 2) * 2048 + (jj // 2) * 128 + 128,
                                                    hb * 260:(hb + 1) * 260]
                                        ).then_inc(svt[pj].h, 16)
                                l_vt = svt[pj].inc(16)

                                for hl in range(4):
                                    @on("tensor")
                                    def _(t, gq=gq, hb=hb, jj=jj, pj=pj, hl=hl,
                                          l_kt=l_kt,
                                          sf0=sub_free.get(0, 0),
                                          sf1=sub_free.get(1, 0),
                                          pbs0=pbs_free[0]):
                                        h = hb * 4 + hl
                                        t.wait_ge(skt[pj].h, l_kt)
                                        if hl == 0 and sf0:
                                            t.wait_ge(sV.h, sf0)
                                        if hl == 2 and sf1:
                                            t.wait_ge(sV.h, sf1)
                                        if hl == 0 and jj == 0 and pbs0:
                                            t.wait_ge(sV.h, pbs0)
                                        bq = (hl % 2) * 64
                                        t.matmul(pb_s[:, hl * 512:hl * 512 + 512],
                                                 b_kt[bq:bq + 64,
                                                      pj * 256 + (hl // 2) * 128:
                                                      pj * 256 + (hl // 2) * 128 + 128],
                                                 a_q[bq:bq + 64,
                                                     (h // 2) * 2048 + gq * 512:
                                                     (h // 2) * 2048 + gq * 512 + 512],
                                                 start=True, stop=True).then_inc(sT.h, 1)
                                    sT.inc()
                                    if hl == 1:
                                        t_s2 = sT.n
                                t_s4 = sT.n
                                kt_free[pj] = t_s4

                                for half in range(2):
                                    @on("vector")
                                    def _(v, jj=jj, bp=bp, nb=nb, half=half,
                                          ts=(t_s2 if half == 0 else t_s4),
                                          l_ncb=l_ncb,
                                          bpsf=bps_free.get(half, 0)):
                                        v.wait_ge(sT.h, ts)
                                        v.wait_ge(scrp.h, l_ncb)
                                        if bpsf:
                                            v.wait_ge(sS.h, bpsf)
                                        v.tensor_tensor(
                                            b_ps[:, half * 1024:half * 1024 + 1024],
                                            pb_s[:, half * 1024:half * 1024 + 1024],
                                            crep[:, half * 1024:half * 1024 + 1024],
                                            ALU.add)
                                        v.engine_nop().then_inc(sV.h, 1)
                                    sub_marks[(jj, half)] = sV.inc()
                                    sub_free[half] = sub_marks[(jj, half)]
                                nc_free[0] = sub_marks[(jj, 1)]

                                for half in range(2):
                                    @on("scalar")
                                    def _(s, jj=jj, bp=bp, half=half,
                                          sv=sub_marks[(jj, half)],
                                          bpf=bp_free.get(bp, 0)):
                                        s.wait_ge(sV.h, sv)
                                        if half == 0 and bpf:
                                            s.wait_ge(sT.h, bpf)
                                        s.activation(b_p[:, bp * 2048 + half * 1024:
                                                         bp * 2048 + half * 1024 + 1024],
                                                     b_ps[:, half * 1024:half * 1024 + 1024],
                                                     AF.Exp, bias=0.0,
                                                     scale=1.0).then_inc(sS.h, 1)
                                    exp_marks[(jj, half)] = sS.inc()
                                    bps_free[half] = exp_marks[(jj, half)]

                                r = jj - 8 * gq
                                msk_marks = [None, None]
                                if r >= 0:
                                    for half in range(2):
                                        @on("vector")
                                        def _(v, r=r, bp=bp, half=half, w_wo=w_wo,
                                              se=exp_marks[(jj, half)]):
                                            v.wait_ge(sS.h, se)
                                            v.wait_ge(sqk.h, w_wo)
                                            v.tensor_tensor(
                                                b_p[:, bp * 2048 + half * 1024:
                                                    bp * 2048 + (half + 1) * 1024]
                                                .rearrange("p (q c) -> p q c", q=2),
                                                b_p[:, bp * 2048 + half * 1024:
                                                    bp * 2048 + (half + 1) * 1024]
                                                .rearrange("p (q c) -> p q c", q=2),
                                                b_msk[0:128, r * 512:(r + 1) * 512]
                                                .unsqueeze(1)
                                                .broadcast_to([128, 2, 512]), ALU.mult)
                                            v.engine_nop().then_inc(sV.h, 1)
                                        msk_marks[half] = sV.inc()

                                for hl in range(4):
                                    @on("tensor")
                                    def _(t, gq=gq, hb=hb, jj=jj, pj=pj, bp=bp, hl=hl,
                                          njj=njj, l_vt=l_vt,
                                          se=exp_marks[(jj, hl // 2)],
                                          vm=msk_marks[hl // 2], vmark=sV.n):
                                        t.wait_ge(svt[pj].h, l_vt)
                                        if vm is not None:
                                            t.wait_ge(sV.h, vm)
                                        else:
                                            t.wait_ge(sS.h, se)
                                        if jj == 0:
                                            t.wait_ge(sV.h, vmark)
                                        t.matmul(PB_A[hl][:, :],
                                                 b_vt[:, pj * 260 + hl * 65:
                                                      pj * 260 + hl * 65 + 65],
                                                 b_p[:, bp * 2048 + hl * 512:
                                                     bp * 2048 + (hl + 1) * 512],
                                                 start=(jj == 0),
                                                 stop=(jj == njj - 1)).then_inc(sT.h, 1)
                                    sT.inc()
                                av_marks[jj] = sT.n
                                vt_free[pj] = av_marks[jj]
                                bp_free[bp] = av_marks[jj]

                            for hl in range(4):
                                @on("vector")
                                def _(v, gq=gq, hb=hb, hl=hl, av=av_marks[njj - 1],
                                      ymark=sat.n):
                                    v.wait_ge(sT.h, av)
                                    v.wait_ge(sat.h, ymark)
                                    v.tensor_copy(b_den[64:65, :], PB_A[hl][64:65, :])
                                    v.tensor_scalar(out=b_den[64:65, :], in0=b_den[64:65, :],
                                                    scalar1=1e-38, scalar2=None, op0=ALU.max)
                                    v.reciprocal(b_den[64:65, :], b_den[64:65, :])
                                    v.engine_nop().then_inc(sV.h, 1)
                                v_den = sV.inc()

                                @on("tensor")
                                def _(t, hl=hl, v_den=v_den, vmark=sV.n):
                                    t.wait_ge(sV.h, v_den)
                                    t.matmul(pb_s[:, 0:512], b_o65[64:65, :],
                                             b_den[64:65, :], start=True,
                                             stop=True).then_inc(sT.h, 1)
                                t_dr = sT.inc()

                                @on("vector")
                                def _(v, hl=hl, t_dr=t_dr):
                                    v.wait_ge(sT.h, t_dr)
                                    v.tensor_copy(b_av[:, :], PB_A[hl][0:64, :])
                                    v.tensor_tensor(b_tmp[:, :], b_av[:, :],
                                                    pb_s[0:64, 0:512], ALU.mult)
                                    v.engine_nop().then_inc(sV.h, 1)
                                v_tmp = sV.inc()

                                @on("sync")
                                def _(sy, gq=gq, hb=hb, hl=hl, v_tmp=v_tmp):
                                    sy.wait_ge(sV.h, v_tmp)
                                    h = hb * 4 + hl
                                    sy.dma_start(
                                        out=b_at[(h % 2) * 64:(h % 2) * 64 + 64,
                                                 (h // 2) * 2048 + gq * 512:
                                                 (h // 2) * 2048 + gq * 512 + 512],
                                        in_=b_tmp[:, :]).then_inc(sat.h, 16)
                                sat.inc(16)
                            pbs_free[0] = sV.n
                    y_at = sat.n

                    for m in range(8):
                        for tc in range(4):
                            pq = (m * 4 + tc) % 2

                            @on("tensor")
                            def _(t, m=m, tc=tc, pq=pq, y_at=y_at, w_wo=w_wo, vmark=sV.n):
                                t.wait_ge(sat.h, y_at)
                                t.wait_ge(sqk.h, w_wo)
                                t.wait_ge(sV.h, vmark)
                                for k in range(7):
                                    t.matmul(pb_s[:, pq * 512 + 1024:pq * 512 + 1536],
                                             b_wo[:, k * 1024 + m * 128:
                                                  k * 1024 + m * 128 + 128],
                                             b_at[:, k * 2048 + tc * 512:
                                                  k * 2048 + (tc + 1) * 512],
                                             start=(k == 0), stop=False)
                                t.matmul(pb_s[:, pq * 512 + 1024:pq * 512 + 1536],
                                         b_wo[:, 7 * 1024 + m * 128:7 * 1024 + m * 128 + 128],
                                         b_at[:, 7 * 2048 + tc * 512:7 * 2048 + (tc + 1) * 512],
                                         start=False, stop=True).then_inc(sT.h, 1)
                            t_wo = sT.inc()

                            @on("vector")
                            def _(v, m=m, tc=tc, pq=pq, t_wo=t_wo):
                                v.wait_ge(sT.h, t_wo)
                                v.tensor_tensor(
                                    stream[:, m * T + tc * 512:m * T + (tc + 1) * 512],
                                    stream[:, m * T + tc * 512:m * T + (tc + 1) * 512],
                                    pb_s[:, pq * 512 + 1024:pq * 512 + 1536], ALU.add)
                                v.engine_nop().then_inc(sV.h, 1)
                            sV.inc()
            return sV.n
        if stage != "mamba":
            v_ffn1 = emit_ffn(0)
            if stage == "ffn1":
                @on("sync")
                def _(sy, v=v_ffn1, tgt=sY.n + 16 * 8):
                    sy.wait_ge(sV.h, v)
                    for cc in range(8):
                        sy.dma_start(out=sdump[cc],
                                     in_=stream[:, cc * T:(cc + 1) * T]).then_inc(sY.h, 16)
                    sy.wait_ge(sY.h, tgt)
                sY.inc(16 * 8)
            barrier()

        if stage in ("attn", "full"):
            v_att = emit_attention()
            if stage == "attn":
                @on("sync")
                def _(sy, v=v_att, tgt=sY.n + 16 * 8):
                    sy.wait_ge(sV.h, v)
                    for cc in range(8):
                        sy.dma_start(out=sdump[cc],
                                     in_=stream[:, cc * T:(cc + 1) * T]).then_inc(sY.h, 16)
                    sy.wait_ge(sY.h, tgt)
                sY.inc(16 * 8)
            barrier()

        if stage == "full":
            v_ffn2 = emit_ffn(1)
            barrier()
            with contextlib.ExitStack() as _ost:
                o_sb = _ost.enter_context(nc.sbuf_tensor("o_sb", [128, 1024], F32))
                po_t = _ost.enter_context(nc.psum_tensor("po_t", [128, 1024], F32))
                for ttile in range(16):
                    @on("tensor")
                    def _(t, ttile=ttile, v_ffn2=v_ffn2, smark=sS.n):
                        t.wait_ge(sV.h, v_ffn2)
                        t.wait_ge(sS.h, smark)
                        for dmc in range(8):
                            t.transpose(po_t[:, dmc * 128:(dmc + 1) * 128],
                                        stream[:, dmc * T + ttile * 128:
                                               dmc * T + ttile * 128 + 128],
                                        iden_sb[:, :]).then_inc(sT.h, 1)
                    t_tr = sT.inc(8)

                    @on("scalar")
                    def _(s, ttile=ttile, t_tr=t_tr, ymark=sY.n):
                        s.wait_ge(sT.h, t_tr)
                        s.wait_ge(sY.h, ymark)
                        s.activation(o_sb[:, :], po_t[:, :], AF.Copy, bias=0.0,
                                     scale=1.0).then_inc(sS.h, 1)
                    s_o = sS.inc()

                    @on("sync")
                    def _(sy, ttile=ttile, s_o=s_o):
                        sy.wait_ge(sS.h, s_o)
                        sy.dma_start(out=yout[ttile * 128:(ttile + 1) * 128, :],
                                     in_=o_sb[:, :]).then_inc(sY.h, 16)
                    sY.inc(16)

                @on("sync")
                def _(sy, tgt=sY.n):
                    sy.wait_ge(sY.h, tgt)


        @block.tensor
        def _(t):
            for fn in Q["tensor"]:
                fn(t)

        @block.vector
        def _(v):
            for fn in Q["vector"]:
                fn(v)

        @block.scalar
        def _(s):
            for fn in Q["scalar"]:
                fn(s)

        @block.gpsimd
        def _(g):
            for fn in Q["gpsimd"]:
                fn(g)

        @block.sync
        def _(sy):
            for fn in Q["sync"]:
                fn(sy)

    return nc


def _rope_tables():
    half = HD // 2
    inv_freq = 1.0 / 10000.0 ** (np.arange(half, dtype=np.float32) / half)
    t = np.arange(L, dtype=np.float32)
    fr = t[:, None] * inv_freq[None, :]
    return np.cos(fr), np.sin(fr)


def _core_positions(j):
    tiles = np.arange(j, 32, 2)
    return (tiles[:, None] * 128 + np.arange(128)[None, :]).reshape(-1)


def _oh8():
    m = np.zeros((8, NE * 128), np.float32)
    for e in range(NE):
        m[e, e * 128:(e + 1) * 128] = 1.0
    return m


def _mask_tiles(j):
    m = np.zeros((128, 8, 4, 128), np.float32)
    kp = np.arange(128)[:, None]
    qp = np.arange(128)[None, :]
    for r in range(8):
        for cc in range(4):
            arel = 2 * cc + j
            if r < arel:
                m[:, r, cc, :] = 1.0
            elif r == arel:
                m[:, r, cc, :] = (kp <= qp).astype(np.float32)
    return np.ascontiguousarray(m.reshape(128, 8 * 512))


def _stage_inputs(inputs):
    f32 = np.float32
    x = np.asarray(inputs["L0_attn_a_x"], f32)
    W_in = np.asarray(inputs["L0_attn_a_W_in"], f32)
    conv_w = np.asarray(inputs["L0_attn_a_conv_w"], f32)
    conv_b = np.asarray(inputs["L0_attn_a_conv_b"], f32)
    W_x = np.asarray(inputs["L0_attn_a_W_x"], f32)
    W_dt = np.asarray(inputs["L0_attn_a_W_dt"], f32)
    b_dt = np.asarray(inputs["L0_attn_a_b_dt"], f32)
    A_log = np.asarray(inputs["L0_attn_a_A_log"], f32)
    Dp = np.asarray(inputs["L0_attn_a_Dp"], f32)
    W_out = np.asarray(inputs["L0_attn_a_W_out"], f32)
    w_qkv = np.asarray(inputs["L0_attn_b_w_qkv"], f32)
    w_out_attn = np.asarray(inputs["L0_attn_b_w_out"], f32)
    gate_w = np.asarray(inputs["L0_ffn_gate_w"], f32)
    gate_up = np.asarray(inputs["L0_ffn_gate_up_proj"], f32)
    down = np.asarray(inputs["L0_ffn_down_proj"], f32)

    cos_full, sin_full = _rope_tables()
    C = np.ascontiguousarray

    in_maps = []
    for c in range(8):
        b, j = c // 2, c % 2
        ch = slice(j * CH, (j + 1) * CH)
        pos = _core_positions(j)
        cosj = cos_full[pos]
        sinj = sin_full[pos]
        cosr_ = np.tile(cosj.T, (4, 1)).astype(f32)
        sinr_ = np.tile(sinj.T, (4, 1)).astype(f32)
        sinr_[0:32] *= -1.0
        sinr_[64:96] *= -1.0
        d = {
            "xbt": C(x[b].T),
            "xres": C(x[b].T[:, pos]),
            "win_t": C(np.concatenate(
                [W_in[ch], W_in[2048 + j * CH:2048 + (j + 1) * CH]], axis=0).T),
            "convw": C(conv_w[ch, 0, :]),
            "convb": C(conv_b[ch].reshape(CH, 1)),
            "wx_t": C(W_x[:, ch].T),
            "wdt_t": C(W_dt[ch].T),
            "bdt": C(b_dt[ch].reshape(CH, 1)),
            "alog": C(A_log[ch]),
            "dpv": C(Dp[ch].reshape(CH, 1)),
            "wout_t": C(W_out[:, ch].T),
            "wq_t": C(w_qkv[0:DM].T),
            "wk_t": C(w_qkv[DM:2 * DM].T),
            "wv_t": C(w_qkv[2 * DM:3 * DM].T),
            "wo_t": C(w_out_attn.T),
            "gw_t": C(gate_w.T),
            "gup": C(gate_up),
            "dwn": C(down),
            "iden": np.eye(128, dtype=f32),
            "cosr": C(cosr_),
            "sinr": C(sinr_),
            "mask8": _mask_tiles(j),
            "oh8": _oh8(),
        }
        in_maps.append(d)
    return in_maps


_NC_CACHE = {}


def kernel(**inputs):
    stage = _STAGE
    if stage not in _NC_CACHE:
        _NC_CACHE[stage] = build_program(stage)
    nc = _NC_CACHE[stage]
    in_maps = _stage_inputs(inputs)
    res = run_bass_kernel_spmd(nc, in_maps, core_ids=list(range(8)))
    if stage != "full":
        return res
    out = np.zeros((4, L, DM), np.float32)
    for c in range(8):
        b, j = c // 2, c % 2
        pos = _core_positions(j)
        out[b, pos, :] = res.results[c]["yout"]
    return out
